# version 12
# baseline (speedup 1.0000x reference)
import contextlib
import math

import numpy as np
import ml_dtypes
import concourse.bass as bass
import concourse.mybir as mybir
from concourse.bass_utils import run_bass_kernel_spmd

F32 = mybir.dt.float32
BF16 = mybir.dt.bfloat16
AF = mybir.ActivationFunctionType
ALU = mybir.AluOpType
AX = mybir.AxisListType

B, S, D = 2, 4096, 4096
T = B * S
HD = 128
SCALE = 1.0 / math.sqrt(HD)
EPS = 1e-5
THETA = 10000.0
NT = T // 128
ND = D // 128


class Sem:
    def __init__(self, stack, nc, name):
        self.h = stack.enter_context(nc.semaphore(name))
        self.n = 0

    def add(self, k):
        self.n += k
        return self.n


class Prog:
    ENGS = ("sync", "gpsimd", "tensor", "vector", "scalar")

    def __init__(self, nc):
        self.nc = nc
        self.q = {e: [] for e in self.ENGS}
        self.last_wait = {e: {} for e in self.ENGS}

    def op(self, eng, fn, waits=(), inc=None, inck=1):
        lw = self.last_wait[eng]
        wlist = []
        for s, v in waits:
            if v <= 0 or lw.get(id(s), 0) >= v:
                continue
            lw[id(s)] = v
            wlist.append((s, v))
        if inc is not None:
            inc.add(inck)
        self.q[eng].append((tuple(wlist), fn, inc, inck))

    def replay(self, block):
        for e in self.ENGS:
            if not self.q[e]:
                continue
            qq = self.q[e]

            def body(h, qq=qq):
                for waits, fn, inc, inck in qq:
                    for s, v in waits:
                        h.wait_ge(s.h, v)
                    if fn is None:
                        if inc is not None:
                            h.nop().then_inc(inc.h, inck)
                        continue
                    ins = fn(h)
                    if inc is not None:
                        ins.then_inc(inc.h, inck)
            getattr(block, e)(body)


def _f64_rope():
    inv = 1.0 / THETA ** (np.arange(0, HD, 2, dtype=np.float64) / HD)
    t = np.arange(S, dtype=np.float64)
    fr = np.outer(t, inv)
    emb = np.concatenate([fr, fr], axis=1)
    cos, sin = np.cos(emb), np.sin(emb)
    sinS = sin.copy()
    sinS[:, :64] *= -1.0
    return cos, sinS


def host_rope_table():
    cos, sinS = _f64_rope()
    cos2, sinS2 = np.tile(cos, (B, 1)), np.tile(sinS, (B, 1))
    tab = np.concatenate([cos2, sinS2, cos2 * SCALE, sinS2 * SCALE], axis=1)
    return np.ascontiguousarray(tab).astype(ml_dtypes.bfloat16)


def declare_p_inputs(nc):
    io = {}
    io["x_full"] = nc.dram_tensor("x_full", [T, D], F32, kind="ExternalInput")
    io["wP"] = nc.dram_tensor("wP", [D, 1280], F32, kind="ExternalInput")
    io["w_lqT"] = nc.dram_tensor("w_lqT", [1024, D], F32, kind="ExternalInput")
    io["w_lk_c"] = nc.dram_tensor("w_lk_c", [1024, 512], F32, kind="ExternalInput")
    io["ropeT"] = nc.dram_tensor("ropeT", [T, 512], BF16, kind="ExternalInput")
    return io


def declare_p_scratch(nc, external):
    kw = {"kind": "ExternalOutput"} if external else {}
    io = {}
    io["qT_d"] = nc.dram_tensor("qT_d", [4, 128, T], BF16, **kw)
    io["kT_d"] = nc.dram_tensor("kT_d", [128, T], BF16, **kw)
    io["lkT_d"] = nc.dram_tensor("lkT_d", [4, 128, T], BF16, **kw)
    io["v_d"] = nc.dram_tensor("v_d", [T, 128], BF16, **kw)
    io["lv_d"] = nc.dram_tensor("lv_d", [4, T, 128], BF16, **kw)
    io["norm_d"] = nc.dram_tensor("norm_d", [9, T], BF16, **kw)
    return io


def emit_phase_p(pr, ctx, io, sm):
    nc = pr.nc
    E = ctx.enter_context

    wlkc = E(nc.sbuf_tensor("p_wlkc", [128, ND, 512], BF16))
    with contextlib.ExitStack() as tmp:
        wlqT_sb = tmp.enter_context(nc.sbuf_tensor("p_wlqT", [128, 8, D], BF16))
        wlk_sb = tmp.enter_context(nc.sbuf_tensor("p_wlk", [128, 8, 512], BF16))
        plkc0 = tmp.enter_context(nc.psum_tensor("p_plkc0", [128, 512], F32))
        s_w = sm["p_w"]
        s_pre = sm["p_pre"]
        pr.op("gpsimd", lambda h: h.dma_start(
            out=wlqT_sb[:, :, :],
            in_=io["w_lqT"].ap().rearrange("(c p) d -> p c d", p=128)),
            inc=s_w, inck=16)
        pr.op("gpsimd", lambda h: h.dma_start(
            out=wlk_sb[:, :, :],
            in_=io["w_lk_c"].ap().rearrange("(c p) f -> p c f", p=128)),
            inc=s_w, inck=16)
        for dc in range(ND):
            def mmf(h, dc=dc):
                for lc in range(8):
                    ins = h.matmul(plkc0[:, :], wlqT_sb[:, lc, dc * 128:(dc + 1) * 128],
                                   wlk_sb[:, lc, :], start=(lc == 0), stop=(lc == 7))
                return ins
            pr.op("tensor", mmf, waits=[(s_w, 32), (s_pre, 2 * dc)], inc=s_pre)
            pr.op("vector", lambda h, dc=dc: h.tensor_copy(wlkc[:, dc, :], plkc0[:, :]),
                  waits=[(s_pre, 2 * dc + 1)], inc=s_pre)

    xs = [E(nc.sbuf_tensor(f"p_x{i}", [128, D], BF16)) for i in range(2)]
    xn = [E(nc.sbuf_tensor(f"p_xn{i}", [128, D], BF16)) for i in range(2)]
    st = [E(nc.sbuf_tensor(f"p_st{i}", [128, 8], F32)) for i in range(2)]
    xnt = [E(nc.sbuf_tensor(f"p_xnt{i}", [128, ND, 128], BF16)) for i in range(2)]
    ws = E(nc.sbuf_tensor("p_ws", [128, ND, 1280], BF16))
    rtab = [E(nc.sbuf_tensor(f"p_rtab{i}", [128, 4, 128], BF16)) for i in range(2)]
    qkl = E(nc.sbuf_tensor("p_qkl", [128, 9, 128], BF16))
    ropeb = E(nc.sbuf_tensor("p_ropeb", [128, 9, 128], BF16))
    ropet = E(nc.sbuf_tensor("p_ropet", [128, 9, 128], BF16))
    vst = E(nc.sbuf_tensor("p_vst", [128, 4, 5, 128], BF16))
    tst = E(nc.sbuf_tensor("p_tst", [128, 4, 9, 128], BF16))
    sqscr = E(nc.sbuf_tensor("p_sqscr", [128, 128], BF16))
    qn2 = E(nc.sbuf_tensor("p_qn2", [128, NT, 9], F32))
    nrows = E(nc.sbuf_tensor("p_nrows", [9, T], BF16))
    epsv = E(nc.sbuf_tensor("p_epsv", [128, 1], F32))
    ones_bf = E(nc.sbuf_tensor("p_onesb", [128, 128], BF16))
    ident_bf = E(nc.sbuf_tensor("p_identb", [128, 128], BF16))
    ones_f = E(nc.sbuf_tensor("p_onesf", [128, 128], F32))
    ident_f = E(nc.sbuf_tensor("p_identf", [128, 128], F32))
    pq = E(nc.psum_tensor("p_pq", [128, 512], F32))
    pkv = E(nc.psum_tensor("p_pkv", [128, 512], F32))
    plkc = E(nc.psum_tensor("p_plkc", [128, 512], F32))
    plv = E(nc.psum_tensor("p_plv", [128, 512], F32))
    ptb = [E(nc.psum_tensor(f"p_ptb{i}", [128, 1024], BF16)) for i in range(2)]
    pnt = [E(nc.psum_tensor(f"p_pnt{i}", [128, 512], F32)) for i in range(2)]

    s_w = sm["p_w"]; s_pre = sm["p_pre"]
    s_x = sm["p_x"]; s_rt = sm["p_rt"]
    s_ln = sm["p_ln"]; s_lns = sm["p_lns"]; s_xn = sm["p_xn"]
    s_tp = sm["p_tp"]; s_tpe = sm["p_tpe"]
    s_mm = sm["p_mm"]; s_ev = sm["p_ev"]
    s_ro = sm["p_ro"]; s_sc = sm["p_sc"]
    s_out = sm["p_out"]; s_nt = sm["p_nt"]

    pr.op("gpsimd", lambda h: h.memset(epsv[:, :], EPS), waits=[(sm["p_pre"], 64)])
    pr.op("gpsimd", lambda h: h.memset(ones_bf[:, :], 1.0))
    pr.op("gpsimd", lambda h: h.memset(ones_f[:, :], 1.0))
    pr.op("gpsimd", lambda h: h.affine_select(
        out=ident_bf[:, :], in_=ones_bf[:, :], pattern=[[1, 128]],
        compare_op=ALU.is_equal, fill=0.0, base=0, channel_multiplier=-1),
        inc=s_w)
    pr.op("gpsimd", lambda h: h.affine_select(
        out=ident_f[:, :], in_=ones_f[:, :], pattern=[[1, 128]],
        compare_op=ALU.is_equal, fill=0.0, base=0, channel_multiplier=-1),
        inc=s_w)
    pr.op("gpsimd", lambda h: h.dma_start(
        out=ws[:, :, :], in_=io["wP"].ap().rearrange("(c p) f -> p c f", p=128)),
        inc=s_w, inck=16)
    W_RDY = s_w.n

    TPg = 6
    ROg = 7
    EVg = 5
    SCg = 9
    LNg = 4
    LSg = 3

    for g in range(NT):
        par = g % 2
        strip, tt = g // 4, g % 4
        pr.op("gpsimd", lambda h, g=g, par=par: h.dma_start(
            out=xs[par][:, :], in_=io["x_full"][g * 128:(g + 1) * 128, :]),
            waits=[(s_xn, g - 1)], inc=s_x, inck=16)
        pr.op("sync", lambda h, g=g, par=par: h.dma_start(
            out=rtab[par][:, :, :],
            in_=io["ropeT"][g * 128:(g + 1) * 128, :]
                .rearrange("p (f d) -> p f d", d=128)),
            waits=[(s_ro, (g - 1) * ROg)] + ([(s_pre, 64)] if g < 2 else []),
            inc=s_rt, inck=16)
        pr.op("vector", lambda h, par=par: h.reduce_sum(
            st[par][:, 0:1], xs[par][:, :], axis=AX.X),
            waits=[(s_x, (g + 1) * 16), (s_xn, g - 1)], inc=s_ln)
        pr.op("scalar", lambda h, par=par: h.activation(
            xn[par][:, :], xs[par][:, :], AF.Square, accum_out=st[par][:, 1:2]),
            waits=[(s_x, (g + 1) * 16), (s_tp, ((g - 2) * TPg + 4) if g >= 2 else 0)],
            inc=s_lns)
        pr.op("vector", lambda h, par=par: h.tensor_scalar(
            out=st[par][:, 2:3], in0=st[par][:, 0:1], scalar1=1.0 / D, scalar2=None,
            op0=ALU.mult, op1=ALU.bypass), waits=[(s_ln, s_ln.n)], inc=s_ln)
        pr.op("vector", lambda h, par=par: h.tensor_scalar(
            out=st[par][:, 3:4], in0=st[par][:, 2:3], scalar1=st[par][:, 2:3],
            scalar2=None, op0=ALU.mult, op1=ALU.bypass),
            waits=[(s_ln, s_ln.n)], inc=s_ln)
        pr.op("vector", lambda h, par=par: h.tensor_scalar(
            out=st[par][:, 4:5], in0=st[par][:, 1:2], scalar1=1.0 / D,
            scalar2=st[par][:, 3:4], op0=ALU.mult, op1=ALU.subtract),
            waits=[(s_ln, s_ln.n), (s_lns, s_lns.n)], inc=s_ln)
        pr.op("scalar", lambda h, par=par: h.activation(
            st[par][:, 5:6], st[par][:, 4:5], AF.Ln, bias=epsv[:, 0:1]),
            waits=[(s_ln, s_ln.n)], inc=s_lns)
        pr.op("scalar", lambda h, par=par: h.activation(
            st[par][:, 6:7], st[par][:, 5:6], AF.Exp, scale=-0.5),
            waits=[(s_lns, s_lns.n)], inc=s_lns)
        pr.op("vector", lambda h, par=par: h.tensor_scalar(
            out=st[par][:, 7:8], in0=st[par][:, 2:3], scalar1=st[par][:, 6:7],
            scalar2=-1.0, op0=ALU.mult, op1=ALU.mult),
            waits=[(s_lns, s_lns.n)], inc=s_ln)
        pr.op("scalar", lambda h, par=par: h.activation(
            xn[par][:, :], xs[par][:, :], AF.Identity,
            bias=st[par][:, 7:8], scale=st[par][:, 6:7]),
            waits=[(s_ln, s_ln.n)], inc=s_xn)
        for j in range(4):
            G = g * TPg + j
            def tpf(h, j=j, par=par):
                for k in range(8):
                    dc = j * 8 + k
                    ins = h.transpose(ptb[(g * TPg + j) % 2][:, k * 128:(k + 1) * 128],
                                      xn[par][:, dc * 128:(dc + 1) * 128],
                                      ident_bf[:, :])
                return ins
            pr.op("tensor", tpf,
                  waits=[(s_xn, g + 1), (s_w, W_RDY), (s_tpe, G - 1)]
                        + ([(s_pre, 64)] if g == 0 else []), inc=s_tp)
            pr.op("vector", lambda h, j=j, par=par, G=G: h.tensor_copy(
                xnt[par][:, j * 8:(j + 1) * 8, :].rearrange("p c d -> p (c d)"),
                ptb[G % 2][:, 0:1024]),
                waits=[(s_tp, G + 1)], inc=s_tpe)
        segs = [(pq, 0, 512), (pkv, 512, 768), (None, 0, 512), (plv, 768, 1280)]
        for si, (pslot, lo, hi) in enumerate(segs):
            def mmf(h, si=si, pslot=pslot, lo=lo, hi=hi, par=par):
                for dc in range(ND):
                    if si == 2:
                        ins = h.matmul(plkc[:, 0:512], xnt[par][:, dc, :],
                                       wlkc[:, dc, :], start=(dc == 0),
                                       stop=(dc == ND - 1))
                    else:
                        ins = h.matmul(pslot[:, 0:hi - lo], xnt[par][:, dc, :],
                                       ws[:, dc, lo:hi], start=(dc == 0),
                                       stop=(dc == ND - 1))
                return ins
            pr.op("tensor", mmf,
                  waits=[(s_tpe, g * TPg + 4), (s_ev, g * EVg), (s_pre, 64)],
                  inc=s_mm)
        M0 = g * 4
        evw = [(s_ro, (g - 1) * ROg + 6), (s_out, (strip - 1) * 14 * 16)]
        pr.op("scalar", lambda h: h.copy(
            qkl[:, 0:4, :].rearrange("p h d -> p (h d)"), pq[:, 0:512]),
            waits=[(s_mm, M0 + 1)] + evw, inc=s_ev)
        pr.op("scalar", lambda h: h.copy(qkl[:, 4, :], pkv[:, 0:128]),
              waits=[(s_mm, M0 + 2)], inc=s_ev)
        pr.op("scalar", lambda h, tt=tt: h.copy(vst[:, tt, 0, :], pkv[:, 128:256]),
              inc=s_ev)
        pr.op("scalar", lambda h: h.copy(
            qkl[:, 5:9, :].rearrange("p h d -> p (h d)"), plkc[:, 0:512]),
            waits=[(s_mm, M0 + 3)], inc=s_ev)
        pr.op("scalar", lambda h, tt=tt: h.copy(
            vst[:, tt, 1:5, :].rearrange("p h d -> p (h d)"), plv[:, 0:512]),
            waits=[(s_mm, M0 + 4)], inc=s_ev)
        RO0 = g * ROg
        groups = [(0, 4, 0), (4, 1, 2), (5, 4, 2)]
        for (h0, nh, tf) in groups:
            pr.op("vector", lambda h, h0=h0, nh=nh, tf=tf, par=par: h.tensor_tensor(
                ropeb[:, h0:h0 + nh, :], qkl[:, h0:h0 + nh, :],
                bass.AP(rtab[par], tf * 128, [[4 * 128, 128], [0, nh], [1, 128]]),
                op=ALU.mult),
                waits=[(s_ev, g * EVg + EVg), (s_rt, (g + 1) * 16),
                       (s_tp, g * TPg),
                       (s_sc, g * SCg)],
                inc=s_ro)
        for (h0, nh, tf) in groups:
            pr.op("vector", lambda h, h0=h0, nh=nh, tf=tf, par=par: h.tensor_tensor(
                bass.AP(ropet, h0 * 128, [[9 * 128, 128], [128, nh], [64, 2], [1, 64]]),
                bass.AP(qkl, h0 * 128 + 64, [[9 * 128, 128], [128, nh], [-64, 2], [1, 64]]),
                bass.AP(rtab[par], (tf + 1) * 128, [[4 * 128, 128], [0, nh], [64, 2], [1, 64]]),
                op=ALU.mult), inc=s_ro)
        pr.op("vector", lambda h: h.tensor_tensor(
            ropeb[:, :, :], ropeb[:, :, :], ropet[:, :, :], op=ALU.add),
            waits=[(s_ro, RO0 + 6)], inc=s_ro)
        for hh in range(9):
            pr.op("scalar", lambda h, hh=hh, g=g: h.activation(
                sqscr[:, :], ropeb[:, hh, :], AF.Square,
                accum_out=qn2[:, g, hh:hh + 1]),
                waits=[(s_ro, RO0 + ROg)], inc=s_sc)
        for j, (o0, cnt) in enumerate([(0, 8), (8, 1)]):
            G = g * TPg + 4 + j
            def tpq(h, o0=o0, cnt=cnt, G=G):
                for k in range(cnt):
                    ins = h.transpose(ptb[G % 2][:, k * 128:(k + 1) * 128],
                                      ropeb[:, o0 + k, :], ident_bf[:, :])
                return ins
            pr.op("tensor", tpq, waits=[(s_ro, RO0 + ROg), (s_tpe, G - 1)], inc=s_tp)
            pr.op("vector", lambda h, o0=o0, cnt=cnt, G=G, tt=tt: h.tensor_copy(
                tst[:, tt, o0:o0 + cnt, :].rearrange("p h d -> p (h d)"),
                ptb[G % 2][:, 0:cnt * 128]),
                waits=[(s_tp, G + 1), (s_out, (strip - 1) * 14 * 16)], inc=s_tpe)
        if tt == 3:
            OUTW = [(s_tpe, (g + 1) * TPg), (s_ev, (g + 1) * EVg)]
            for hh in range(4):
                pr.op("sync", lambda h, s=strip, hh=hh: h.dma_start(
                    out=io["qT_d"].ap()[hh, :, s * 512:(s + 1) * 512]
                        .rearrange("p (q d) -> p q d", q=4),
                    in_=bass.AP(tst, hh * 128,
                                [[4 * 9 * 128, 128], [9 * 128, 4], [1, 128]])),
                    waits=OUTW, inc=s_out, inck=16)
            pr.op("sync", lambda h, s=strip: h.dma_start(
                out=io["kT_d"].ap()[:, s * 512:(s + 1) * 512]
                    .rearrange("p (q d) -> p q d", q=4),
                in_=bass.AP(tst, 4 * 128,
                            [[4 * 9 * 128, 128], [9 * 128, 4], [1, 128]])),
                inc=s_out, inck=16)
            for hh in range(4):
                pr.op("sync", lambda h, s=strip, hh=hh: h.dma_start(
                    out=io["lkT_d"].ap()[hh, :, s * 512:(s + 1) * 512]
                        .rearrange("p (q d) -> p q d", q=4),
                    in_=bass.AP(tst, (5 + hh) * 128,
                                [[4 * 9 * 128, 128], [9 * 128, 4], [1, 128]])),
                    inc=s_out, inck=16)
            pr.op("sync", lambda h, s=strip: h.dma_start(
                out=io["v_d"].ap()[s * 512:(s + 1) * 512, :]
                    .rearrange("(q p) d -> p q d", p=128),
                in_=vst[:, :, 0, :]), inc=s_out, inck=16)
            for hh in range(4):
                pr.op("sync", lambda h, s=strip, hh=hh: h.dma_start(
                    out=io["lv_d"].ap()[hh, s * 512:(s + 1) * 512, :]
                        .rearrange("(q p) d -> p q d", p=128),
                    in_=vst[:, :, 1 + hh, :]), inc=s_out, inck=16)

    for g in range(NT):
        pr.op("tensor", lambda h, g=g: h.transpose(
            pnt[g % 2][0:9, 0:128], qn2[:, g, :], ident_f[:, :]),
            waits=[(s_sc, (g + 1) * SCg), (s_nt, 2 * g - 1)], inc=s_nt)
        pr.op("vector", lambda h, g=g: h.tensor_copy(
            nrows[0:9, g * 128:(g + 1) * 128], pnt[g % 2][0:9, 0:128]),
            waits=[(s_nt, 2 * g + 1)], inc=s_nt)
    pr.op("scalar", lambda h: h.activation(nrows[0:9, :], nrows[0:9, :], AF.Ln),
          waits=[(s_nt, 2 * NT)], inc=s_sc)
    pr.op("scalar", lambda h: h.activation(nrows[0:9, :], nrows[0:9, :], AF.Exp,
                                           scale=0.5),
          waits=[(s_sc, s_sc.n)], inc=s_sc)
    pr.op("sync", lambda h: h.dma_start(out=io["norm_d"].ap()[0:9, :],
                                        in_=nrows[0:9, :]),
          waits=[(s_sc, s_sc.n)], inc=s_out, inck=16)
    return {"nrows": nrows}


def make_p_sems(stack, nc):
    names = ["p_w", "p_pre", "p_x", "p_rt", "p_ln", "p_lns", "p_xn", "p_tp",
             "p_tpe", "p_mm", "p_ev", "p_ro", "p_sc", "p_out", "p_nt"]
    return {n: Sem(stack, nc, n) for n in names}


def build_p_debug():
    nc = bass.Bass("TRN2", target_bir_lowering=False)
    io = declare_p_inputs(nc)
    io.update(declare_p_scratch(nc, external=True))
    with contextlib.ExitStack() as ctx:
        sm = make_p_sems(ctx, nc)
        pr = Prog(nc)
        emit_phase_p(pr, ctx, io, sm)
        pr.op("sync", None, waits=[(sm["p_out"], sm["p_out"].n)])
        block = ctx.enter_context(nc.Block())
        pr.replay(block)
    return nc


def host_phase_p_inputs(c, inp):
    qs = slice(512 * c, 512 * c + 512)
    ks = slice(128 * c, 128 * c + 128)
    wP = np.concatenate(
        [inp["L0_attn_w_q"][:, qs], inp["L0_attn_w_k"][:, ks],
         inp["L0_attn_w_v"][:, ks], inp["L0_attn_w_lv"][:, qs]], axis=1)
    return {
        "x_full": np.ascontiguousarray(
            np.asarray(inp["L0_attn_hidden_states"], dtype=np.float32).reshape(T, D)),
        "wP": np.ascontiguousarray(wP, dtype=np.float32),
        "w_lqT": np.ascontiguousarray(np.asarray(inp["L0_attn_w_lq"]).T, dtype=np.float32),
        "w_lk_c": np.ascontiguousarray(np.asarray(inp["L0_attn_w_lk"])[:, qs], dtype=np.float32),
        "ropeT": host_rope_table(),
    }


def declare_a_out(nc, external):
    kw = {"kind": "ExternalOutput"} if external else {}
    io = {"contrib0": nc.dram_tensor("contrib0", [1024, S], BF16, **kw),
          "contrib1": nc.dram_tensor("contrib1", [1024, S], BF16, **kw)}
    import os as _os
    if _os.environ.get("DBG_SIG"):
        io["dbg_sig"] = nc.dram_tensor("dbg_sig", [128, 512], F32,
                                       kind="ExternalOutput")
    return io


def make_a_sems(stack, nc):
    names = ["a_kv", "a_q", "a_n", "a_mb", "a_mbe", "a_qk", "a_sub", "a_exp",
             "a_msk", "a_pv", "a_srw", "a_oev", "a_out", "a_kmx", "a_rcp"]
    return {n: Sem(stack, nc, n) for n in names}


def emit_phase_a(pr, ctx, io, sm, after=(), post_b0=None):
    nc = pr.nc
    E = ctx.enter_context

    kTs = [E(nc.sbuf_tensor(f"a_kT{i}", [128, S], BF16)) for i in range(2)]
    vs = [E(nc.sbuf_tensor(f"a_v{i}", [128, 32, 128], BF16)) for i in range(2)]
    qTs = [E(nc.sbuf_tensor(f"a_qT{i}", [128, S], BF16)) for i in range(2)]
    mnst0 = E(nc.sbuf_tensor("a_mnst", [1, S], BF16))
    mnst = [mnst0, mnst0]
    knst = mnst0
    mrow0 = E(nc.sbuf_tensor("a_mrow", [1, S], BF16))
    mrow = [mrow0, mrow0]
    kmx = E(nc.sbuf_tensor("a_kmx", [1, 16], F32))
    pbuf = [E(nc.sbuf_tensor(f"a_pbuf{i}", [128, 2048], BF16)) for i in range(2)]
    msn = [E(nc.sbuf_tensor(f"a_msn{i}", [128, 1], BF16)) for i in range(2)]
    msb = [E(nc.sbuf_tensor(f"a_msb{i}", [128, 1], F32)) for i in range(2)]
    ones512 = E(nc.sbuf_tensor("a_ones512", [128, 512], BF16))
    mask4 = E(nc.sbuf_tensor("a_mask4", [128, 4, 512], BF16))
    pbuf2 = E(nc.sbuf_tensor("a_pbuf2", [128, 2048], BF16))
    rrow = E(nc.sbuf_tensor("a_rrow", [128, 512], BF16))
    rrowf = E(nc.sbuf_tensor("a_rrowf", [1, 512], F32))
    rb = [E(nc.sbuf_tensor(f"a_rb{i}", [128, 512], F32)) for i in range(2)]
    outs = [E(nc.sbuf_tensor(f"a_out{i}", [128, 512], BF16)) for i in range(2)]
    outs2 = [E(nc.sbuf_tensor(f"a_out2{i}", [128, 512], F32)) for i in range(2)]
    ones1b = E(nc.sbuf_tensor("a_ones1b", [1, 128], BF16))
    ones1f = E(nc.sbuf_tensor("a_ones1f", [1, 128], F32))
    ones128 = E(nc.sbuf_tensor("a_ones128", [128, 1], BF16))
    onescol = E(nc.sbuf_tensor("a_onescol", [128, 128], BF16))
    e0b = E(nc.sbuf_tensor("a_e0b", [128, 128], BF16))
    e0f = E(nc.sbuf_tensor("a_e0f", [128, 128], F32))

    ps = [E(nc.psum_tensor(f"a_ps{i}", [128, 1024], F32)) for i in range(2)]
    po = [E(nc.psum_tensor(f"a_po{i}", [128, 512], F32)) for i in range(2)]
    psm = E(nc.psum_tensor("a_psm", [128, 512], F32))
    pmb = E(nc.psum_tensor("a_pmb", [128, 512], F32))

    s_kv = sm["a_kv"]; s_q = sm["a_q"]; s_n = sm["a_n"]
    s_mb = sm["a_mb"]; s_mbe = sm["a_mbe"]
    s_qk = sm["a_qk"]; s_sub = sm["a_sub"]; s_exp = sm["a_exp"]
    s_msk = sm["a_msk"]; s_pv = sm["a_pv"]; s_srw = sm["a_srw"]
    s_oev = sm["a_oev"]; s_out = sm["a_out"]; s_kmx = sm["a_kmx"]
    s_oev2 = sm["a_sub"]; s_rcp = sm["a_rcp"]

    pr.op("gpsimd", lambda h: h.memset(ones1b[:, :], 1.0), waits=list(after))
    pr.op("gpsimd", lambda h: h.memset(ones1f[:, :], 1.0))
    pr.op("gpsimd", lambda h: h.memset(ones512[:, :], 1.0))
    for jj in range(4):
        pr.op("gpsimd", lambda h, jj=jj: h.affine_select(
            out=mask4[:, jj, :], in_=ones512[:, :], pattern=[[1, 512]],
            compare_op=ALU.is_ge, fill=0.0, base=-jj * 128,
            channel_multiplier=-1))
    pr.op("gpsimd", lambda h: h.memset(onescol[:, :], 0.0))
    pr.op("gpsimd", lambda h: h.memset(onescol[:, 0:1], 1.0))
    pr.op("gpsimd", lambda h: h.memset(e0b[:, :], 0.0))
    pr.op("gpsimd", lambda h: h.memset(e0b[0:1, :], 1.0))
    pr.op("gpsimd", lambda h: h.memset(e0f[:, :], 0.0))
    pr.op("gpsimd", lambda h: h.memset(e0f[0:1, :], 1.0))
    import os as _os3
    pr.op("gpsimd", lambda h: h.memset(
        rrow[:, :], 0.5 if _os3.environ.get("ABL_SIG1") else 0.0))
    for i in range(2):
        pr.op("gpsimd", lambda h, i=i: h.memset(msn[i][:, :], 0.0))
    import os as _os
    if _os.environ.get("ABL_RB1"):
        for i in range(2):
            pr.op("gpsimd", lambda h, i=i: h.memset(rb[i][:, :], 1.0))
    pr.op("gpsimd", lambda h: h.memset(ones128[:, :], 1.0), inc=s_kmx)

    runs = []
    for b in range(B):
        runs.append(("attn", b, [0, 1, 2, 3], 4, 0))
        for hl in range(4):
            runs.append(("lat", b, [hl], 5 + hl, 512))

    for r, (kind, b, qheads, krow, outbase) in enumerate(runs):
        pr.op("sync", lambda h, b=b, krow=krow: h.dma_start(
            out=knst[0:1, :], in_=io["norm_d"][krow:krow + 1, b * S:(b + 1) * S]),
            waits=list(after) + [(s_kmx, s_kmx.n)], inc=s_n, inck=16)
        pr.op("vector", lambda h, r=r: h.reduce_max(
            kmx[0:1, r:r + 1], knst[0:1, :], axis=AX.X),
            waits=[(s_n, s_n.n), (s_kmx, s_kmx.n)], inc=s_kmx)
    KMX_RDY = s_kmx.n

    qk_after_run, pv_after_run = [], []
    qk_after_qrun = []
    last_exp_half = [0, 0]
    last_pv_par = [0, 0]
    last_pv_diag = 0
    mnst_rd = [0, 0]
    mrow_used = 0
    msb_used = [0, 0]
    qrun = 0
    strip_g = 0

    for r, (kind, b, qheads, krow, outbase) in enumerate(runs):
        kvp = r % 2
        tok0 = b * S
        kv_waits = list(after) if r == 0 else \
            [(s_qk, qk_after_run[r - 2]), (s_pv, pv_after_run[r - 2])] if r >= 2 else []
        if kind == "attn":
            pr.op("sync", lambda h, kvp=kvp, tok0=tok0: h.dma_start(
                out=kTs[kvp][:, :], in_=io["kT_d"][:, tok0:tok0 + S]),
                waits=kv_waits, inc=s_kv, inck=16)
            pr.op("sync", lambda h, kvp=kvp, tok0=tok0: h.dma_start(
                out=vs[kvp][:, :, :],
                in_=io["v_d"][tok0:tok0 + S, :].rearrange("(c p) d -> p c d", p=128)),
                inc=s_kv, inck=16)
        else:
            hl = qheads[0]
            pr.op("sync", lambda h, kvp=kvp, tok0=tok0, hl=hl: h.dma_start(
                out=kTs[kvp][:, :], in_=io["lkT_d"][hl, :, tok0:tok0 + S]),
                waits=kv_waits, inc=s_kv, inck=16)
            pr.op("sync", lambda h, kvp=kvp, tok0=tok0, hl=hl: h.dma_start(
                out=vs[kvp][:, :, :],
                in_=io["lv_d"][hl, tok0:tok0 + S, :].rearrange("(c p) d -> p c d", p=128)),
                inc=s_kv, inck=16)
        KV_RDY = s_kv.n

        for qh in qheads:
            qp = qrun % 2
            qgate = [(s_qk, qk_after_qrun[qrun - 2])] if qrun >= 2 else list(after)
            pr.op("sync", lambda h, qp=qp, qh=qh, tok0=tok0: h.dma_start(
                out=qTs[qp][:, :], in_=io["qT_d"][qh, :, tok0:tok0 + S]),
                waits=qgate, inc=s_q, inck=16)
            Q_RDY = s_q.n
            pr.op("sync", lambda h, qp=qp, qh=qh, tok0=tok0: h.dma_start(
                out=mnst[qp][0:1, :], in_=io["norm_d"][qh:qh + 1, tok0:tok0 + S]),
                waits=[(s_srw, s_srw.n), (s_kmx, KMX_RDY)], inc=s_n, inck=16)
            N_RDY = s_n.n
            pr.op("vector", lambda h, qp=qp, r=r: h.tensor_scalar(
                out=mrow[qp][0:1, :], in0=mnst[qp][0:1, :],
                scalar1=kmx[0:1, r:r + 1], scalar2=-1.0,
                op0=ALU.mult, op1=ALU.mult),
                waits=[(s_kmx, KMX_RDY), (s_n, N_RDY)],
                inc=s_srw)
            MR = s_srw.n
            mnst_rd[qp] = MR

            steps = [(s, g2) for s in range(8) for g2 in range(s + 1)]
            prev = None

            def emit_pv(st):
                s, g2 = st
                sp = (strip_g0 + s) % 2
                NC = 4 * (s + 1)
                diag = (g2 == s)
                gp = g2 % 2
                src = pbuf2 if diag else pbuf[gp]
                def pvf(h, g2=g2, src=src, kvp=kvp, NC=NC, sp=sp):
                    for j in range(4):
                        kc = g2 * 4 + j
                        h.matmul(po[sp][:, :], vs[kvp][:, kc, :],
                                 src[:, j * 512:(j + 1) * 512],
                                 start=(kc == 0), stop=(kc == NC - 1))
                    for j in range(4):
                        kc = g2 * 4 + j
                        ins = h.matmul(psm[:, :], onescol[:, :],
                                       src[:, j * 512:(j + 1) * 512],
                                       start=(kc == 0), stop=(kc == NC - 1))
                    return ins
                pv_waits = [(s_exp, exp_of[st])]
                if diag:
                    pv_waits.append((s_msk, msk_of[st]))
                if g2 == 0:
                    pv_waits += [(s_oev, max(0, strip_g0 + s - 1)),
                                 (s_srw, s_srw.n), (s_rcp, s_rcp.n)]
                pr.op("tensor", pvf, waits=pv_waits, inc=s_pv, inck=4)
                if diag:
                    nonlocal last_pv_diag
                    last_pv_diag = s_pv.n
                    last_pv_par[gp] = s_pv.n
                else:
                    last_pv_par[gp] = s_pv.n
                if diag:
                    emit_epilogue(s)

            def emit_epilogue(s):
                msb_used[(strip_g0 + s) % 2] = s_exp.n
                sp = (strip_g0 + s) % 2
                sg = strip_g0 + s
                q0 = s * 512
                import os as _os
                if _os.environ.get("ABL_SIG1"):
                    pr.op("vector", lambda h: h.reciprocal(rrowf[0:1, :], psm[0:1, :]),
                          waits=[(s_pv, s_pv.n), (s_kmx, 1)], inc=s_srw)
                    pr.op("tensor", lambda h: h.matmul(
                        pmb[:, :], e0b[:, :], rrow[0:128, :], start=True, stop=True),
                        waits=[(s_srw, s_srw.n), (s_mbe, s_mbe.n)], inc=s_mb)
                elif not _os.environ.get("ABL_RB1"):
                    pr.op("scalar", lambda h: h.activation(
                        rrowf[0:1, :], psm[0:1, :], AF.Ln),
                        waits=[(s_pv, s_pv.n), (s_kmx, 1), (s_qk, s_qk.n)],
                        inc=s_rcp)
                    pr.op("scalar", lambda h: h.activation(
                        rrow[0:1, :], rrowf[0:1, :], AF.Exp, scale=-1.0),
                        waits=[(s_rcp, s_rcp.n)], inc=s_rcp)
                    if "dbg_sig" in io:
                        pr.op("sync", lambda h, sg=sg: h.dma_start(
                            out=io["dbg_sig"][sg:sg + 1, :], in_=rrowf[0:1, :]),
                            waits=[(s_rcp, s_rcp.n)], inc=s_out, inck=16)
                    pr.op("tensor", lambda h: h.matmul(
                        pmb[:, :], e0b[:, :], rrow[0:128, :], start=True, stop=True),
                        waits=[(s_rcp, s_rcp.n), (s_mbe, s_mbe.n)], inc=s_mb)
                    import os as _os2
                    if _os2.environ.get("ABL_SCE"):
                        pr.op("scalar", lambda h, sp=sp: h.copy(rb[sp][:, :], pmb[:, :]),
                              waits=[(s_mb, s_mb.n), (s_oev, max(0, sg - 1))], inc=s_mbe)
                    else:
                        pr.op("vector", lambda h, sp=sp: h.tensor_copy(rb[sp][:, :], pmb[:, :]),
                              waits=[(s_mb, s_mb.n), (s_oev, max(0, sg - 1))], inc=s_mbe)
                else:
                    pr.op("vector", lambda h: h.reciprocal(rrow[0:1, :], psm[0:1, :]),
                          waits=[(s_pv, s_pv.n), (s_kmx, 1)], inc=s_srw)
                pr.op("vector", lambda h, sp=sp: h.tensor_copy(
                    outs2[sp][:, :], po[sp][:, :]),
                    waits=[(s_mbe, s_mbe.n), (s_oev2, max(0, sg - 1))],
                    inc=s_oev2)
                pr.op("vector", lambda h, sp=sp: h.tensor_tensor(
                    out=outs[sp][:, :], in0=outs2[sp][:, :], in1=rb[sp][:, :],
                    op=ALU.mult),
                    waits=[(s_oev2, s_oev2.n), (s_out, 16 * max(0, sg - 1))],
                    inc=s_oev)
                row0 = outbase + qh * 128
                ct = io["contrib0"] if b == 0 else io["contrib1"]
                pr.op("sync", lambda h, sp=sp, row0=row0, q0=q0, ct=ct: h.dma_start(
                    out=ct[row0:row0 + 128, q0:q0 + 512],
                    in_=outs[sp][:, :]),
                    waits=[(s_oev, s_oev.n)], inc=s_out, inck=16)

            strip_g0 = strip_g
            exp_of, msk_of = {}, {}
            msb_rdy = {}

            def emit_msb_prep(s8):
                sp8 = (strip_g0 + s8) % 2
                pr.op("vector", lambda h, sp8=sp8, s8=s8, qp=qp: h.tensor_reduce(
                    msn[sp8][0:1, 0:1], mrow[qp][0:1, s8 * 512:(s8 + 1) * 512],
                    axis=AX.X, op=ALU.min),
                    waits=[(s_srw, MR), (s_exp, msb_used[sp8]), (s_kmx, 1)],
                    inc=s_srw)
                pr.op("tensor", lambda h, sp8=sp8: h.matmul(
                    pmb[:, 0:1], e0b[:, :], msn[sp8][0:128, 0:1],
                    start=True, stop=True),
                    waits=[(s_srw, s_srw.n), (s_mbe, s_mbe.n)], inc=s_mb)
                pr.op("vector", lambda h, sp8=sp8: h.tensor_copy(
                    msb[sp8][:, 0:1], pmb[:, 0:1]),
                    waits=[(s_mb, s_mb.n)], inc=s_mbe)
                msb_rdy[s8] = s_mbe.n
            for st in steps:
                s, g2 = st
                q0 = s * 512
                gp = g2 % 2
                diag = (g2 == s)
                if prev is not None:
                    emit_pv(prev)
                    prev = None
                sp8 = (strip_g0 + s) % 2
                if g2 == 0:
                    emit_msb_prep(s)
                for half in range(2):
                    def qkf(h, g2=g2, half=half, kvp=kvp, qp=qp, q0=q0):
                        for j in range(2):
                            kc = g2 * 4 + half * 2 + j
                            ins = h.matmul(
                                ps[half][:, j * 512:(j + 1) * 512],
                                kTs[kvp][:, kc * 128:(kc + 1) * 128],
                                qTs[qp][:, q0:q0 + 512],
                                start=True, stop=True)
                        return ins
                    pr.op("tensor", qkf,
                          waits=[(s_kv, KV_RDY), (s_q, Q_RDY),
                                 (s_exp, last_exp_half[half])],
                          inc=s_qk)
                    pr.op("scalar", lambda h, gp=gp, half=half, sp8=sp8: h.activation(
                        bass.AP(pbuf[gp], half * 1024, [[2048, 128], [1, 1024]]),
                        ps[half][:, :], AF.Exp, bias=msb[sp8][:, 0:1]),
                        waits=[(s_qk, s_qk.n), (s_pv, last_pv_par[gp]),
                               (s_mbe, msb_rdy[s])],
                        inc=s_exp)
                    last_exp_half[half] = s_exp.n
                exp_of[st] = s_exp.n
                if diag:
                    for j in range(4):
                        pr.op("vector", lambda h, gp=gp, j=j:
                              h.tensor_tensor(
                                  out=pbuf2[:, j * 512:(j + 1) * 512],
                                  in0=pbuf[gp][:, j * 512:(j + 1) * 512],
                                  in1=mask4[:, j, :], op=ALU.mult),
                              waits=[(s_exp, s_exp.n), (s_pv, last_pv_diag),
                                     (s_kmx, 1)],
                              inc=s_msk)
                    msk_of[st] = s_msk.n
                prev = st
            emit_pv(prev)
            strip_g += 8
            qk_after_qrun.append(s_qk.n)
            mrow_used = s_qk.n
            qrun += 1
        qk_after_run.append(s_qk.n)
        pv_after_run.append(s_pv.n)
        if r == 4:
            sm["a_out_b0"] = s_out.n
            if post_b0 is not None:
                post_b0()
    return {}


def build_pa_debug():
    nc = bass.Bass("TRN2", target_bir_lowering=False)
    io = declare_p_inputs(nc)
    io.update(declare_p_scratch(nc, external=True))
    io.update(declare_a_out(nc, external=True))
    with contextlib.ExitStack() as ctx:
        sm = make_p_sems(ctx, nc)
        sm.update(make_a_sems(ctx, nc))
        pr = Prog(nc)
        with contextlib.ExitStack() as pctx:
            emit_phase_p(pr, pctx, io, sm)
        gate = [(sm["p_out"], sm["p_out"].n)]
        emit_phase_a(pr, ctx, io, sm, after=gate)
        pr.op("sync", None, waits=[(sm["a_out"], sm["a_out"].n)])
        block = ctx.enter_context(nc.Block())
        pr.replay(block)
    return nc


def declare_c_inputs(nc, gathered_external):
    io = {}
    kwg = {"kind": "ExternalInput"} if gathered_external else {}
    io["gath0"] = nc.dram_tensor("gath0", [8192, S], BF16, **kwg)
    io["gath1"] = nc.dram_tensor("gath1", [8192, S], BF16, **kwg)
    io["wcat"] = nc.dram_tensor("wcat", [8192, 512], F32, kind="ExternalInput")
    io["x_sl"] = nc.dram_tensor("x_sl", [T, 512], F32, kind="ExternalInput")
    io["WbT_c"] = nc.dram_tensor("WbT_c", [D, 512], F32, kind="ExternalInput")
    io["AG_cat"] = nc.dram_tensor("AG_cat", [D, 72], F32, kind="ExternalInput")
    io["B_c"] = nc.dram_tensor("B_c", [64, 512], F32, kind="ExternalInput")
    io["y_sl"] = nc.dram_tensor("y_sl", [T, 512], F32, kind="ExternalOutput")
    io["o1_d"] = nc.dram_tensor("o1_d", [T, 512], BF16)
    io["x1_d"] = nc.dram_tensor("x1_d", [T, 512], BF16)
    io["ffn_d"] = nc.dram_tensor("ffn_d", [T, 512], BF16)
    for i in range(3):
        io[f"sti{i}"] = nc.dram_tensor(f"sti{i}", [2, 128, NT], F32)
        io[f"sto{i}"] = nc.dram_tensor(f"sto{i}", [2, 128, NT], F32)
    io["xfc0"] = nc.dram_tensor("xfc0", [512, S], BF16)
    io["xfc1"] = nc.dram_tensor("xfc1", [512, S], BF16)
    io["xfg0"] = nc.dram_tensor("xfg0", [D, S], BF16)
    io["xfg1"] = nc.dram_tensor("xfg1", [D, S], BF16)
    return io


def make_c_sems(stack, nc):
    names = ["c_w", "c_g", "c_mm1", "c_st", "c_sts", "c_ev", "c_dma", "c_cc",
             "c_fin", "c_x", "c_tp", "c_tpe", "c_mm3", "c_hg", "c_t2", "c_out",
             "c_fins"]
    return {n: Sem(stack, nc, n) for n in names}


def alloc_c_sbuf(nc, ctx):
    E = ctx.enter_context
    bufs = {}
    bufs["stats"] = E(nc.sbuf_tensor("c_stats", [128, 6, NT], F32))
    bufs["fin"] = E(nc.sbuf_tensor("c_fin", [128, 6, NT], F32))
    bufs["ftmp"] = E(nc.sbuf_tensor("c_ftmp", [128, 2, NT], F32))
    bufs["scr5"] = E(nc.sbuf_tensor("c_scr5", [128, 512], BF16))
    bufs["epsv"] = E(nc.sbuf_tensor("c_epsv", [128, 1], F32))
    bufs["ones_bf"] = E(nc.sbuf_tensor("c_onesb", [128, 128], BF16))
    bufs["ident_bf"] = E(nc.sbuf_tensor("c_identb", [128, 128], BF16))
    bufs["hg"] = E(nc.sbuf_tensor("c_hg", [128, 72], F32))
    bufs["t2b"] = E(nc.sbuf_tensor("c_t2b", [128, 40], F32))
    bufs["wd"] = E(nc.sbuf_tensor("c_wd", [128, 8], F32))
    bufs["hsc"] = E(nc.sbuf_tensor("c_hsc", [128, 64], BF16))
    bufs["hscT"] = E(nc.sbuf_tensor("c_hscT", [64, 128], BF16))
    bufs["xt"] = [E(nc.sbuf_tensor(f"c_xt{i}", [128, 512], F32)) for i in range(2)]
    bufs["aux"] = [E(nc.sbuf_tensor(f"c_aux{i}", [128, 512], BF16)) for i in range(2)]
    bufs["ev2"] = [E(nc.sbuf_tensor(f"c_ev2{i}", [128, 512], BF16)) for i in range(2)]
    _t1 = E(nc.sbuf_tensor("c_tmp1", [128, 512], F32))
    bufs["tmp1"] = [_t1, _t1]
    bufs["outt"] = bufs["xt"]
    bufs["xfst"] = [E(nc.sbuf_tensor(f"c_xfst{i}", [128, 4, 128], BF16)) for i in range(2)]
    bufs["wcat_sb"] = E(nc.sbuf_tensor("c_wcat", [128, 64, 512], BF16))
    return bufs


def emit_phase_c(pr, ctx, io, sm, after=(), upto=None, bufs=None,
                 after_w=None, cc_gate0=1, cc_gate1=1):
    nc = pr.nc
    E = ctx.enter_context

    s_w = sm["c_w"]; s_g = sm["c_g"]; s_mm1 = sm["c_mm1"]
    s_st = sm["c_st"]; s_sts = sm["c_sts"]; s_ev = sm["c_ev"]
    s_dma = sm["c_dma"]; s_cc = sm["c_cc"]; s_fin = sm["c_fin"]
    s_x = sm["c_x"]; s_tp = sm["c_tp"]; s_tpe = sm["c_tpe"]
    s_mm3 = sm["c_mm3"]; s_hg = sm["c_hg"]; s_t2 = sm["c_t2"]
    s_out = sm["c_out"]; s_fns = sm["c_fins"]

    if bufs is None:
        bufs = alloc_c_sbuf(nc, ctx)
    if after_w is None:
        after_w = list(after)
    stats = bufs["stats"]; fin = bufs["fin"]; ftmp = bufs["ftmp"]
    scr5 = bufs["scr5"]; epsv = bufs["epsv"]; ones_bf = bufs["ones_bf"]
    ident_bf = bufs["ident_bf"]; hg = bufs["hg"]; t2b = bufs["t2b"]
    wd = bufs["wd"]; hsc = bufs["hsc"]; hscT = bufs["hscT"]
    xt = bufs["xt"]; aux = bufs["aux"]; ev2 = bufs["ev2"]
    tmp1 = bufs["tmp1"]; outt = bufs["outt"]; xfst = bufs["xfst"]

    po1 = [E(nc.psum_tensor(f"c_po1{i}", [128, 512], F32)) for i in range(2)]
    pf = [E(nc.psum_tensor(f"c_pf{i}", [128, 512], F32)) for i in range(2)]
    phg = [E(nc.psum_tensor(f"c_phg{i}", [128, 512], F32)) for i in range(2)]
    ptb = [E(nc.psum_tensor(f"c_ptb{i}", [128, 1024], BF16)) for i in range(2)]

    pr.op("gpsimd", lambda h: h.memset(epsv[:, :], EPS), waits=list(after_w))
    pr.op("gpsimd", lambda h: h.memset(ones_bf[:, :], 1.0))
    pr.op("gpsimd", lambda h: h.affine_select(
        out=ident_bf[:, :], in_=ones_bf[:, :], pattern=[[1, 128]],
        compare_op=ALU.is_equal, fill=0.0, base=0, channel_multiplier=-1),
        inc=s_w)
    ID_RDY = s_w.n

    def stats_ar(idx):
        pr.op("sync", lambda h, idx=idx: h.dma_start(
            out=io[f"sti{idx}"].ap().rearrange("s p t -> p s t"),
            in_=stats[:, 2 * idx:2 * idx + 2, :]),
            waits=[(s_dma, s_dma.n)], inc=s_dma, inck=16)
        pr.op("gpsimd", lambda h, idx=idx: h.collective_compute(
            "AllReduce", ALU.add, replica_groups=[list(range(8))],
            ins=[io[f"sti{idx}"].ap().opt()], outs=[io[f"sto{idx}"].ap().opt()]),
            waits=[(s_dma, s_dma.n)], inc=s_cc)
        pr.op("sync", lambda h, idx=idx: h.dma_start(
            out=stats[:, 2 * idx:2 * idx + 2, :],
            in_=io[f"sto{idx}"].ap().rearrange("s p t -> p s t")),
            waits=[(s_cc, s_cc.n)], inc=s_dma, inck=16)
        AR = s_dma.n
        pr.op("vector", lambda h, idx=idx: h.tensor_scalar(
            out=fin[:, 2 * idx, :], in0=stats[:, 2 * idx, :],
            scalar1=1.0 / D, scalar2=None, op0=ALU.mult, op1=ALU.bypass),
            waits=[(s_dma, AR)], inc=s_fin)
        pr.op("vector", lambda h, idx=idx: h.tensor_tensor(
            out=ftmp[:, 0, :], in0=fin[:, 2 * idx, :], in1=fin[:, 2 * idx, :],
            op=ALU.mult), waits=[(s_fin, s_fin.n)], inc=s_fin)
        pr.op("vector", lambda h, idx=idx: h.tensor_scalar(
            out=ftmp[:, 1, :], in0=stats[:, 2 * idx + 1, :],
            scalar1=1.0 / D, scalar2=None, op0=ALU.mult, op1=ALU.bypass),
            waits=[(s_dma, AR)], inc=s_fin)
        pr.op("vector", lambda h: h.tensor_tensor(
            out=ftmp[:, 1, :], in0=ftmp[:, 1, :], in1=ftmp[:, 0, :],
            op=ALU.subtract), waits=[(s_fin, s_fin.n)], inc=s_fin)
        pr.op("scalar", lambda h: h.activation(
            ftmp[:, 0, :], ftmp[:, 1, :], AF.Ln, bias=epsv[:, 0:1]),
            waits=[(s_fin, s_fin.n)], inc=s_fns)
        pr.op("scalar", lambda h, idx=idx: h.activation(
            fin[:, 2 * idx + 1, :], ftmp[:, 0, :], AF.Exp, scale=-0.5),
            waits=[(s_fns, s_fns.n)], inc=s_fns)

    po1_rd = [0, 0]
    evdma = [0, 0]
    with contextlib.ExitStack() as c1ctx:
        wcat_sb = bufs["wcat_sb"]
        gbuf = [c1ctx.enter_context(nc.sbuf_tensor(f"c_gb{i}", [128, 64, 256], BF16))
                for i in range(3)]
        pr.op("gpsimd", lambda h: h.dma_start(
            out=wcat_sb[:, :, :],
            in_=io["wcat"].ap().rearrange("(c p) f -> p c f", p=128)),
            waits=list(after_w), inc=s_w, inck=16)
        W1_RDY = s_w.n
        for tg in range(NT // 2):
            gp = tg % 3
            gsrc = io["gath0"] if tg < 16 else io["gath1"]
            gcol = (tg % 16) * 256
            pr.op("sync", lambda h, gp=gp, gsrc=gsrc, gcol=gcol: h.dma_start(
                out=gbuf[gp][:, :, :],
                in_=gsrc[:, gcol:gcol + 256]
                    .rearrange("(c p) t -> p c t", p=128)),
                waits=[(s_mm1, max(0, 2 * tg - 4)),
                       (s_cc, cc_gate0 if tg < 16 else cc_gate1)]
                      + (list(after) if tg < 3 else []),
                inc=s_g, inck=16)
            G_RDY = s_g.n
            for ti in range(2):
                g = tg * 2 + ti
                par = g % 2
                def mm1(h, gp=gp, ti=ti, par=par):
                    for fc in range(64):
                        ins = h.matmul(po1[par][:, :],
                                       gbuf[gp][:, fc, ti * 128:(ti + 1) * 128],
                                       wcat_sb[:, fc, :],
                                       start=(fc == 0), stop=(fc == 63))
                    return ins
                pr.op("tensor", mm1,
                      waits=[(s_g, G_RDY), (s_w, W1_RDY), (s_ev, max(0, g - 1)),
                             (s_st, po1_rd[par])],
                      inc=s_mm1)
                MM = s_mm1.n
                if upto != "c1m":
                    pr.op("vector", lambda h, par=par, g=g: h.reduce_sum(
                        stats[:, 0, g:g + 1], po1[par][:, :], axis=AX.X),
                        waits=[(s_mm1, MM)], inc=s_st)
                    po1_rd[par] = s_st.n
                if upto not in ("c1m", "c1s"):
                    pr.op("scalar", lambda h, par=par, g=g: h.activation(
                        scr5[:, :], po1[par][:, :], AF.Square,
                        accum_out=stats[:, 1, g:g + 1]),
                        waits=[(s_mm1, MM), (s_st, po1_rd[par])], inc=s_sts)
                pr.op("scalar", lambda h, par=par: h.copy(
                    ev2[par][:, :], po1[par][:, :]),
                    waits=[(s_dma, evdma[par]), (s_mm1, MM)], inc=s_ev)
                pr.op("sync", lambda h, g=g, par=par: h.dma_start(
                    out=io["o1_d"][g * 128:(g + 1) * 128, :], in_=ev2[par][:, :]),
                    waits=[(s_ev, s_ev.n)], inc=s_dma, inck=16)
                evdma[par] = s_dma.n
    C1_MM = s_mm1.n
    if upto in ("c1", "c1m", "c1s"):
        pr.op("sync", None, waits=[(s_sts, s_sts.n), (s_st, s_st.n), (s_dma, s_dma.n)])
        return {}
    pr.op("sync", None, waits=[(s_sts, s_sts.n), (s_st, s_st.n), (s_dma, s_dma.n)],
          inc=s_dma, inck=1)
    stats_ar(0)
    FIN2 = s_fns.n
    if upto == "ar1":
        pr.op("sync", None, waits=[(s_fns, FIN2)])
        return {}

    x1_done = [0, 0]
    x1_st = [0, 0]
    for g in range(NT):
        par = g % 2
        pr.op("sync", lambda h, g=g, par=par: h.dma_start(
            out=xt[par][:, :], in_=io["x_sl"][g * 128:(g + 1) * 128, :]),
            waits=[(s_st, x1_st[par]), (s_dma, x1_done[par])], inc=s_x, inck=16)
        pr.op("sync", lambda h, g=g, par=par: h.dma_start(
            out=aux[par][:, :], in_=io["o1_d"][g * 128:(g + 1) * 128, :]),
            inc=s_x, inck=16)
        X_RDY = s_x.n
        pr.op("vector", lambda h, g=g, par=par: h.tensor_scalar(
            out=tmp1[par][:, :], in0=aux[par][:, :],
            scalar1=fin[:, 0, g:g + 1], scalar2=fin[:, 1, g:g + 1],
            op0=ALU.subtract, op1=ALU.mult),
            waits=[(s_x, X_RDY), (s_fns, FIN2)], inc=s_st)
        pr.op("vector", lambda h, par=par: h.tensor_tensor(
            out=aux[par][:, :], in0=tmp1[par][:, :], in1=xt[par][:, :], op=ALU.add),
            waits=[(s_st, s_st.n)], inc=s_st)
        X1_RDY = s_st.n
        pr.op("vector", lambda h, par=par, g=g: h.reduce_sum(
            stats[:, 2, g:g + 1], aux[par][:, :], axis=AX.X),
            waits=[(s_st, X1_RDY)], inc=s_st)
        x1_st[par] = s_st.n
        pr.op("scalar", lambda h, par=par, g=g: h.activation(
            scr5[:, :], aux[par][:, :], AF.Square, accum_out=stats[:, 3, g:g + 1]),
            waits=[(s_st, X1_RDY)], inc=s_sts)
        pr.op("sync", lambda h, g=g, par=par: h.dma_start(
            out=io["x1_d"][g * 128:(g + 1) * 128, :], in_=aux[par][:, :]),
            waits=[(s_sts, s_sts.n)], inc=s_dma, inck=16)
        x1_done[par] = s_dma.n
    if upto == "c2a":
        pr.op("sync", None, waits=[(s_sts, s_sts.n), (s_st, s_st.n), (s_dma, s_dma.n)])
        return {}
    pr.op("sync", None, waits=[(s_sts, s_sts.n), (s_st, s_st.n), (s_dma, s_dma.n)],
          inc=s_dma, inck=1)
    stats_ar(1)
    FIN3 = s_fns.n

    xf_dma = [0, 0]
    for g in range(NT):
        par = g % 2
        pr.op("sync", lambda h, g=g, par=par: h.dma_start(
            out=aux[par][:, :], in_=io["x1_d"][g * 128:(g + 1) * 128, :]),
            waits=[(s_st, x1_st[par]), (s_fns, FIN3)], inc=s_x, inck=16)
        X_RDY = s_x.n
        pr.op("vector", lambda h, g=g, par=par: h.tensor_scalar(
            out=scr5[:, :], in0=aux[par][:, :],
            scalar1=fin[:, 2, g:g + 1], scalar2=fin[:, 3, g:g + 1],
            op0=ALU.subtract, op1=ALU.mult),
            waits=[(s_x, X_RDY), (s_fns, FIN3), (s_tp, max(0, s_tp.n))], inc=s_st)
        x1_st[par] = s_st.n
        XF_RDY = s_st.n
        def tpf(h, g=g):
            for j in range(4):
                ins = h.transpose(ptb[g % 2][:, j * 128:(j + 1) * 128],
                                  scr5[:, j * 128:(j + 1) * 128], ident_bf[:, :])
            return ins
        pr.op("tensor", tpf, waits=[(s_st, XF_RDY), (s_w, ID_RDY),
                                    (s_tpe, max(0, g - 1))], inc=s_tp)
        pr.op("vector", lambda h, g=g: h.tensor_copy(
            xfst[g % 2][:, :, :].rearrange("p c t -> p (c t)"),
            ptb[g % 2][:, 0:512]),
            waits=[(s_tp, s_tp.n), (s_dma, xf_dma[g % 2])], inc=s_tpe)
        xfd = io["xfc0"] if g < 32 else io["xfc1"]
        gc = (g % 32) * 128
        pr.op("sync", lambda h, xfd=xfd, gc=gc, g=g: h.dma_start(
            out=xfd[:, gc:gc + 128].rearrange("(c p) t -> p c t", p=128),
            in_=xfst[g % 2][:, :, :]),
            waits=[(s_tpe, s_tpe.n)], inc=s_dma, inck=16)
        xf_dma[g % 2] = s_dma.n
        if g == 31:
            pr.op("gpsimd", lambda h: h.collective_compute(
                "AllGather", ALU.bypass, replica_groups=[list(range(8))],
                ins=[io["xfc0"].ap().opt()], outs=[io["xfg0"].ap().opt()]),
                waits=[(s_dma, s_dma.n)], inc=s_cc)
            AG2a = s_cc.n
    if upto == "c2b":
        pr.op("sync", None, waits=[(s_dma, s_dma.n)])
        return {}
    pr.op("gpsimd", lambda h: h.collective_compute(
        "AllGather", ALU.bypass, replica_groups=[list(range(8))],
        ins=[io["xfc1"].ap().opt()], outs=[io["xfg1"].ap().opt()]),
        waits=[(s_dma, s_dma.n)], inc=s_cc)
    AG2 = s_cc.n

    pf_rd = [0, 0]
    ev3 = [0, 0]
    with contextlib.ExitStack() as c3ctx:
        wb_sb = c3ctx.enter_context(nc.sbuf_tensor("c_wb", [128, ND, 512], BF16))
        ag_sb = c3ctx.enter_context(nc.sbuf_tensor("c_ag", [128, ND, 72], BF16))
        b_sb = c3ctx.enter_context(nc.sbuf_tensor("c_b", [64, 512], BF16))
        xfb = [c3ctx.enter_context(nc.sbuf_tensor(f"c_xfb{i}", [128, ND, 256], BF16))
               for i in range(3)]
        pr.op("gpsimd", lambda h: h.dma_start(
            out=wb_sb[:, :, :],
            in_=io["WbT_c"].ap().rearrange("(c p) f -> p c f", p=128)),
            waits=[(s_mm1, C1_MM)], inc=s_w, inck=16)
        pr.op("gpsimd", lambda h: h.dma_start(
            out=ag_sb[:, :, :],
            in_=io["AG_cat"].ap().rearrange("(c p) f -> p c f", p=128)),
            inc=s_w, inck=16)
        pr.op("gpsimd", lambda h: h.dma_start(
            out=b_sb[0:64, :], in_=io["B_c"].ap()), inc=s_w, inck=16)
        W3_RDY = s_w.n
        for tg in range(NT // 2):
            gp = tg % 3
            xsrc = io["xfg0"] if tg < 16 else io["xfg1"]
            xcol = (tg % 16) * 256
            pr.op("sync", lambda h, gp=gp, xsrc=xsrc, xcol=xcol: h.dma_start(
                out=xfb[gp][:, :, :],
                in_=xsrc[:, xcol:xcol + 256]
                    .rearrange("(c p) t -> p c t", p=128)),
                waits=[(s_cc, AG2a if tg < 16 else AG2),
                       (s_mm3, max(0, 2 * (2 * tg - 4)))],
                inc=s_g, inck=16)
            G_RDY = s_g.n
            for ti in range(2):
                g = tg * 2 + ti
                par = g % 2
                def mmf(h, gp=gp, ti=ti, par=par):
                    for dc in range(ND):
                        h.matmul(pf[par][:, :],
                                 xfb[gp][:, dc, ti * 128:(ti + 1) * 128],
                                 wb_sb[:, dc, :],
                                 start=(dc == 0), stop=False)
                        ins = h.matmul(phg[par][:, 0:72],
                                       xfb[gp][:, dc, ti * 128:(ti + 1) * 128],
                                       ag_sb[:, dc, :],
                                       start=(dc == 0), stop=(dc == ND - 1))
                    return ins
                pr.op("tensor", mmf,
                      waits=[(s_g, G_RDY), (s_w, W3_RDY), (s_hg, max(0, g - 1)),
                             (s_ev, ev3[par]), (s_st, pf_rd[par])],
                      inc=s_mm3)
                MM = s_mm3.n
                pr.op("vector", lambda h, par=par: h.tensor_copy(
                    hg[:, :], phg[par][:, 0:72]), waits=[(s_mm3, MM)], inc=s_hg)
                HG = s_hg.n
                pr.op("vector", lambda h: h.reduce_max(
                    t2b[:, 0:1], hg[:, 64:72], axis=AX.X),
                    waits=[(s_hg, HG), (s_t2, s_t2.n)], inc=s_t2)
                pr.op("vector", lambda h: h.tensor_scalar(
                    out=t2b[:, 1:2], in0=t2b[:, 0:1], scalar1=-1.0, scalar2=None,
                    op0=ALU.mult, op1=ALU.bypass),
                    waits=[(s_t2, s_t2.n)], inc=s_t2)
                pr.op("scalar", lambda h: h.activation(
                    t2b[:, 8:16], hg[:, 64:72], AF.Exp, bias=t2b[:, 1:2]),
                    waits=[(s_t2, s_t2.n)], inc=s_t2)
                pr.op("vector", lambda h: h.reduce_max(
                    t2b[:, 2:3], t2b[:, 8:16], axis=AX.X),
                    waits=[(s_t2, s_t2.n)], inc=s_t2)
                pr.op("vector", lambda h: h.tensor_scalar(
                    out=t2b[:, 16:24], in0=t2b[:, 8:16], scalar1=t2b[:, 2:3],
                    scalar2=None, op0=ALU.is_ge, op1=ALU.bypass),
                    waits=[(s_t2, s_t2.n)], inc=s_t2)
                pr.op("vector", lambda h: h.tensor_scalar(
                    out=t2b[:, 24:32], in0=t2b[:, 16:24], scalar1=2.0, scalar2=None,
                    op0=ALU.mult, op1=ALU.bypass),
                    waits=[(s_t2, s_t2.n)], inc=s_t2)
                pr.op("vector", lambda h: h.tensor_tensor(
                    out=t2b[:, 24:32], in0=t2b[:, 8:16], in1=t2b[:, 24:32],
                    op=ALU.subtract), waits=[(s_t2, s_t2.n)], inc=s_t2)
                pr.op("vector", lambda h: h.reduce_max(
                    t2b[:, 3:4], t2b[:, 24:32], axis=AX.X),
                    waits=[(s_t2, s_t2.n)], inc=s_t2)
                pr.op("vector", lambda h: h.tensor_scalar(
                    out=t2b[:, 32:40], in0=t2b[:, 24:32], scalar1=t2b[:, 3:4],
                    scalar2=None, op0=ALU.is_ge, op1=ALU.bypass),
                    waits=[(s_t2, s_t2.n)], inc=s_t2)
                pr.op("vector", lambda h: h.tensor_tensor(
                    out=t2b[:, 16:24], in0=t2b[:, 16:24], in1=t2b[:, 32:40],
                    op=ALU.add), waits=[(s_t2, s_t2.n)], inc=s_t2)
                pr.op("vector", lambda h: h.tensor_tensor(
                    out=t2b[:, 4:5], in0=t2b[:, 2:3], in1=t2b[:, 3:4],
                    op=ALU.add), waits=[(s_t2, s_t2.n)], inc=s_t2)
                pr.op("vector", lambda h: h.reciprocal(
                    t2b[:, 5:6], t2b[:, 4:5]), waits=[(s_t2, s_t2.n)], inc=s_t2)
                pr.op("vector", lambda h: h.tensor_tensor(
                    out=t2b[:, 8:16], in0=t2b[:, 8:16], in1=t2b[:, 16:24],
                    op=ALU.mult), waits=[(s_t2, s_t2.n)], inc=s_t2)
                pr.op("vector", lambda h: h.tensor_scalar(
                    out=wd[:, 0:8], in0=t2b[:, 8:16], scalar1=t2b[:, 5:6],
                    scalar2=2.0, op0=ALU.mult, op1=ALU.mult),
                    waits=[(s_t2, s_t2.n)], inc=s_t2)
                for e in range(8):
                    pr.op("vector", lambda h, e=e: h.tensor_scalar(
                        out=hsc[:, e * 8:(e + 1) * 8], in0=hg[:, e * 8:(e + 1) * 8],
                        scalar1=wd[:, e:e + 1], scalar2=None,
                        op0=ALU.mult, op1=ALU.bypass),
                        waits=[(s_t2, s_t2.n)], inc=s_t2)
                pr.op("tensor", lambda h, g=g: h.transpose(
                    ptb[g % 2][0:64, 0:128], hsc[:, :], ident_bf[:, :]),
                    waits=[(s_t2, s_t2.n), (s_tpe, s_tpe.n), (s_w, ID_RDY)],
                    inc=s_tp)
                pr.op("vector", lambda h, g=g: h.tensor_copy(
                    hscT[0:64, :], ptb[g % 2][0:64, 0:128]),
                    waits=[(s_tp, s_tp.n)], inc=s_tpe)
                pr.op("tensor", lambda h, par=par: h.matmul(
                    pf[par][:, :], hscT[0:64, :], b_sb[0:64, :],
                    start=False, stop=True),
                    waits=[(s_tpe, s_tpe.n)], inc=s_mm3)
                MM2 = s_mm3.n
                pr.op("vector", lambda h, par=par, g=g: h.reduce_sum(
                    stats[:, 4, g:g + 1], pf[par][:, :], axis=AX.X),
                    waits=[(s_mm3, MM2)], inc=s_st)
                pf_rd[par] = s_st.n
                pr.op("scalar", lambda h, par=par, g=g: h.activation(
                    scr5[:, :], pf[par][:, :], AF.Square,
                    accum_out=stats[:, 5, g:g + 1]),
                    waits=[(s_mm3, MM2), (s_st, pf_rd[par])], inc=s_sts)
                pr.op("scalar", lambda h, par=par: h.copy(
                    ev2[par][:, :], pf[par][:, :]),
                    waits=[(s_dma, evdma[par])], inc=s_ev)
                ev3[par] = s_ev.n
                pr.op("sync", lambda h, g=g, par=par: h.dma_start(
                    out=io["ffn_d"][g * 128:(g + 1) * 128, :], in_=ev2[par][:, :]),
                    waits=[(s_ev, s_ev.n)], inc=s_dma, inck=16)
                evdma[par] = s_dma.n
    if upto == "c3":
        pr.op("sync", None, waits=[(s_sts, s_sts.n), (s_st, s_st.n), (s_dma, s_dma.n)])
        return {}
    pr.op("sync", None, waits=[(s_sts, s_sts.n), (s_st, s_st.n), (s_dma, s_dma.n)],
          inc=s_dma, inck=1)
    stats_ar(2)
    FIN4 = s_fns.n

    c4_st = [0, 0]
    for g in range(NT):
        par = g % 2
        pr.op("sync", lambda h, g=g, par=par: h.dma_start(
            out=ev2[par][:, :], in_=io["x1_d"][g * 128:(g + 1) * 128, :]),
            waits=[(s_out, 16 * max(0, g - 1)), (s_st, c4_st[par]), (s_fns, FIN4),
                   (s_dma, evdma[par])],
            inc=s_x, inck=16)
        pr.op("sync", lambda h, g=g, par=par: h.dma_start(
            out=aux[par][:, :], in_=io["ffn_d"][g * 128:(g + 1) * 128, :]),
            inc=s_x, inck=16)
        X_RDY = s_x.n
        pr.op("vector", lambda h, g=g, par=par: h.tensor_scalar(
            out=tmp1[par][:, :], in0=aux[par][:, :],
            scalar1=fin[:, 4, g:g + 1], scalar2=fin[:, 5, g:g + 1],
            op0=ALU.subtract, op1=ALU.mult),
            waits=[(s_x, X_RDY), (s_fns, FIN4)], inc=s_st)
        pr.op("vector", lambda h, g=g, par=par: h.tensor_tensor(
            out=outt[par][:, :], in0=tmp1[par][:, :], in1=ev2[par][:, :], op=ALU.add),
            waits=[(s_st, s_st.n)], inc=s_st)
        c4_st[par] = s_st.n
        pr.op("sync", lambda h, g=g, par=par: h.dma_start(
            out=io["y_sl"][g * 128:(g + 1) * 128, :], in_=outt[par][:, :]),
            waits=[(s_st, s_st.n)], inc=s_out, inck=16)
    pr.op("sync", None, waits=[(s_out, s_out.n)])
    return {}


def build_c_debug(upto=None):
    nc = bass.Bass("TRN2", target_bir_lowering=False)
    io = declare_c_inputs(nc, gathered_external=True)
    with contextlib.ExitStack() as ctx:
        sm = make_c_sems(ctx, nc)
        pr = Prog(nc)
        emit_phase_c(pr, ctx, io, sm, upto=upto, cc_gate0=0, cc_gate1=0)
        block = ctx.enter_context(nc.Block())
        pr.replay(block)
    return nc


def host_phase_c_inputs(c, inp, gathered=None):
    Fc = slice(512 * c, 512 * c + 512)
    w_o = np.asarray(inp["L0_attn_w_o"], dtype=np.float32)
    w_lo = np.asarray(inp["L0_attn_w_lo"], dtype=np.float32)
    rows = []
    for r in range(8):
        fr = slice(4 * r * 128, (4 * r + 4) * 128)
        rows.append(w_o[fr, Fc])
        rows.append(w_lo[fr, Fc])
    wcat = np.ascontiguousarray(np.concatenate(rows, axis=0))
    A = np.asarray(inp["L0_ffn_A"], dtype=np.float32).reshape(64, D).T
    gw = np.asarray(inp["L0_ffn_gate_W"], dtype=np.float32).T
    Bm = np.asarray(inp["L0_ffn_B"], dtype=np.float32)
    B_perm = np.ascontiguousarray(Bm.transpose(0, 2, 1).reshape(64, D))
    out = {
        "wcat": wcat,
        "x_sl": np.ascontiguousarray(
            np.asarray(inp["L0_attn_hidden_states"], dtype=np.float32)
            .reshape(T, D)[:, Fc]),
        "WbT_c": np.ascontiguousarray(
            np.asarray(inp["L0_ffn_W_base"], dtype=np.float32).T[:, Fc]),
        "AG_cat": np.ascontiguousarray(np.concatenate([A, gw], axis=1)),
        "B_c": np.ascontiguousarray(B_perm[:, Fc]),
    }
    if gathered is not None:
        out["gath0"] = np.ascontiguousarray(gathered[:, 0:S])
        out["gath1"] = np.ascontiguousarray(gathered[:, S:])
    return out


def build_full():
    nc = bass.Bass("TRN2", target_bir_lowering=False)
    io = declare_p_inputs(nc)
    io.update(declare_p_scratch(nc, external=False))
    io.update(declare_a_out(nc, external=False))
    io.update(declare_c_inputs(nc, gathered_external=False))
    with contextlib.ExitStack() as ctx:
        sm = make_p_sems(ctx, nc)
        sm.update(make_a_sems(ctx, nc))
        sm.update(make_c_sems(ctx, nc))
        pr = Prog(nc)
        with contextlib.ExitStack() as pctx:
            emit_phase_p(pr, pctx, io, sm)
        gate_a = [(sm["p_out"], sm["p_out"].n)]
        cbufs = alloc_c_sbuf(nc, ctx)
        s_cc = sm["c_cc"]
        holder = {}

        def post_b0():
            pass
        with contextlib.ExitStack() as actx:
            emit_phase_a(pr, actx, io, sm, after=gate_a, post_b0=post_b0)
        pr.op("gpsimd", lambda h: h.collective_compute(
            "AllGather", ALU.bypass, replica_groups=[list(range(8))],
            ins=[io["contrib0"].ap().opt()], outs=[io["gath0"].ap().opt()]),
            waits=[(sm["a_out"], sm["a_out_b0"])], inc=s_cc)
        cg0 = s_cc.n
        pr.op("gpsimd", lambda h: h.collective_compute(
            "AllGather", ALU.bypass, replica_groups=[list(range(8))],
            ins=[io["contrib1"].ap().opt()], outs=[io["gath1"].ap().opt()]),
            waits=[(sm["a_out"], sm["a_out"].n)], inc=s_cc)
        cg1 = s_cc.n
        a_done = [(sm["a_qk"], sm["a_qk"].n), (sm["a_pv"], sm["a_pv"].n),
                  (sm["a_out"], sm["a_out"].n)]
        emit_phase_c(pr, ctx, io, sm, after=a_done, bufs=cbufs,
                     after_w=gate_a, cc_gate0=cg0, cc_gate1=cg1)
        block = ctx.enter_context(nc.Block())
        pr.replay(block)
    return nc


_NC_CACHE = {}


def kernel(**inputs):
    if "full" not in _NC_CACHE:
        _NC_CACHE["full"] = build_full()
    nc = _NC_CACHE["full"]
    in_maps = []
    for c in range(8):
        m = host_phase_p_inputs(c, inputs)
        m.update(host_phase_c_inputs(c, inputs))
        in_maps.append(m)
    res = run_bass_kernel_spmd(nc, in_maps, core_ids=list(range(8)))
    y = np.concatenate([res.results[c]["y_sl"] for c in range(8)], axis=1)
    return np.ascontiguousarray(y.reshape(B, S, D))


def kernel_unfused(**inputs):
    nc1 = build_pa_debug()
    in1 = [host_phase_p_inputs(c, inputs) for c in range(8)]
    res1 = run_bass_kernel_spmd(nc1, in1, core_ids=list(range(8)))
    gathered = np.concatenate(
        [np.concatenate([res1.results[c][f"contrib{i}"] for c in range(8)], axis=0)
         for i in range(2)], axis=1)
    nc2 = build_c_debug()
    in2 = [host_phase_c_inputs(c, inputs, gathered) for c in range(8)]
    res2 = run_bass_kernel_spmd(nc2, in2, core_ids=list(range(8)))
    y = np.concatenate([res2.results[c]["y_sl"] for c in range(8)], axis=1)
    return y.reshape(B, S, D)


# version 13
# speedup vs baseline: 1.0933x; 1.0933x over previous
import contextlib
import math

import numpy as np
import ml_dtypes
import concourse.bass as bass
import concourse.mybir as mybir
from concourse.bass_utils import run_bass_kernel_spmd

F32 = mybir.dt.float32
BF16 = mybir.dt.bfloat16
AF = mybir.ActivationFunctionType
ALU = mybir.AluOpType
AX = mybir.AxisListType

B, S, D = 2, 4096, 4096
T = B * S
HD = 128
SCALE = 1.0 / math.sqrt(HD)
EPS = 1e-5
THETA = 10000.0
NT = T // 128
ND = D // 128


class Sem:
    def __init__(self, stack, nc, name):
        self.h = stack.enter_context(nc.semaphore(name))
        self.n = 0

    def add(self, k):
        self.n += k
        return self.n


class Prog:
    ENGS = ("sync", "gpsimd", "tensor", "vector", "scalar")

    def __init__(self, nc):
        self.nc = nc
        self.q = {e: [] for e in self.ENGS}
        self.last_wait = {e: {} for e in self.ENGS}

    def op(self, eng, fn, waits=(), inc=None, inck=1):
        lw = self.last_wait[eng]
        wlist = []
        for s, v in waits:
            if v <= 0 or lw.get(id(s), 0) >= v:
                continue
            lw[id(s)] = v
            wlist.append((s, v))
        if inc is not None:
            inc.add(inck)
        self.q[eng].append((tuple(wlist), fn, inc, inck))

    def replay(self, block):
        for e in self.ENGS:
            if not self.q[e]:
                continue
            qq = self.q[e]

            def body(h, qq=qq):
                for waits, fn, inc, inck in qq:
                    for s, v in waits:
                        h.wait_ge(s.h, v)
                    if fn is None:
                        if inc is not None:
                            h.nop().then_inc(inc.h, inck)
                        continue
                    ins = fn(h)
                    if inc is not None:
                        ins.then_inc(inc.h, inck)
            getattr(block, e)(body)


def _f64_rope():
    inv = 1.0 / THETA ** (np.arange(0, HD, 2, dtype=np.float64) / HD)
    t = np.arange(S, dtype=np.float64)
    fr = np.outer(t, inv)
    emb = np.concatenate([fr, fr], axis=1)
    cos, sin = np.cos(emb), np.sin(emb)
    sinS = sin.copy()
    sinS[:, :64] *= -1.0
    return cos, sinS


def host_rope_table():
    cos, sinS = _f64_rope()
    cos2, sinS2 = np.tile(cos, (B, 1)), np.tile(sinS, (B, 1))
    tab = np.concatenate([cos2, sinS2, cos2 * SCALE, sinS2 * SCALE], axis=1)
    return np.ascontiguousarray(tab).astype(ml_dtypes.bfloat16)


def declare_p_inputs(nc):
    io = {}
    io["x_full"] = nc.dram_tensor("x_full", [T, D], F32, kind="ExternalInput")
    io["wP"] = nc.dram_tensor("wP", [D, 1280], F32, kind="ExternalInput")
    io["w_lqT"] = nc.dram_tensor("w_lqT", [1024, D], F32, kind="ExternalInput")
    io["w_lk_c"] = nc.dram_tensor("w_lk_c", [1024, 512], F32, kind="ExternalInput")
    io["ropeT"] = nc.dram_tensor("ropeT", [T, 512], BF16, kind="ExternalInput")
    return io


def declare_p_scratch(nc, external):
    kw = {"kind": "ExternalOutput"} if external else {}
    io = {}
    io["qT_d"] = nc.dram_tensor("qT_d", [4, 128, T], BF16, **kw)
    io["kT_d"] = nc.dram_tensor("kT_d", [128, T], BF16, **kw)
    io["lkT_d"] = nc.dram_tensor("lkT_d", [4, 128, T], BF16, **kw)
    io["v_d"] = nc.dram_tensor("v_d", [T, 128], BF16, **kw)
    io["lv_d"] = nc.dram_tensor("lv_d", [4, T, 128], BF16, **kw)
    io["norm_d"] = nc.dram_tensor("norm_d", [9, T], BF16, **kw)
    return io


def emit_phase_p(pr, ctx, io, sm):
    nc = pr.nc
    E = ctx.enter_context

    wlkc = E(nc.sbuf_tensor("p_wlkc", [128, ND, 512], BF16))
    with contextlib.ExitStack() as tmp:
        wlqT_sb = tmp.enter_context(nc.sbuf_tensor("p_wlqT", [128, 8, D], BF16))
        wlk_sb = tmp.enter_context(nc.sbuf_tensor("p_wlk", [128, 8, 512], BF16))
        plkc0 = tmp.enter_context(nc.psum_tensor("p_plkc0", [128, 512], F32))
        s_w = sm["p_w"]
        s_pre = sm["p_pre"]
        pr.op("gpsimd", lambda h: h.dma_start(
            out=wlqT_sb[:, :, :],
            in_=io["w_lqT"].ap().rearrange("(c p) d -> p c d", p=128)),
            inc=s_w, inck=16)
        pr.op("gpsimd", lambda h: h.dma_start(
            out=wlk_sb[:, :, :],
            in_=io["w_lk_c"].ap().rearrange("(c p) f -> p c f", p=128)),
            inc=s_w, inck=16)
        for dc in range(ND):
            def mmf(h, dc=dc):
                for lc in range(8):
                    ins = h.matmul(plkc0[:, :], wlqT_sb[:, lc, dc * 128:(dc + 1) * 128],
                                   wlk_sb[:, lc, :], start=(lc == 0), stop=(lc == 7))
                return ins
            pr.op("tensor", mmf, waits=[(s_w, 32), (s_pre, 2 * dc)], inc=s_pre)
            pr.op("vector", lambda h, dc=dc: h.tensor_copy(wlkc[:, dc, :], plkc0[:, :]),
                  waits=[(s_pre, 2 * dc + 1)], inc=s_pre)

    xs = [E(nc.sbuf_tensor(f"p_x{i}", [128, D], BF16)) for i in range(2)]
    xn = [E(nc.sbuf_tensor(f"p_xn{i}", [128, D], BF16)) for i in range(2)]
    st = [E(nc.sbuf_tensor(f"p_st{i}", [128, 8], F32)) for i in range(2)]
    xnt = [E(nc.sbuf_tensor(f"p_xnt{i}", [128, ND, 128], BF16)) for i in range(2)]
    ws = E(nc.sbuf_tensor("p_ws", [128, ND, 1280], BF16))
    rtab = [E(nc.sbuf_tensor(f"p_rtab{i}", [128, 4, 128], BF16)) for i in range(2)]
    qkl = E(nc.sbuf_tensor("p_qkl", [128, 9, 128], BF16))
    ropeb = E(nc.sbuf_tensor("p_ropeb", [128, 9, 128], BF16))
    ropet = E(nc.sbuf_tensor("p_ropet", [128, 9, 128], BF16))
    vst = E(nc.sbuf_tensor("p_vst", [128, 4, 5, 128], BF16))
    tst = E(nc.sbuf_tensor("p_tst", [128, 4, 9, 128], BF16))
    sqscr = E(nc.sbuf_tensor("p_sqscr", [128, 128], BF16))
    qn2 = E(nc.sbuf_tensor("p_qn2", [128, NT, 9], F32))
    nrows = E(nc.sbuf_tensor("p_nrows", [9, T], BF16))
    epsv = E(nc.sbuf_tensor("p_epsv", [128, 1], F32))
    ones_bf = E(nc.sbuf_tensor("p_onesb", [128, 128], BF16))
    ident_bf = E(nc.sbuf_tensor("p_identb", [128, 128], BF16))
    ones_f = E(nc.sbuf_tensor("p_onesf", [128, 128], F32))
    ident_f = E(nc.sbuf_tensor("p_identf", [128, 128], F32))
    pq = E(nc.psum_tensor("p_pq", [128, 512], F32))
    pkv = E(nc.psum_tensor("p_pkv", [128, 512], F32))
    plkc = E(nc.psum_tensor("p_plkc", [128, 512], F32))
    plv = E(nc.psum_tensor("p_plv", [128, 512], F32))
    ptb = [E(nc.psum_tensor(f"p_ptb{i}", [128, 1024], BF16)) for i in range(2)]
    pnt = [E(nc.psum_tensor(f"p_pnt{i}", [128, 512], F32)) for i in range(2)]

    s_w = sm["p_w"]; s_pre = sm["p_pre"]
    s_x = sm["p_x"]; s_rt = sm["p_rt"]
    s_ln = sm["p_ln"]; s_lns = sm["p_lns"]; s_xn = sm["p_xn"]
    s_tp = sm["p_tp"]; s_tpe = sm["p_tpe"]
    s_mm = sm["p_mm"]; s_ev = sm["p_ev"]
    s_ro = sm["p_ro"]; s_sc = sm["p_sc"]
    s_out = sm["p_out"]; s_nt = sm["p_nt"]

    pr.op("gpsimd", lambda h: h.memset(epsv[:, :], EPS), waits=[(sm["p_pre"], 64)])
    pr.op("gpsimd", lambda h: h.memset(ones_bf[:, :], 1.0))
    pr.op("gpsimd", lambda h: h.memset(ones_f[:, :], 1.0))
    pr.op("gpsimd", lambda h: h.affine_select(
        out=ident_bf[:, :], in_=ones_bf[:, :], pattern=[[1, 128]],
        compare_op=ALU.is_equal, fill=0.0, base=0, channel_multiplier=-1),
        inc=s_w)
    pr.op("gpsimd", lambda h: h.affine_select(
        out=ident_f[:, :], in_=ones_f[:, :], pattern=[[1, 128]],
        compare_op=ALU.is_equal, fill=0.0, base=0, channel_multiplier=-1),
        inc=s_w)
    pr.op("gpsimd", lambda h: h.dma_start(
        out=ws[:, :, :], in_=io["wP"].ap().rearrange("(c p) f -> p c f", p=128)),
        inc=s_w, inck=16)
    W_RDY = s_w.n

    TPg = 6
    ROg = 7
    EVg = 5
    SCg = 9
    LNg = 4
    LSg = 3

    for g in range(NT):
        par = g % 2
        strip, tt = g // 4, g % 4
        pr.op("gpsimd", lambda h, g=g, par=par: h.dma_start(
            out=xs[par][:, :], in_=io["x_full"][g * 128:(g + 1) * 128, :]),
            waits=[(s_xn, g - 1)], inc=s_x, inck=16)
        pr.op("sync", lambda h, g=g, par=par: h.dma_start(
            out=rtab[par][:, :, :],
            in_=io["ropeT"][g * 128:(g + 1) * 128, :]
                .rearrange("p (f d) -> p f d", d=128)),
            waits=[(s_ro, (g - 1) * ROg)] + ([(s_pre, 64)] if g < 2 else []),
            inc=s_rt, inck=16)
        pr.op("vector", lambda h, par=par: h.reduce_sum(
            st[par][:, 0:1], xs[par][:, :], axis=AX.X),
            waits=[(s_x, (g + 1) * 16), (s_xn, g - 1)], inc=s_ln)
        pr.op("scalar", lambda h, par=par: h.activation(
            xn[par][:, :], xs[par][:, :], AF.Square, accum_out=st[par][:, 1:2]),
            waits=[(s_x, (g + 1) * 16), (s_tp, ((g - 2) * TPg + 4) if g >= 2 else 0)],
            inc=s_lns)
        pr.op("vector", lambda h, par=par: h.tensor_scalar(
            out=st[par][:, 2:3], in0=st[par][:, 0:1], scalar1=1.0 / D, scalar2=None,
            op0=ALU.mult, op1=ALU.bypass), waits=[(s_ln, s_ln.n)], inc=s_ln)
        pr.op("vector", lambda h, par=par: h.tensor_scalar(
            out=st[par][:, 3:4], in0=st[par][:, 2:3], scalar1=st[par][:, 2:3],
            scalar2=None, op0=ALU.mult, op1=ALU.bypass),
            waits=[(s_ln, s_ln.n)], inc=s_ln)
        pr.op("vector", lambda h, par=par: h.tensor_scalar(
            out=st[par][:, 4:5], in0=st[par][:, 1:2], scalar1=1.0 / D,
            scalar2=st[par][:, 3:4], op0=ALU.mult, op1=ALU.subtract),
            waits=[(s_ln, s_ln.n), (s_lns, s_lns.n)], inc=s_ln)
        pr.op("scalar", lambda h, par=par: h.activation(
            st[par][:, 5:6], st[par][:, 4:5], AF.Ln, bias=epsv[:, 0:1]),
            waits=[(s_ln, s_ln.n)], inc=s_lns)
        pr.op("scalar", lambda h, par=par: h.activation(
            st[par][:, 6:7], st[par][:, 5:6], AF.Exp, scale=-0.5),
            waits=[(s_lns, s_lns.n)], inc=s_lns)
        pr.op("vector", lambda h, par=par: h.tensor_scalar(
            out=st[par][:, 7:8], in0=st[par][:, 2:3], scalar1=st[par][:, 6:7],
            scalar2=-1.0, op0=ALU.mult, op1=ALU.mult),
            waits=[(s_lns, s_lns.n)], inc=s_ln)
        pr.op("scalar", lambda h, par=par: h.activation(
            xn[par][:, :], xs[par][:, :], AF.Identity,
            bias=st[par][:, 7:8], scale=st[par][:, 6:7]),
            waits=[(s_ln, s_ln.n)], inc=s_xn)
        for j in range(4):
            G = g * TPg + j
            def tpf(h, j=j, par=par):
                for k in range(8):
                    dc = j * 8 + k
                    ins = h.transpose(ptb[(g * TPg + j) % 2][:, k * 128:(k + 1) * 128],
                                      xn[par][:, dc * 128:(dc + 1) * 128],
                                      ident_bf[:, :])
                return ins
            pr.op("tensor", tpf,
                  waits=[(s_xn, g + 1), (s_w, W_RDY), (s_tpe, G - 1)]
                        + ([(s_pre, 64)] if g == 0 else []), inc=s_tp)
            pr.op("vector", lambda h, j=j, par=par, G=G: h.tensor_copy(
                xnt[par][:, j * 8:(j + 1) * 8, :].rearrange("p c d -> p (c d)"),
                ptb[G % 2][:, 0:1024]),
                waits=[(s_tp, G + 1)], inc=s_tpe)
        segs = [(pq, 0, 512), (pkv, 512, 768), (None, 0, 512), (plv, 768, 1280)]
        for si, (pslot, lo, hi) in enumerate(segs):
            def mmf(h, si=si, pslot=pslot, lo=lo, hi=hi, par=par):
                for dc in range(ND):
                    if si == 2:
                        ins = h.matmul(plkc[:, 0:512], xnt[par][:, dc, :],
                                       wlkc[:, dc, :], start=(dc == 0),
                                       stop=(dc == ND - 1))
                    else:
                        ins = h.matmul(pslot[:, 0:hi - lo], xnt[par][:, dc, :],
                                       ws[:, dc, lo:hi], start=(dc == 0),
                                       stop=(dc == ND - 1))
                return ins
            pr.op("tensor", mmf,
                  waits=[(s_tpe, g * TPg + 4), (s_ev, g * EVg), (s_pre, 64)],
                  inc=s_mm)
        M0 = g * 4
        evw = [(s_ro, (g - 1) * ROg + 6), (s_out, (strip - 1) * 14 * 16)]
        pr.op("scalar", lambda h: h.copy(
            qkl[:, 0:4, :].rearrange("p h d -> p (h d)"), pq[:, 0:512]),
            waits=[(s_mm, M0 + 1)] + evw, inc=s_ev)
        pr.op("scalar", lambda h: h.copy(qkl[:, 4, :], pkv[:, 0:128]),
              waits=[(s_mm, M0 + 2)], inc=s_ev)
        pr.op("scalar", lambda h, tt=tt: h.copy(vst[:, tt, 0, :], pkv[:, 128:256]),
              inc=s_ev)
        pr.op("scalar", lambda h: h.copy(
            qkl[:, 5:9, :].rearrange("p h d -> p (h d)"), plkc[:, 0:512]),
            waits=[(s_mm, M0 + 3)], inc=s_ev)
        pr.op("scalar", lambda h, tt=tt: h.copy(
            vst[:, tt, 1:5, :].rearrange("p h d -> p (h d)"), plv[:, 0:512]),
            waits=[(s_mm, M0 + 4)], inc=s_ev)
        RO0 = g * ROg
        groups = [(0, 4, 0), (4, 1, 2), (5, 4, 2)]
        for (h0, nh, tf) in groups:
            pr.op("vector", lambda h, h0=h0, nh=nh, tf=tf, par=par: h.tensor_tensor(
                ropeb[:, h0:h0 + nh, :], qkl[:, h0:h0 + nh, :],
                bass.AP(rtab[par], tf * 128, [[4 * 128, 128], [0, nh], [1, 128]]),
                op=ALU.mult),
                waits=[(s_ev, g * EVg + EVg), (s_rt, (g + 1) * 16),
                       (s_tp, g * TPg),
                       (s_sc, g * SCg)],
                inc=s_ro)
        for (h0, nh, tf) in groups:
            pr.op("vector", lambda h, h0=h0, nh=nh, tf=tf, par=par: h.tensor_tensor(
                bass.AP(ropet, h0 * 128, [[9 * 128, 128], [128, nh], [64, 2], [1, 64]]),
                bass.AP(qkl, h0 * 128 + 64, [[9 * 128, 128], [128, nh], [-64, 2], [1, 64]]),
                bass.AP(rtab[par], (tf + 1) * 128, [[4 * 128, 128], [0, nh], [64, 2], [1, 64]]),
                op=ALU.mult), inc=s_ro)
        pr.op("vector", lambda h: h.tensor_tensor(
            ropeb[:, :, :], ropeb[:, :, :], ropet[:, :, :], op=ALU.add),
            waits=[(s_ro, RO0 + 6)], inc=s_ro)
        for hh in range(9):
            pr.op("scalar", lambda h, hh=hh, g=g: h.activation(
                sqscr[:, :], ropeb[:, hh, :], AF.Square,
                accum_out=qn2[:, g, hh:hh + 1]),
                waits=[(s_ro, RO0 + ROg)], inc=s_sc)
        for j, (o0, cnt) in enumerate([(0, 8), (8, 1)]):
            G = g * TPg + 4 + j
            def tpq(h, o0=o0, cnt=cnt, G=G):
                for k in range(cnt):
                    ins = h.transpose(ptb[G % 2][:, k * 128:(k + 1) * 128],
                                      ropeb[:, o0 + k, :], ident_bf[:, :])
                return ins
            pr.op("tensor", tpq, waits=[(s_ro, RO0 + ROg), (s_tpe, G - 1)], inc=s_tp)
            pr.op("vector", lambda h, o0=o0, cnt=cnt, G=G, tt=tt: h.tensor_copy(
                tst[:, tt, o0:o0 + cnt, :].rearrange("p h d -> p (h d)"),
                ptb[G % 2][:, 0:cnt * 128]),
                waits=[(s_tp, G + 1), (s_out, (strip - 1) * 14 * 16)], inc=s_tpe)
        if tt == 3:
            OUTW = [(s_tpe, (g + 1) * TPg), (s_ev, (g + 1) * EVg)]
            for hh in range(4):
                pr.op("sync", lambda h, s=strip, hh=hh: h.dma_start(
                    out=io["qT_d"].ap()[hh, :, s * 512:(s + 1) * 512]
                        .rearrange("p (q d) -> p q d", q=4),
                    in_=bass.AP(tst, hh * 128,
                                [[4 * 9 * 128, 128], [9 * 128, 4], [1, 128]])),
                    waits=OUTW, inc=s_out, inck=16)
            pr.op("sync", lambda h, s=strip: h.dma_start(
                out=io["kT_d"].ap()[:, s * 512:(s + 1) * 512]
                    .rearrange("p (q d) -> p q d", q=4),
                in_=bass.AP(tst, 4 * 128,
                            [[4 * 9 * 128, 128], [9 * 128, 4], [1, 128]])),
                inc=s_out, inck=16)
            for hh in range(4):
                pr.op("sync", lambda h, s=strip, hh=hh: h.dma_start(
                    out=io["lkT_d"].ap()[hh, :, s * 512:(s + 1) * 512]
                        .rearrange("p (q d) -> p q d", q=4),
                    in_=bass.AP(tst, (5 + hh) * 128,
                                [[4 * 9 * 128, 128], [9 * 128, 4], [1, 128]])),
                    inc=s_out, inck=16)
            pr.op("sync", lambda h, s=strip: h.dma_start(
                out=io["v_d"].ap()[s * 512:(s + 1) * 512, :]
                    .rearrange("(q p) d -> p q d", p=128),
                in_=vst[:, :, 0, :]), inc=s_out, inck=16)
            for hh in range(4):
                pr.op("sync", lambda h, s=strip, hh=hh: h.dma_start(
                    out=io["lv_d"].ap()[hh, s * 512:(s + 1) * 512, :]
                        .rearrange("(q p) d -> p q d", p=128),
                    in_=vst[:, :, 1 + hh, :]), inc=s_out, inck=16)

    for g in range(NT):
        pr.op("tensor", lambda h, g=g: h.transpose(
            pnt[g % 2][0:9, 0:128], qn2[:, g, :], ident_f[:, :]),
            waits=[(s_sc, (g + 1) * SCg), (s_nt, 2 * g - 1)], inc=s_nt)
        pr.op("vector", lambda h, g=g: h.tensor_copy(
            nrows[0:9, g * 128:(g + 1) * 128], pnt[g % 2][0:9, 0:128]),
            waits=[(s_nt, 2 * g + 1)], inc=s_nt)
    pr.op("scalar", lambda h: h.activation(nrows[0:9, :], nrows[0:9, :], AF.Ln),
          waits=[(s_nt, 2 * NT)], inc=s_sc)
    pr.op("scalar", lambda h: h.activation(nrows[0:9, :], nrows[0:9, :], AF.Exp,
                                           scale=0.5),
          waits=[(s_sc, s_sc.n)], inc=s_sc)
    pr.op("sync", lambda h: h.dma_start(out=io["norm_d"].ap()[0:9, :],
                                        in_=nrows[0:9, :]),
          waits=[(s_sc, s_sc.n)], inc=s_out, inck=16)
    return {"nrows": nrows}


def make_p_sems(stack, nc):
    names = ["p_w", "p_pre", "p_x", "p_rt", "p_ln", "p_lns", "p_xn", "p_tp",
             "p_tpe", "p_mm", "p_ev", "p_ro", "p_sc", "p_out", "p_nt"]
    return {n: Sem(stack, nc, n) for n in names}


def build_p_debug():
    nc = bass.Bass("TRN2", target_bir_lowering=False)
    io = declare_p_inputs(nc)
    io.update(declare_p_scratch(nc, external=True))
    with contextlib.ExitStack() as ctx:
        sm = make_p_sems(ctx, nc)
        pr = Prog(nc)
        emit_phase_p(pr, ctx, io, sm)
        pr.op("sync", None, waits=[(sm["p_out"], sm["p_out"].n)])
        block = ctx.enter_context(nc.Block())
        pr.replay(block)
    return nc


def host_phase_p_inputs(c, inp):
    qs = slice(512 * c, 512 * c + 512)
    ks = slice(128 * c, 128 * c + 128)
    wP = np.concatenate(
        [inp["L0_attn_w_q"][:, qs], inp["L0_attn_w_k"][:, ks],
         inp["L0_attn_w_v"][:, ks], inp["L0_attn_w_lv"][:, qs]], axis=1)
    return {
        "x_full": np.ascontiguousarray(
            np.asarray(inp["L0_attn_hidden_states"], dtype=np.float32).reshape(T, D)),
        "wP": np.ascontiguousarray(wP, dtype=np.float32),
        "w_lqT": np.ascontiguousarray(np.asarray(inp["L0_attn_w_lq"]).T, dtype=np.float32),
        "w_lk_c": np.ascontiguousarray(np.asarray(inp["L0_attn_w_lk"])[:, qs], dtype=np.float32),
        "ropeT": host_rope_table(),
    }


def declare_a_out(nc, external):
    kw = {"kind": "ExternalOutput"} if external else {}
    return {"contrib0": nc.dram_tensor("contrib0", [1024, S], BF16, **kw),
            "contrib1": nc.dram_tensor("contrib1", [1024, S], BF16, **kw)}


def make_a_sems(stack, nc):
    names = ["a_kv", "a_q", "a_n", "a_mb", "a_mbe", "a_qk", "a_sub", "a_exp",
             "a_msk", "a_pv", "a_srw", "a_oev", "a_out", "a_kmx"]
    return {n: Sem(stack, nc, n) for n in names}


def emit_phase_a(pr, ctx, io, sm, after=(), post_b0=None):
    nc = pr.nc
    E = ctx.enter_context

    kTs = [E(nc.sbuf_tensor(f"a_kT{i}", [128, S], BF16)) for i in range(2)]
    vs = [E(nc.sbuf_tensor(f"a_v{i}", [128, 32, 128], BF16)) for i in range(2)]
    qTs = [E(nc.sbuf_tensor(f"a_qT{i}", [128, S], BF16)) for i in range(2)]
    mnst0 = E(nc.sbuf_tensor("a_mnst", [1, S], BF16))
    mnst = [mnst0, mnst0]
    knst = mnst0
    mrow0 = E(nc.sbuf_tensor("a_mrow", [1, S], BF16))
    mrow = [mrow0, mrow0]
    kmx = E(nc.sbuf_tensor("a_kmx", [1, 16], F32))
    pbuf = [E(nc.sbuf_tensor(f"a_pbuf{i}", [128, 2048], BF16)) for i in range(2)]
    msn = [E(nc.sbuf_tensor(f"a_msn{i}", [1, 1], BF16)) for i in range(2)]
    msb = [E(nc.sbuf_tensor(f"a_msb{i}", [128, 1], F32)) for i in range(2)]
    pbuf2 = E(nc.sbuf_tensor("a_pbuf2", [128, 2048], BF16))
    rrow = E(nc.sbuf_tensor("a_rrow", [1, 512], F32))
    rb = [E(nc.sbuf_tensor(f"a_rb{i}", [128, 512], F32)) for i in range(2)]
    outs = [E(nc.sbuf_tensor(f"a_out{i}", [128, 512], BF16)) for i in range(2)]
    ones1b = E(nc.sbuf_tensor("a_ones1b", [1, 128], BF16))
    ones1f = E(nc.sbuf_tensor("a_ones1f", [1, 128], F32))
    ones128 = E(nc.sbuf_tensor("a_ones128", [128, 1], BF16))

    ps = [E(nc.psum_tensor(f"a_ps{i}", [128, 1024], F32)) for i in range(2)]
    po = [E(nc.psum_tensor(f"a_po{i}", [128, 512], F32)) for i in range(2)]
    psm = E(nc.psum_tensor("a_psm", [128, 512], F32))
    pmb = E(nc.psum_tensor("a_pmb", [128, 512], F32))

    s_kv = sm["a_kv"]; s_q = sm["a_q"]; s_n = sm["a_n"]
    s_mb = sm["a_mb"]; s_mbe = sm["a_mbe"]
    s_qk = sm["a_qk"]; s_sub = sm["a_sub"]; s_exp = sm["a_exp"]
    s_msk = sm["a_msk"]; s_pv = sm["a_pv"]; s_srw = sm["a_srw"]
    s_oev = sm["a_oev"]; s_out = sm["a_out"]; s_kmx = sm["a_kmx"]

    pr.op("gpsimd", lambda h: h.memset(ones1b[:, :], 1.0), waits=list(after))
    pr.op("gpsimd", lambda h: h.memset(ones1f[:, :], 1.0))
    pr.op("gpsimd", lambda h: h.memset(ones128[:, :], 1.0), inc=s_kmx)

    runs = []
    for b in range(B):
        runs.append(("attn", b, [0, 1, 2, 3], 4, 0))
        for hl in range(4):
            runs.append(("lat", b, [hl], 5 + hl, 512))

    for r, (kind, b, qheads, krow, outbase) in enumerate(runs):
        pr.op("sync", lambda h, b=b, krow=krow: h.dma_start(
            out=knst[0:1, :], in_=io["norm_d"][krow:krow + 1, b * S:(b + 1) * S]),
            waits=list(after) + [(s_kmx, s_kmx.n)], inc=s_n, inck=16)
        pr.op("vector", lambda h, r=r: h.reduce_max(
            kmx[0:1, r:r + 1], knst[0:1, :], axis=AX.X),
            waits=[(s_n, s_n.n), (s_kmx, s_kmx.n)], inc=s_kmx)
    KMX_RDY = s_kmx.n

    qk_after_run, pv_after_run = [], []
    qk_after_qrun = []
    last_exp_half = [0, 0]
    last_pv_par = [0, 0]
    last_pv_diag = 0
    mnst_rd = [0, 0]
    mrow_used = 0
    msb_used = [0, 0]
    qrun = 0
    strip_g = 0

    for r, (kind, b, qheads, krow, outbase) in enumerate(runs):
        kvp = r % 2
        tok0 = b * S
        kv_waits = list(after) if r == 0 else \
            [(s_qk, qk_after_run[r - 2]), (s_pv, pv_after_run[r - 2])] if r >= 2 else []
        if kind == "attn":
            pr.op("sync", lambda h, kvp=kvp, tok0=tok0: h.dma_start(
                out=kTs[kvp][:, :], in_=io["kT_d"][:, tok0:tok0 + S]),
                waits=kv_waits, inc=s_kv, inck=16)
            pr.op("sync", lambda h, kvp=kvp, tok0=tok0: h.dma_start(
                out=vs[kvp][:, :, :],
                in_=io["v_d"][tok0:tok0 + S, :].rearrange("(c p) d -> p c d", p=128)),
                inc=s_kv, inck=16)
        else:
            hl = qheads[0]
            pr.op("sync", lambda h, kvp=kvp, tok0=tok0, hl=hl: h.dma_start(
                out=kTs[kvp][:, :], in_=io["lkT_d"][hl, :, tok0:tok0 + S]),
                waits=kv_waits, inc=s_kv, inck=16)
            pr.op("sync", lambda h, kvp=kvp, tok0=tok0, hl=hl: h.dma_start(
                out=vs[kvp][:, :, :],
                in_=io["lv_d"][hl, tok0:tok0 + S, :].rearrange("(c p) d -> p c d", p=128)),
                inc=s_kv, inck=16)
        KV_RDY = s_kv.n

        for qh in qheads:
            qp = qrun % 2
            qgate = [(s_qk, qk_after_qrun[qrun - 2])] if qrun >= 2 else list(after)
            pr.op("sync", lambda h, qp=qp, qh=qh, tok0=tok0: h.dma_start(
                out=qTs[qp][:, :], in_=io["qT_d"][qh, :, tok0:tok0 + S]),
                waits=qgate, inc=s_q, inck=16)
            Q_RDY = s_q.n
            pr.op("sync", lambda h, qp=qp, qh=qh, tok0=tok0: h.dma_start(
                out=mnst[qp][0:1, :], in_=io["norm_d"][qh:qh + 1, tok0:tok0 + S]),
                waits=[(s_srw, s_srw.n), (s_kmx, KMX_RDY)], inc=s_n, inck=16)
            N_RDY = s_n.n
            pr.op("vector", lambda h, qp=qp, r=r: h.tensor_scalar(
                out=mrow[qp][0:1, :], in0=mnst[qp][0:1, :],
                scalar1=kmx[0:1, r:r + 1], scalar2=-1.0,
                op0=ALU.mult, op1=ALU.mult),
                waits=[(s_kmx, KMX_RDY), (s_n, N_RDY)],
                inc=s_srw)
            MR = s_srw.n
            mnst_rd[qp] = MR

            steps = [(s, g2) for s in range(8) for g2 in range(s + 1)]
            prev = None

            def emit_pv(st):
                s, g2 = st
                sp = (strip_g0 + s) % 2
                NC = 4 * (s + 1)
                diag = (g2 == s)
                gp = g2 % 2
                src = pbuf2 if diag else pbuf[gp]
                def pvf(h, g2=g2, src=src, kvp=kvp, NC=NC, sp=sp):
                    for j in range(4):
                        kc = g2 * 4 + j
                        h.matmul(po[sp][:, :], vs[kvp][:, kc, :],
                                 src[:, j * 512:(j + 1) * 512],
                                 start=(kc == 0), stop=(kc == NC - 1))
                        ins = h.matmul(psm[0:1, :], ones128[:, 0:1],
                                       src[:, j * 512:(j + 1) * 512],
                                       start=(kc == 0), stop=(kc == NC - 1))
                    return ins
                pv_waits = [(s_exp, exp_of[st])]
                if diag:
                    pv_waits.append((s_msk, msk_of[st]))
                if g2 == 0:
                    pv_waits += [(s_oev, max(0, strip_g0 + s - 1)), (s_srw, s_srw.n)]
                pr.op("tensor", pvf, waits=pv_waits, inc=s_pv, inck=4)
                if diag:
                    nonlocal last_pv_diag
                    last_pv_diag = s_pv.n
                else:
                    last_pv_par[gp] = s_pv.n
                if diag:
                    emit_epilogue(s)

            def emit_epilogue(s):
                msb_used[(strip_g0 + s) % 2] = s_exp.n
                sp = (strip_g0 + s) % 2
                sg = strip_g0 + s
                q0 = s * 512
                pr.op("vector", lambda h: h.reciprocal(rrow[0:1, :], psm[0:1, :]),
                      waits=[(s_pv, s_pv.n)], inc=s_srw)
                pr.op("tensor", lambda h: h.matmul(
                    pmb[:, :], ones1f[0:1, :], rrow[0:1, :], start=True, stop=True),
                    waits=[(s_srw, s_srw.n), (s_mbe, s_mbe.n)], inc=s_mb)
                pr.op("vector", lambda h, sp=sp: h.tensor_copy(rb[sp][:, :], pmb[:, :]),
                      waits=[(s_mb, s_mb.n), (s_oev, max(0, sg - 1))], inc=s_mbe)
                pr.op("vector", lambda h, sp=sp: h.tensor_tensor(
                    out=outs[sp][:, :], in0=po[sp][:, :], in1=rb[sp][:, :], op=ALU.mult),
                    waits=[(s_mbe, s_mbe.n), (s_out, 16 * max(0, sg - 1))],
                    inc=s_oev)
                row0 = outbase + qh * 128
                ct = io["contrib0"] if b == 0 else io["contrib1"]
                pr.op("sync", lambda h, sp=sp, row0=row0, q0=q0, ct=ct: h.dma_start(
                    out=ct[row0:row0 + 128, q0:q0 + 512],
                    in_=outs[sp][:, :]),
                    waits=[(s_oev, s_oev.n)], inc=s_out, inck=16)

            strip_g0 = strip_g
            exp_of, msk_of = {}, {}
            msb_rdy = {}

            def emit_msb_prep(s8):
                sp8 = (strip_g0 + s8) % 2
                pr.op("vector", lambda h, sp8=sp8, s8=s8, qp=qp: h.tensor_reduce(
                    msn[sp8][0:1, 0:1], mrow[qp][0:1, s8 * 512:(s8 + 1) * 512],
                    axis=AX.X, op=ALU.min),
                    waits=[(s_srw, MR), (s_exp, msb_used[sp8])], inc=s_srw)
                pr.op("tensor", lambda h, sp8=sp8: h.matmul(
                    pmb[:, 0:1], ones1b[0:1, :], msn[sp8][0:1, 0:1],
                    start=True, stop=True),
                    waits=[(s_srw, s_srw.n), (s_mbe, s_mbe.n)], inc=s_mb)
                pr.op("vector", lambda h, sp8=sp8: h.tensor_copy(
                    msb[sp8][:, 0:1], pmb[:, 0:1]),
                    waits=[(s_mb, s_mb.n)], inc=s_mbe)
                msb_rdy[s8] = s_mbe.n
            for st in steps:
                s, g2 = st
                q0 = s * 512
                gp = g2 % 2
                diag = (g2 == s)
                sp8 = (strip_g0 + s) % 2
                if g2 == 0:
                    emit_msb_prep(s)
                for half in range(2):
                    def qkf(h, g2=g2, half=half, kvp=kvp, qp=qp, q0=q0):
                        for j in range(2):
                            kc = g2 * 4 + half * 2 + j
                            ins = h.matmul(
                                ps[half][:, j * 512:(j + 1) * 512],
                                kTs[kvp][:, kc * 128:(kc + 1) * 128],
                                qTs[qp][:, q0:q0 + 512],
                                start=True, stop=True)
                        return ins
                    pr.op("tensor", qkf,
                          waits=[(s_kv, KV_RDY), (s_q, Q_RDY),
                                 (s_exp, last_exp_half[half])],
                          inc=s_qk)
                    pr.op("scalar", lambda h, gp=gp, half=half, sp8=sp8: h.activation(
                        bass.AP(pbuf[gp], half * 1024, [[2048, 128], [1, 1024]]),
                        ps[half][:, :], AF.Exp, bias=msb[sp8][:, 0:1]),
                        waits=[(s_qk, s_qk.n), (s_pv, last_pv_par[gp]),
                               (s_mbe, msb_rdy[s])],
                        inc=s_exp)
                    last_exp_half[half] = s_exp.n
                exp_of[st] = s_exp.n
                if diag:
                    for j in range(4):
                        kc = g2 * 4 + j
                        pr.op("gpsimd", lambda h, gp=gp, j=j, kc=kc, q0=q0:
                              h.affine_select(
                                  out=pbuf2[:, j * 512:(j + 1) * 512],
                                  in_=pbuf[gp][:, j * 512:(j + 1) * 512],
                                  pattern=[[1, 512]], compare_op=ALU.is_ge,
                                  fill=0.0, base=q0 - kc * 128,
                                  channel_multiplier=-1),
                              waits=[(s_exp, s_exp.n), (s_pv, last_pv_diag)],
                              inc=s_msk)
                    msk_of[st] = s_msk.n
                if prev is not None:
                    emit_pv(prev)
                prev = st
            emit_pv(prev)
            strip_g += 8
            qk_after_qrun.append(s_qk.n)
            mrow_used = s_qk.n
            qrun += 1
        qk_after_run.append(s_qk.n)
        pv_after_run.append(s_pv.n)
        if r == 4:
            sm["a_out_b0"] = s_out.n
            if post_b0 is not None:
                post_b0()
    return {}


def build_pa_debug():
    nc = bass.Bass("TRN2", target_bir_lowering=False)
    io = declare_p_inputs(nc)
    io.update(declare_p_scratch(nc, external=True))
    io.update(declare_a_out(nc, external=True))
    with contextlib.ExitStack() as ctx:
        sm = make_p_sems(ctx, nc)
        sm.update(make_a_sems(ctx, nc))
        pr = Prog(nc)
        with contextlib.ExitStack() as pctx:
            emit_phase_p(pr, pctx, io, sm)
        gate = [(sm["p_out"], sm["p_out"].n)]
        emit_phase_a(pr, ctx, io, sm, after=gate)
        pr.op("sync", None, waits=[(sm["a_out"], sm["a_out"].n)])
        block = ctx.enter_context(nc.Block())
        pr.replay(block)
    return nc


def declare_c_inputs(nc, gathered_external):
    io = {}
    kwg = {"kind": "ExternalInput"} if gathered_external else {"addr_space": "Shared"}
    io["gath0"] = nc.dram_tensor("gath0", [8192, S], BF16, **kwg)
    io["gath1"] = nc.dram_tensor("gath1", [8192, S], BF16, **kwg)
    io["wcat"] = nc.dram_tensor("wcat", [8192, 512], F32, kind="ExternalInput")
    io["x_sl"] = nc.dram_tensor("x_sl", [T, 512], F32, kind="ExternalInput")
    io["WbT_c"] = nc.dram_tensor("WbT_c", [D, 512], F32, kind="ExternalInput")
    io["AG_cat"] = nc.dram_tensor("AG_cat", [D, 72], F32, kind="ExternalInput")
    io["B_c"] = nc.dram_tensor("B_c", [64, 512], F32, kind="ExternalInput")
    io["y_sl"] = nc.dram_tensor("y_sl", [T, 512], F32, kind="ExternalOutput")
    io["o1_d"] = nc.dram_tensor("o1_d", [T, 512], BF16)
    io["x1_d"] = nc.dram_tensor("x1_d", [T, 512], BF16)
    io["ffn_d"] = nc.dram_tensor("ffn_d", [T, 512], BF16)
    for i in range(3):
        io[f"sti{i}"] = nc.dram_tensor(f"sti{i}", [2, 128, NT], F32)
        io[f"sto{i}"] = nc.dram_tensor(f"sto{i}", [2, 128, NT], F32)
    io["xfc_d"] = nc.dram_tensor("xfc_d", [512, T], BF16)
    io["xfg_d"] = nc.dram_tensor("xfg_d", [D, T], BF16)
    return io


def make_c_sems(stack, nc):
    names = ["c_w", "c_g", "c_mm1", "c_st", "c_sts", "c_ev", "c_dma", "c_cc",
             "c_fin", "c_x", "c_tp", "c_tpe", "c_mm3", "c_hg", "c_t2", "c_out",
             "c_fins"]
    return {n: Sem(stack, nc, n) for n in names}


def alloc_c_sbuf(nc, ctx):
    E = ctx.enter_context
    bufs = {}
    bufs["stats"] = E(nc.sbuf_tensor("c_stats", [128, 6, NT], F32))
    bufs["fin"] = E(nc.sbuf_tensor("c_fin", [128, 6, NT], F32))
    bufs["ftmp"] = E(nc.sbuf_tensor("c_ftmp", [128, 2, NT], F32))
    bufs["scr5"] = E(nc.sbuf_tensor("c_scr5", [128, 512], BF16))
    bufs["epsv"] = E(nc.sbuf_tensor("c_epsv", [128, 1], F32))
    bufs["ones_bf"] = E(nc.sbuf_tensor("c_onesb", [128, 128], BF16))
    bufs["ident_bf"] = E(nc.sbuf_tensor("c_identb", [128, 128], BF16))
    bufs["hg"] = E(nc.sbuf_tensor("c_hg", [128, 72], F32))
    bufs["t2b"] = E(nc.sbuf_tensor("c_t2b", [128, 40], F32))
    bufs["wd"] = E(nc.sbuf_tensor("c_wd", [128, 8], F32))
    bufs["hsc"] = E(nc.sbuf_tensor("c_hsc", [128, 64], BF16))
    bufs["hscT"] = E(nc.sbuf_tensor("c_hscT", [64, 128], BF16))
    bufs["xt"] = [E(nc.sbuf_tensor(f"c_xt{i}", [128, 512], F32)) for i in range(2)]
    bufs["aux"] = [E(nc.sbuf_tensor(f"c_aux{i}", [128, 512], BF16)) for i in range(2)]
    bufs["ev2"] = [E(nc.sbuf_tensor(f"c_ev2{i}", [128, 512], BF16)) for i in range(2)]
    _t1 = E(nc.sbuf_tensor("c_tmp1", [128, 512], F32))
    bufs["tmp1"] = [_t1, _t1]
    bufs["outt"] = bufs["xt"]
    bufs["xfst"] = [E(nc.sbuf_tensor(f"c_xfst{i}", [128, 4, 128], BF16)) for i in range(2)]
    bufs["wcat_sb"] = E(nc.sbuf_tensor("c_wcat", [128, 64, 512], BF16))
    return bufs


def emit_phase_c(pr, ctx, io, sm, after=(), upto=None, bufs=None,
                 after_w=None, cc_gate0=1, cc_gate1=1):
    nc = pr.nc
    E = ctx.enter_context

    s_w = sm["c_w"]; s_g = sm["c_g"]; s_mm1 = sm["c_mm1"]
    s_st = sm["c_st"]; s_sts = sm["c_sts"]; s_ev = sm["c_ev"]
    s_dma = sm["c_dma"]; s_cc = sm["c_cc"]; s_fin = sm["c_fin"]
    s_x = sm["c_x"]; s_tp = sm["c_tp"]; s_tpe = sm["c_tpe"]
    s_mm3 = sm["c_mm3"]; s_hg = sm["c_hg"]; s_t2 = sm["c_t2"]
    s_out = sm["c_out"]; s_fns = sm["c_fins"]

    if bufs is None:
        bufs = alloc_c_sbuf(nc, ctx)
    if after_w is None:
        after_w = list(after)
    stats = bufs["stats"]; fin = bufs["fin"]; ftmp = bufs["ftmp"]
    scr5 = bufs["scr5"]; epsv = bufs["epsv"]; ones_bf = bufs["ones_bf"]
    ident_bf = bufs["ident_bf"]; hg = bufs["hg"]; t2b = bufs["t2b"]
    wd = bufs["wd"]; hsc = bufs["hsc"]; hscT = bufs["hscT"]
    xt = bufs["xt"]; aux = bufs["aux"]; ev2 = bufs["ev2"]
    tmp1 = bufs["tmp1"]; outt = bufs["outt"]; xfst = bufs["xfst"]

    po1 = [E(nc.psum_tensor(f"c_po1{i}", [128, 512], F32)) for i in range(2)]
    pf = [E(nc.psum_tensor(f"c_pf{i}", [128, 512], F32)) for i in range(2)]
    phg = [E(nc.psum_tensor(f"c_phg{i}", [128, 512], F32)) for i in range(2)]
    ptb = [E(nc.psum_tensor(f"c_ptb{i}", [128, 1024], BF16)) for i in range(2)]

    pr.op("gpsimd", lambda h: h.memset(epsv[:, :], EPS), waits=list(after_w))
    pr.op("gpsimd", lambda h: h.memset(ones_bf[:, :], 1.0))
    pr.op("gpsimd", lambda h: h.affine_select(
        out=ident_bf[:, :], in_=ones_bf[:, :], pattern=[[1, 128]],
        compare_op=ALU.is_equal, fill=0.0, base=0, channel_multiplier=-1),
        inc=s_w)
    ID_RDY = s_w.n

    def stats_ar(idx):
        pr.op("sync", lambda h, idx=idx: h.dma_start(
            out=io[f"sti{idx}"].ap().rearrange("s p t -> p s t"),
            in_=stats[:, 2 * idx:2 * idx + 2, :]),
            waits=[(s_dma, s_dma.n)], inc=s_dma, inck=16)
        pr.op("gpsimd", lambda h, idx=idx: h.collective_compute(
            "AllReduce", ALU.add, replica_groups=[list(range(8))],
            ins=[io[f"sti{idx}"].ap().opt()], outs=[io[f"sto{idx}"].ap().opt()]),
            waits=[(s_dma, s_dma.n)], inc=s_cc)
        pr.op("sync", lambda h, idx=idx: h.dma_start(
            out=stats[:, 2 * idx:2 * idx + 2, :],
            in_=io[f"sto{idx}"].ap().rearrange("s p t -> p s t")),
            waits=[(s_cc, s_cc.n)], inc=s_dma, inck=16)
        AR = s_dma.n
        pr.op("vector", lambda h, idx=idx: h.tensor_scalar(
            out=fin[:, 2 * idx, :], in0=stats[:, 2 * idx, :],
            scalar1=1.0 / D, scalar2=None, op0=ALU.mult, op1=ALU.bypass),
            waits=[(s_dma, AR)], inc=s_fin)
        pr.op("vector", lambda h, idx=idx: h.tensor_tensor(
            out=ftmp[:, 0, :], in0=fin[:, 2 * idx, :], in1=fin[:, 2 * idx, :],
            op=ALU.mult), waits=[(s_fin, s_fin.n)], inc=s_fin)
        pr.op("vector", lambda h, idx=idx: h.tensor_scalar(
            out=ftmp[:, 1, :], in0=stats[:, 2 * idx + 1, :],
            scalar1=1.0 / D, scalar2=None, op0=ALU.mult, op1=ALU.bypass),
            waits=[(s_dma, AR)], inc=s_fin)
        pr.op("vector", lambda h: h.tensor_tensor(
            out=ftmp[:, 1, :], in0=ftmp[:, 1, :], in1=ftmp[:, 0, :],
            op=ALU.subtract), waits=[(s_fin, s_fin.n)], inc=s_fin)
        pr.op("scalar", lambda h: h.activation(
            ftmp[:, 0, :], ftmp[:, 1, :], AF.Ln, bias=epsv[:, 0:1]),
            waits=[(s_fin, s_fin.n)], inc=s_fns)
        pr.op("scalar", lambda h, idx=idx: h.activation(
            fin[:, 2 * idx + 1, :], ftmp[:, 0, :], AF.Exp, scale=-0.5),
            waits=[(s_fns, s_fns.n)], inc=s_fns)

    po1_rd = [0, 0]
    evdma = [0, 0]
    with contextlib.ExitStack() as c1ctx:
        wcat_sb = bufs["wcat_sb"]
        gbuf = [c1ctx.enter_context(nc.sbuf_tensor(f"c_gb{i}", [128, 64, 256], BF16))
                for i in range(2)]
        pr.op("gpsimd", lambda h: h.dma_start(
            out=wcat_sb[:, :, :],
            in_=io["wcat"].ap().rearrange("(c p) f -> p c f", p=128)),
            waits=list(after_w), inc=s_w, inck=16)
        W1_RDY = s_w.n
        for tg in range(NT // 2):
            gp = tg % 2
            gsrc = io["gath0"] if tg < 16 else io["gath1"]
            gcol = (tg % 16) * 256
            pr.op("sync", lambda h, gp=gp, gsrc=gsrc, gcol=gcol: h.dma_start(
                out=gbuf[gp][:, :, :],
                in_=gsrc[:, gcol:gcol + 256]
                    .rearrange("(c p) t -> p c t", p=128)),
                waits=[(s_mm1, max(0, 2 * tg - 2)),
                       (s_cc, cc_gate0 if tg < 16 else cc_gate1)]
                      + (list(after) if tg < 2 else []),
                inc=s_g, inck=16)
            G_RDY = s_g.n
            for ti in range(2):
                g = tg * 2 + ti
                par = g % 2
                def mm1(h, gp=gp, ti=ti, par=par):
                    for fc in range(64):
                        ins = h.matmul(po1[par][:, :],
                                       gbuf[gp][:, fc, ti * 128:(ti + 1) * 128],
                                       wcat_sb[:, fc, :],
                                       start=(fc == 0), stop=(fc == 63))
                    return ins
                pr.op("tensor", mm1,
                      waits=[(s_g, G_RDY), (s_w, W1_RDY), (s_ev, max(0, g - 1)),
                             (s_st, po1_rd[par])],
                      inc=s_mm1)
                MM = s_mm1.n
                if upto != "c1m":
                    pr.op("vector", lambda h, par=par, g=g: h.reduce_sum(
                        stats[:, 0, g:g + 1], po1[par][:, :], axis=AX.X),
                        waits=[(s_mm1, MM)], inc=s_st)
                    po1_rd[par] = s_st.n
                if upto not in ("c1m", "c1s"):
                    pr.op("scalar", lambda h, par=par, g=g: h.activation(
                        scr5[:, :], po1[par][:, :], AF.Square,
                        accum_out=stats[:, 1, g:g + 1]),
                        waits=[(s_mm1, MM), (s_st, po1_rd[par])], inc=s_sts)
                pr.op("scalar", lambda h, par=par: h.copy(
                    ev2[par][:, :], po1[par][:, :]),
                    waits=[(s_dma, evdma[par]), (s_mm1, MM)], inc=s_ev)
                pr.op("sync", lambda h, g=g, par=par: h.dma_start(
                    out=io["o1_d"][g * 128:(g + 1) * 128, :], in_=ev2[par][:, :]),
                    waits=[(s_ev, s_ev.n)], inc=s_dma, inck=16)
                evdma[par] = s_dma.n
    C1_MM = s_mm1.n
    if upto in ("c1", "c1m", "c1s"):
        pr.op("sync", None, waits=[(s_sts, s_sts.n), (s_st, s_st.n), (s_dma, s_dma.n)])
        return {}
    pr.op("sync", None, waits=[(s_sts, s_sts.n), (s_st, s_st.n), (s_dma, s_dma.n)],
          inc=s_dma, inck=1)
    stats_ar(0)
    FIN2 = s_fns.n
    if upto == "ar1":
        pr.op("sync", None, waits=[(s_fns, FIN2)])
        return {}

    x1_done = [0, 0]
    x1_st = [0, 0]
    for g in range(NT):
        par = g % 2
        pr.op("sync", lambda h, g=g, par=par: h.dma_start(
            out=xt[par][:, :], in_=io["x_sl"][g * 128:(g + 1) * 128, :]),
            waits=[(s_st, x1_st[par]), (s_dma, x1_done[par])], inc=s_x, inck=16)
        pr.op("sync", lambda h, g=g, par=par: h.dma_start(
            out=aux[par][:, :], in_=io["o1_d"][g * 128:(g + 1) * 128, :]),
            inc=s_x, inck=16)
        X_RDY = s_x.n
        pr.op("vector", lambda h, g=g, par=par: h.tensor_scalar(
            out=tmp1[par][:, :], in0=aux[par][:, :],
            scalar1=fin[:, 0, g:g + 1], scalar2=fin[:, 1, g:g + 1],
            op0=ALU.subtract, op1=ALU.mult),
            waits=[(s_x, X_RDY), (s_fns, FIN2)], inc=s_st)
        pr.op("vector", lambda h, par=par: h.tensor_tensor(
            out=aux[par][:, :], in0=tmp1[par][:, :], in1=xt[par][:, :], op=ALU.add),
            waits=[(s_st, s_st.n)], inc=s_st)
        X1_RDY = s_st.n
        pr.op("vector", lambda h, par=par, g=g: h.reduce_sum(
            stats[:, 2, g:g + 1], aux[par][:, :], axis=AX.X),
            waits=[(s_st, X1_RDY)], inc=s_st)
        x1_st[par] = s_st.n
        pr.op("scalar", lambda h, par=par, g=g: h.activation(
            scr5[:, :], aux[par][:, :], AF.Square, accum_out=stats[:, 3, g:g + 1]),
            waits=[(s_st, X1_RDY)], inc=s_sts)
        pr.op("sync", lambda h, g=g, par=par: h.dma_start(
            out=io["x1_d"][g * 128:(g + 1) * 128, :], in_=aux[par][:, :]),
            waits=[(s_sts, s_sts.n)], inc=s_dma, inck=16)
        x1_done[par] = s_dma.n
    if upto == "c2a":
        pr.op("sync", None, waits=[(s_sts, s_sts.n), (s_st, s_st.n), (s_dma, s_dma.n)])
        return {}
    pr.op("sync", None, waits=[(s_sts, s_sts.n), (s_st, s_st.n), (s_dma, s_dma.n)],
          inc=s_dma, inck=1)
    stats_ar(1)
    FIN3 = s_fns.n

    xf_dma = [0, 0]
    for g in range(NT):
        par = g % 2
        pr.op("sync", lambda h, g=g, par=par: h.dma_start(
            out=aux[par][:, :], in_=io["x1_d"][g * 128:(g + 1) * 128, :]),
            waits=[(s_st, x1_st[par]), (s_fns, FIN3)], inc=s_x, inck=16)
        X_RDY = s_x.n
        pr.op("vector", lambda h, g=g, par=par: h.tensor_scalar(
            out=scr5[:, :], in0=aux[par][:, :],
            scalar1=fin[:, 2, g:g + 1], scalar2=fin[:, 3, g:g + 1],
            op0=ALU.subtract, op1=ALU.mult),
            waits=[(s_x, X_RDY), (s_fns, FIN3), (s_tp, max(0, s_tp.n))], inc=s_st)
        x1_st[par] = s_st.n
        XF_RDY = s_st.n
        def tpf(h, g=g):
            for j in range(4):
                ins = h.transpose(ptb[g % 2][:, j * 128:(j + 1) * 128],
                                  scr5[:, j * 128:(j + 1) * 128], ident_bf[:, :])
            return ins
        pr.op("tensor", tpf, waits=[(s_st, XF_RDY), (s_w, ID_RDY),
                                    (s_tpe, max(0, g - 1))], inc=s_tp)
        pr.op("vector", lambda h, g=g: h.tensor_copy(
            xfst[g % 2][:, :, :].rearrange("p c t -> p (c t)"),
            ptb[g % 2][:, 0:512]),
            waits=[(s_tp, s_tp.n), (s_dma, xf_dma[g % 2])], inc=s_tpe)
        pr.op("sync", lambda h, g=g: h.dma_start(
            out=io["xfc_d"][:, g * 128:(g + 1) * 128]
                .rearrange("(c p) t -> p c t", p=128),
            in_=xfst[g % 2][:, :, :]),
            waits=[(s_tpe, s_tpe.n)], inc=s_dma, inck=16)
        xf_dma[g % 2] = s_dma.n
    if upto == "c2b":
        pr.op("sync", None, waits=[(s_dma, s_dma.n)])
        return {}
    pr.op("gpsimd", lambda h: h.collective_compute(
        "AllGather", ALU.bypass, replica_groups=[list(range(8))],
        ins=[io["xfc_d"].ap().opt()], outs=[io["xfg_d"].ap().opt()]),
        waits=[(s_dma, s_dma.n)], inc=s_cc)
    AG2 = s_cc.n

    pf_rd = [0, 0]
    ev3 = [0, 0]
    with contextlib.ExitStack() as c3ctx:
        wb_sb = c3ctx.enter_context(nc.sbuf_tensor("c_wb", [128, ND, 512], BF16))
        ag_sb = c3ctx.enter_context(nc.sbuf_tensor("c_ag", [128, ND, 72], BF16))
        b_sb = c3ctx.enter_context(nc.sbuf_tensor("c_b", [64, 512], BF16))
        xfb = [c3ctx.enter_context(nc.sbuf_tensor(f"c_xfb{i}", [128, ND, 256], BF16))
               for i in range(2)]
        pr.op("gpsimd", lambda h: h.dma_start(
            out=wb_sb[:, :, :],
            in_=io["WbT_c"].ap().rearrange("(c p) f -> p c f", p=128)),
            waits=[(s_mm1, C1_MM)], inc=s_w, inck=16)
        pr.op("gpsimd", lambda h: h.dma_start(
            out=ag_sb[:, :, :],
            in_=io["AG_cat"].ap().rearrange("(c p) f -> p c f", p=128)),
            inc=s_w, inck=16)
        pr.op("gpsimd", lambda h: h.dma_start(
            out=b_sb[0:64, :], in_=io["B_c"].ap()), inc=s_w, inck=16)
        W3_RDY = s_w.n
        for tg in range(NT // 2):
            gp = tg % 2
            pr.op("sync", lambda h, tg=tg, gp=gp: h.dma_start(
                out=xfb[gp][:, :, :],
                in_=io["xfg_d"][:, tg * 256:(tg + 1) * 256]
                    .rearrange("(c p) t -> p c t", p=128)),
                waits=[(s_cc, AG2), (s_mm3, max(0, 2 * (2 * tg - 2)))],
                inc=s_g, inck=16)
            G_RDY = s_g.n
            for ti in range(2):
                g = tg * 2 + ti
                par = g % 2
                def mmf(h, gp=gp, ti=ti, par=par):
                    for dc in range(ND):
                        h.matmul(pf[par][:, :],
                                 xfb[gp][:, dc, ti * 128:(ti + 1) * 128],
                                 wb_sb[:, dc, :],
                                 start=(dc == 0), stop=False)
                        ins = h.matmul(phg[par][:, 0:72],
                                       xfb[gp][:, dc, ti * 128:(ti + 1) * 128],
                                       ag_sb[:, dc, :],
                                       start=(dc == 0), stop=(dc == ND - 1))
                    return ins
                pr.op("tensor", mmf,
                      waits=[(s_g, G_RDY), (s_w, W3_RDY), (s_hg, max(0, g - 1)),
                             (s_ev, ev3[par]), (s_st, pf_rd[par])],
                      inc=s_mm3)
                MM = s_mm3.n
                pr.op("vector", lambda h, par=par: h.tensor_copy(
                    hg[:, :], phg[par][:, 0:72]), waits=[(s_mm3, MM)], inc=s_hg)
                HG = s_hg.n
                pr.op("vector", lambda h: h.reduce_max(
                    t2b[:, 0:1], hg[:, 64:72], axis=AX.X),
                    waits=[(s_hg, HG), (s_t2, s_t2.n)], inc=s_t2)
                pr.op("vector", lambda h: h.tensor_scalar(
                    out=t2b[:, 1:2], in0=t2b[:, 0:1], scalar1=-1.0, scalar2=None,
                    op0=ALU.mult, op1=ALU.bypass),
                    waits=[(s_t2, s_t2.n)], inc=s_t2)
                pr.op("scalar", lambda h: h.activation(
                    t2b[:, 8:16], hg[:, 64:72], AF.Exp, bias=t2b[:, 1:2]),
                    waits=[(s_t2, s_t2.n)], inc=s_t2)
                pr.op("vector", lambda h: h.reduce_max(
                    t2b[:, 2:3], t2b[:, 8:16], axis=AX.X),
                    waits=[(s_t2, s_t2.n)], inc=s_t2)
                pr.op("vector", lambda h: h.tensor_scalar(
                    out=t2b[:, 16:24], in0=t2b[:, 8:16], scalar1=t2b[:, 2:3],
                    scalar2=None, op0=ALU.is_ge, op1=ALU.bypass),
                    waits=[(s_t2, s_t2.n)], inc=s_t2)
                pr.op("vector", lambda h: h.tensor_scalar(
                    out=t2b[:, 24:32], in0=t2b[:, 16:24], scalar1=2.0, scalar2=None,
                    op0=ALU.mult, op1=ALU.bypass),
                    waits=[(s_t2, s_t2.n)], inc=s_t2)
                pr.op("vector", lambda h: h.tensor_tensor(
                    out=t2b[:, 24:32], in0=t2b[:, 8:16], in1=t2b[:, 24:32],
                    op=ALU.subtract), waits=[(s_t2, s_t2.n)], inc=s_t2)
                pr.op("vector", lambda h: h.reduce_max(
                    t2b[:, 3:4], t2b[:, 24:32], axis=AX.X),
                    waits=[(s_t2, s_t2.n)], inc=s_t2)
                pr.op("vector", lambda h: h.tensor_scalar(
                    out=t2b[:, 32:40], in0=t2b[:, 24:32], scalar1=t2b[:, 3:4],
                    scalar2=None, op0=ALU.is_ge, op1=ALU.bypass),
                    waits=[(s_t2, s_t2.n)], inc=s_t2)
                pr.op("vector", lambda h: h.tensor_tensor(
                    out=t2b[:, 16:24], in0=t2b[:, 16:24], in1=t2b[:, 32:40],
                    op=ALU.add), waits=[(s_t2, s_t2.n)], inc=s_t2)
                pr.op("vector", lambda h: h.tensor_tensor(
                    out=t2b[:, 4:5], in0=t2b[:, 2:3], in1=t2b[:, 3:4],
                    op=ALU.add), waits=[(s_t2, s_t2.n)], inc=s_t2)
                pr.op("vector", lambda h: h.reciprocal(
                    t2b[:, 5:6], t2b[:, 4:5]), waits=[(s_t2, s_t2.n)], inc=s_t2)
                pr.op("vector", lambda h: h.tensor_tensor(
                    out=t2b[:, 8:16], in0=t2b[:, 8:16], in1=t2b[:, 16:24],
                    op=ALU.mult), waits=[(s_t2, s_t2.n)], inc=s_t2)
                pr.op("vector", lambda h: h.tensor_scalar(
                    out=wd[:, 0:8], in0=t2b[:, 8:16], scalar1=t2b[:, 5:6],
                    scalar2=2.0, op0=ALU.mult, op1=ALU.mult),
                    waits=[(s_t2, s_t2.n)], inc=s_t2)
                for e in range(8):
                    pr.op("vector", lambda h, e=e: h.tensor_scalar(
                        out=hsc[:, e * 8:(e + 1) * 8], in0=hg[:, e * 8:(e + 1) * 8],
                        scalar1=wd[:, e:e + 1], scalar2=None,
                        op0=ALU.mult, op1=ALU.bypass),
                        waits=[(s_t2, s_t2.n)], inc=s_t2)
                pr.op("tensor", lambda h, g=g: h.transpose(
                    ptb[g % 2][0:64, 0:128], hsc[:, :], ident_bf[:, :]),
                    waits=[(s_t2, s_t2.n), (s_tpe, s_tpe.n), (s_w, ID_RDY)],
                    inc=s_tp)
                pr.op("vector", lambda h, g=g: h.tensor_copy(
                    hscT[0:64, :], ptb[g % 2][0:64, 0:128]),
                    waits=[(s_tp, s_tp.n)], inc=s_tpe)
                pr.op("tensor", lambda h, par=par: h.matmul(
                    pf[par][:, :], hscT[0:64, :], b_sb[0:64, :],
                    start=False, stop=True),
                    waits=[(s_tpe, s_tpe.n)], inc=s_mm3)
                MM2 = s_mm3.n
                pr.op("vector", lambda h, par=par, g=g: h.reduce_sum(
                    stats[:, 4, g:g + 1], pf[par][:, :], axis=AX.X),
                    waits=[(s_mm3, MM2)], inc=s_st)
                pf_rd[par] = s_st.n
                pr.op("scalar", lambda h, par=par, g=g: h.activation(
                    scr5[:, :], pf[par][:, :], AF.Square,
                    accum_out=stats[:, 5, g:g + 1]),
                    waits=[(s_mm3, MM2), (s_st, pf_rd[par])], inc=s_sts)
                pr.op("scalar", lambda h, par=par: h.copy(
                    ev2[par][:, :], pf[par][:, :]),
                    waits=[(s_dma, evdma[par])], inc=s_ev)
                ev3[par] = s_ev.n
                pr.op("sync", lambda h, g=g, par=par: h.dma_start(
                    out=io["ffn_d"][g * 128:(g + 1) * 128, :], in_=ev2[par][:, :]),
                    waits=[(s_ev, s_ev.n)], inc=s_dma, inck=16)
                evdma[par] = s_dma.n
    if upto == "c3":
        pr.op("sync", None, waits=[(s_sts, s_sts.n), (s_st, s_st.n), (s_dma, s_dma.n)])
        return {}
    pr.op("sync", None, waits=[(s_sts, s_sts.n), (s_st, s_st.n), (s_dma, s_dma.n)],
          inc=s_dma, inck=1)
    stats_ar(2)
    FIN4 = s_fns.n

    c4_st = [0, 0]
    for g in range(NT):
        par = g % 2
        pr.op("sync", lambda h, g=g, par=par: h.dma_start(
            out=ev2[par][:, :], in_=io["x1_d"][g * 128:(g + 1) * 128, :]),
            waits=[(s_out, 16 * max(0, g - 1)), (s_st, c4_st[par]), (s_fns, FIN4),
                   (s_dma, evdma[par])],
            inc=s_x, inck=16)
        pr.op("sync", lambda h, g=g, par=par: h.dma_start(
            out=aux[par][:, :], in_=io["ffn_d"][g * 128:(g + 1) * 128, :]),
            inc=s_x, inck=16)
        X_RDY = s_x.n
        pr.op("vector", lambda h, g=g, par=par: h.tensor_scalar(
            out=tmp1[par][:, :], in0=aux[par][:, :],
            scalar1=fin[:, 4, g:g + 1], scalar2=fin[:, 5, g:g + 1],
            op0=ALU.subtract, op1=ALU.mult),
            waits=[(s_x, X_RDY), (s_fns, FIN4)], inc=s_st)
        pr.op("vector", lambda h, g=g, par=par: h.tensor_tensor(
            out=outt[par][:, :], in0=tmp1[par][:, :], in1=ev2[par][:, :], op=ALU.add),
            waits=[(s_st, s_st.n)], inc=s_st)
        c4_st[par] = s_st.n
        pr.op("sync", lambda h, g=g, par=par: h.dma_start(
            out=io["y_sl"][g * 128:(g + 1) * 128, :], in_=outt[par][:, :]),
            waits=[(s_st, s_st.n)], inc=s_out, inck=16)
    pr.op("sync", None, waits=[(s_out, s_out.n)])
    return {}


def build_c_debug(upto=None):
    nc = bass.Bass("TRN2", target_bir_lowering=False)
    io = declare_c_inputs(nc, gathered_external=True)
    with contextlib.ExitStack() as ctx:
        sm = make_c_sems(ctx, nc)
        pr = Prog(nc)
        emit_phase_c(pr, ctx, io, sm, upto=upto, cc_gate0=0, cc_gate1=0)
        block = ctx.enter_context(nc.Block())
        pr.replay(block)
    return nc


def host_phase_c_inputs(c, inp, gathered=None):
    Fc = slice(512 * c, 512 * c + 512)
    w_o = np.asarray(inp["L0_attn_w_o"], dtype=np.float32)
    w_lo = np.asarray(inp["L0_attn_w_lo"], dtype=np.float32)
    rows = []
    for r in range(8):
        fr = slice(4 * r * 128, (4 * r + 4) * 128)
        rows.append(w_o[fr, Fc])
        rows.append(w_lo[fr, Fc])
    wcat = np.ascontiguousarray(np.concatenate(rows, axis=0))
    A = np.asarray(inp["L0_ffn_A"], dtype=np.float32).reshape(64, D).T
    gw = np.asarray(inp["L0_ffn_gate_W"], dtype=np.float32).T
    Bm = np.asarray(inp["L0_ffn_B"], dtype=np.float32)
    B_perm = np.ascontiguousarray(Bm.transpose(0, 2, 1).reshape(64, D))
    out = {
        "wcat": wcat,
        "x_sl": np.ascontiguousarray(
            np.asarray(inp["L0_attn_hidden_states"], dtype=np.float32)
            .reshape(T, D)[:, Fc]),
        "WbT_c": np.ascontiguousarray(
            np.asarray(inp["L0_ffn_W_base"], dtype=np.float32).T[:, Fc]),
        "AG_cat": np.ascontiguousarray(np.concatenate([A, gw], axis=1)),
        "B_c": np.ascontiguousarray(B_perm[:, Fc]),
    }
    if gathered is not None:
        out["gath0"] = np.ascontiguousarray(gathered[:, 0:S])
        out["gath1"] = np.ascontiguousarray(gathered[:, S:])
    return out


def build_full():
    nc = bass.Bass("TRN2", target_bir_lowering=False)
    io = declare_p_inputs(nc)
    io.update(declare_p_scratch(nc, external=False))
    io.update(declare_a_out(nc, external=False))
    io.update(declare_c_inputs(nc, gathered_external=False))
    with contextlib.ExitStack() as ctx:
        sm = make_p_sems(ctx, nc)
        sm.update(make_a_sems(ctx, nc))
        sm.update(make_c_sems(ctx, nc))
        pr = Prog(nc)
        with contextlib.ExitStack() as pctx:
            emit_phase_p(pr, pctx, io, sm)
        gate_a = [(sm["p_out"], sm["p_out"].n)]
        cbufs = alloc_c_sbuf(nc, ctx)
        s_cc = sm["c_cc"]
        holder = {}

        def post_b0():
            pr.op("gpsimd", lambda h: h.collective_compute(
                "AllGather", ALU.bypass, replica_groups=[list(range(8))],
                ins=[io["contrib0"].ap().opt()], outs=[io["gath0"].ap().opt()]),
                waits=[(sm["a_out"], sm["a_out_b0"])], inc=s_cc)
            holder["cg0"] = s_cc.n
        with contextlib.ExitStack() as actx:
            emit_phase_a(pr, actx, io, sm, after=gate_a, post_b0=post_b0)
        cg0 = holder["cg0"]
        pr.op("gpsimd", lambda h: h.collective_compute(
            "AllGather", ALU.bypass, replica_groups=[list(range(8))],
            ins=[io["contrib1"].ap().opt()], outs=[io["gath1"].ap().opt()]),
            waits=[(sm["a_out"], sm["a_out"].n)], inc=s_cc)
        cg1 = s_cc.n
        a_done = [(sm["a_qk"], sm["a_qk"].n), (sm["a_pv"], sm["a_pv"].n),
                  (sm["a_out"], sm["a_out"].n)]
        emit_phase_c(pr, ctx, io, sm, after=a_done, bufs=cbufs,
                     after_w=gate_a, cc_gate0=cg0, cc_gate1=cg1)
        block = ctx.enter_context(nc.Block())
        pr.replay(block)
    return nc


_NC_CACHE = {}


def kernel(**inputs):
    if "full" not in _NC_CACHE:
        _NC_CACHE["full"] = build_full()
    nc = _NC_CACHE["full"]
    in_maps = []
    for c in range(8):
        m = host_phase_p_inputs(c, inputs)
        m.update(host_phase_c_inputs(c, inputs))
        in_maps.append(m)
    res = run_bass_kernel_spmd(nc, in_maps, core_ids=list(range(8)))
    y = np.concatenate([res.results[c]["y_sl"] for c in range(8)], axis=1)
    return np.ascontiguousarray(y.reshape(B, S, D))


def kernel_unfused(**inputs):
    nc1 = build_pa_debug()
    in1 = [host_phase_p_inputs(c, inputs) for c in range(8)]
    res1 = run_bass_kernel_spmd(nc1, in1, core_ids=list(range(8)))
    gathered = np.concatenate(
        [np.concatenate([res1.results[c][f"contrib{i}"] for c in range(8)], axis=0)
         for i in range(2)], axis=1)
    nc2 = build_c_debug()
    in2 = [host_phase_c_inputs(c, inputs, gathered) for c in range(8)]
    res2 = run_bass_kernel_spmd(nc2, in2, core_ids=list(range(8)))
    y = np.concatenate([res2.results[c]["y_sl"] for c in range(8)], axis=1)
    return y.reshape(B, S, D)


# version 14 (final)
# speedup vs baseline: 1.1077x; 1.0132x over previous
import contextlib
import math

import numpy as np
import ml_dtypes
import concourse.bass as bass
import concourse.mybir as mybir
from concourse.bass_utils import run_bass_kernel_spmd

F32 = mybir.dt.float32
BF16 = mybir.dt.bfloat16
AF = mybir.ActivationFunctionType
ALU = mybir.AluOpType
AX = mybir.AxisListType

B, S, D = 2, 4096, 4096
T = B * S
HD = 128
SCALE = 1.0 / math.sqrt(HD)
EPS = 1e-5
THETA = 10000.0
NT = T // 128
ND = D // 128


class Sem:
    def __init__(self, stack, nc, name):
        self.h = stack.enter_context(nc.semaphore(name))
        self.n = 0

    def add(self, k):
        self.n += k
        return self.n


class Prog:
    ENGS = ("sync", "gpsimd", "tensor", "vector", "scalar")

    def __init__(self, nc):
        self.nc = nc
        self.q = {e: [] for e in self.ENGS}
        self.last_wait = {e: {} for e in self.ENGS}

    def op(self, eng, fn, waits=(), inc=None, inck=1):
        lw = self.last_wait[eng]
        wlist = []
        for s, v in waits:
            if v <= 0 or lw.get(id(s), 0) >= v:
                continue
            lw[id(s)] = v
            wlist.append((s, v))
        if inc is not None:
            inc.add(inck)
        self.q[eng].append((tuple(wlist), fn, inc, inck))

    def replay(self, block):
        for e in self.ENGS:
            if not self.q[e]:
                continue
            qq = self.q[e]

            def body(h, qq=qq):
                for waits, fn, inc, inck in qq:
                    for s, v in waits:
                        h.wait_ge(s.h, v)
                    if fn is None:
                        if inc is not None:
                            h.nop().then_inc(inc.h, inck)
                        continue
                    ins = fn(h)
                    if inc is not None:
                        ins.then_inc(inc.h, inck)
            getattr(block, e)(body)


def _f64_rope():
    inv = 1.0 / THETA ** (np.arange(0, HD, 2, dtype=np.float64) / HD)
    t = np.arange(S, dtype=np.float64)
    fr = np.outer(t, inv)
    emb = np.concatenate([fr, fr], axis=1)
    cos, sin = np.cos(emb), np.sin(emb)
    sinS = sin.copy()
    sinS[:, :64] *= -1.0
    return cos, sinS


def host_rope_table():
    cos, sinS = _f64_rope()
    cos2, sinS2 = np.tile(cos, (B, 1)), np.tile(sinS, (B, 1))
    tab = np.concatenate([cos2, sinS2, cos2 * SCALE, sinS2 * SCALE], axis=1)
    return np.ascontiguousarray(tab).astype(ml_dtypes.bfloat16)


def declare_p_inputs(nc):
    io = {}
    io["x_full"] = nc.dram_tensor("x_full", [T, D], F32, kind="ExternalInput")
    io["wP"] = nc.dram_tensor("wP", [D, 1280], F32, kind="ExternalInput")
    io["w_lqT"] = nc.dram_tensor("w_lqT", [1024, D], F32, kind="ExternalInput")
    io["w_lk_c"] = nc.dram_tensor("w_lk_c", [1024, 512], F32, kind="ExternalInput")
    io["ropeT"] = nc.dram_tensor("ropeT", [T, 512], BF16, kind="ExternalInput")
    return io


def declare_p_scratch(nc, external):
    kw = {"kind": "ExternalOutput"} if external else {}
    io = {}
    io["qT_d"] = nc.dram_tensor("qT_d", [4, 128, T], BF16, **kw)
    io["kT_d"] = nc.dram_tensor("kT_d", [128, T], BF16, **kw)
    io["lkT_d"] = nc.dram_tensor("lkT_d", [4, 128, T], BF16, **kw)
    io["v_d"] = nc.dram_tensor("v_d", [T, 128], BF16, **kw)
    io["lv_d"] = nc.dram_tensor("lv_d", [4, T, 128], BF16, **kw)
    io["norm_d"] = nc.dram_tensor("norm_d", [9, T], BF16, **kw)
    return io


def emit_phase_p(pr, ctx, io, sm):
    nc = pr.nc
    E = ctx.enter_context

    wlkc = E(nc.sbuf_tensor("p_wlkc", [128, ND, 512], BF16))
    with contextlib.ExitStack() as tmp:
        wlqT_sb = tmp.enter_context(nc.sbuf_tensor("p_wlqT", [128, 8, D], BF16))
        wlk_sb = tmp.enter_context(nc.sbuf_tensor("p_wlk", [128, 8, 512], BF16))
        plkc0 = tmp.enter_context(nc.psum_tensor("p_plkc0", [128, 512], F32))
        s_w = sm["p_w"]
        s_pre = sm["p_pre"]
        pr.op("gpsimd", lambda h: h.dma_start(
            out=wlqT_sb[:, :, :],
            in_=io["w_lqT"].ap().rearrange("(c p) d -> p c d", p=128)),
            inc=s_w, inck=16)
        pr.op("gpsimd", lambda h: h.dma_start(
            out=wlk_sb[:, :, :],
            in_=io["w_lk_c"].ap().rearrange("(c p) f -> p c f", p=128)),
            inc=s_w, inck=16)
        for dc in range(ND):
            def mmf(h, dc=dc):
                for lc in range(8):
                    ins = h.matmul(plkc0[:, :], wlqT_sb[:, lc, dc * 128:(dc + 1) * 128],
                                   wlk_sb[:, lc, :], start=(lc == 0), stop=(lc == 7))
                return ins
            pr.op("tensor", mmf, waits=[(s_w, 32), (s_pre, 2 * dc)], inc=s_pre)
            pr.op("vector", lambda h, dc=dc: h.tensor_copy(wlkc[:, dc, :], plkc0[:, :]),
                  waits=[(s_pre, 2 * dc + 1)], inc=s_pre)

    xs = [E(nc.sbuf_tensor(f"p_x{i}", [128, D], BF16)) for i in range(2)]
    xn = [E(nc.sbuf_tensor(f"p_xn{i}", [128, D], BF16)) for i in range(2)]
    st = [E(nc.sbuf_tensor(f"p_st{i}", [128, 8], F32)) for i in range(2)]
    xnt = [E(nc.sbuf_tensor(f"p_xnt{i}", [128, ND, 128], BF16)) for i in range(2)]
    ws = E(nc.sbuf_tensor("p_ws", [128, ND, 1280], BF16))
    rtab = [E(nc.sbuf_tensor(f"p_rtab{i}", [128, 4, 128], BF16)) for i in range(2)]
    qkl = E(nc.sbuf_tensor("p_qkl", [128, 9, 128], BF16))
    ropeb = E(nc.sbuf_tensor("p_ropeb", [128, 9, 128], BF16))
    ropet = E(nc.sbuf_tensor("p_ropet", [128, 9, 128], BF16))
    vst = E(nc.sbuf_tensor("p_vst", [128, 4, 5, 128], BF16))
    tst = E(nc.sbuf_tensor("p_tst", [128, 4, 9, 128], BF16))
    sqscr = E(nc.sbuf_tensor("p_sqscr", [128, 128], BF16))
    qn2 = E(nc.sbuf_tensor("p_qn2", [128, NT, 9], F32))
    nrows = E(nc.sbuf_tensor("p_nrows", [9, T], BF16))
    epsv = E(nc.sbuf_tensor("p_epsv", [128, 1], F32))
    ones_bf = E(nc.sbuf_tensor("p_onesb", [128, 128], BF16))
    ident_bf = E(nc.sbuf_tensor("p_identb", [128, 128], BF16))
    ones_f = E(nc.sbuf_tensor("p_onesf", [128, 128], F32))
    ident_f = E(nc.sbuf_tensor("p_identf", [128, 128], F32))
    pq = E(nc.psum_tensor("p_pq", [128, 512], F32))
    pkv = E(nc.psum_tensor("p_pkv", [128, 512], F32))
    plkc = E(nc.psum_tensor("p_plkc", [128, 512], F32))
    plv = E(nc.psum_tensor("p_plv", [128, 512], F32))
    ptb = [E(nc.psum_tensor(f"p_ptb{i}", [128, 1024], BF16)) for i in range(2)]
    pnt = [E(nc.psum_tensor(f"p_pnt{i}", [128, 512], F32)) for i in range(2)]

    s_w = sm["p_w"]; s_pre = sm["p_pre"]
    s_x = sm["p_x"]; s_rt = sm["p_rt"]
    s_ln = sm["p_ln"]; s_lns = sm["p_lns"]; s_xn = sm["p_xn"]
    s_tp = sm["p_tp"]; s_tpe = sm["p_tpe"]
    s_mm = sm["p_mm"]; s_ev = sm["p_ev"]
    s_ro = sm["p_ro"]; s_sc = sm["p_sc"]
    s_out = sm["p_out"]; s_nt = sm["p_nt"]

    pr.op("gpsimd", lambda h: h.memset(epsv[:, :], EPS), waits=[(sm["p_pre"], 64)])
    pr.op("gpsimd", lambda h: h.memset(ones_bf[:, :], 1.0))
    pr.op("gpsimd", lambda h: h.memset(ones_f[:, :], 1.0))
    pr.op("gpsimd", lambda h: h.affine_select(
        out=ident_bf[:, :], in_=ones_bf[:, :], pattern=[[1, 128]],
        compare_op=ALU.is_equal, fill=0.0, base=0, channel_multiplier=-1),
        inc=s_w)
    pr.op("gpsimd", lambda h: h.affine_select(
        out=ident_f[:, :], in_=ones_f[:, :], pattern=[[1, 128]],
        compare_op=ALU.is_equal, fill=0.0, base=0, channel_multiplier=-1),
        inc=s_w)
    pr.op("gpsimd", lambda h: h.dma_start(
        out=ws[:, :, :], in_=io["wP"].ap().rearrange("(c p) f -> p c f", p=128)),
        inc=s_w, inck=16)
    W_RDY = s_w.n

    TPg = 6
    ROg = 7
    EVg = 5
    SCg = 9
    LNg = 4
    LSg = 3

    for g in range(NT):
        par = g % 2
        strip, tt = g // 4, g % 4
        pr.op("gpsimd", lambda h, g=g, par=par: h.dma_start(
            out=xs[par][:, :], in_=io["x_full"][g * 128:(g + 1) * 128, :]),
            waits=[(s_xn, g - 1)], inc=s_x, inck=16)
        pr.op("sync", lambda h, g=g, par=par: h.dma_start(
            out=rtab[par][:, :, :],
            in_=io["ropeT"][g * 128:(g + 1) * 128, :]
                .rearrange("p (f d) -> p f d", d=128)),
            waits=[(s_ro, (g - 1) * ROg)] + ([(s_pre, 64)] if g < 2 else []),
            inc=s_rt, inck=16)
        pr.op("vector", lambda h, par=par: h.reduce_sum(
            st[par][:, 0:1], xs[par][:, :], axis=AX.X),
            waits=[(s_x, (g + 1) * 16), (s_xn, g - 1)], inc=s_ln)
        pr.op("scalar", lambda h, par=par: h.activation(
            xn[par][:, :], xs[par][:, :], AF.Square, accum_out=st[par][:, 1:2]),
            waits=[(s_x, (g + 1) * 16), (s_tp, ((g - 2) * TPg + 4) if g >= 2 else 0)],
            inc=s_lns)
        pr.op("vector", lambda h, par=par: h.tensor_scalar(
            out=st[par][:, 2:3], in0=st[par][:, 0:1], scalar1=1.0 / D, scalar2=None,
            op0=ALU.mult, op1=ALU.bypass), waits=[(s_ln, s_ln.n)], inc=s_ln)
        pr.op("vector", lambda h, par=par: h.tensor_scalar(
            out=st[par][:, 3:4], in0=st[par][:, 2:3], scalar1=st[par][:, 2:3],
            scalar2=None, op0=ALU.mult, op1=ALU.bypass),
            waits=[(s_ln, s_ln.n)], inc=s_ln)
        pr.op("vector", lambda h, par=par: h.tensor_scalar(
            out=st[par][:, 4:5], in0=st[par][:, 1:2], scalar1=1.0 / D,
            scalar2=st[par][:, 3:4], op0=ALU.mult, op1=ALU.subtract),
            waits=[(s_ln, s_ln.n), (s_lns, s_lns.n)], inc=s_ln)
        pr.op("scalar", lambda h, par=par: h.activation(
            st[par][:, 5:6], st[par][:, 4:5], AF.Ln, bias=epsv[:, 0:1]),
            waits=[(s_ln, s_ln.n)], inc=s_lns)
        pr.op("scalar", lambda h, par=par: h.activation(
            st[par][:, 6:7], st[par][:, 5:6], AF.Exp, scale=-0.5),
            waits=[(s_lns, s_lns.n)], inc=s_lns)
        pr.op("vector", lambda h, par=par: h.tensor_scalar(
            out=st[par][:, 7:8], in0=st[par][:, 2:3], scalar1=st[par][:, 6:7],
            scalar2=-1.0, op0=ALU.mult, op1=ALU.mult),
            waits=[(s_lns, s_lns.n)], inc=s_ln)
        pr.op("scalar", lambda h, par=par: h.activation(
            xn[par][:, :], xs[par][:, :], AF.Identity,
            bias=st[par][:, 7:8], scale=st[par][:, 6:7]),
            waits=[(s_ln, s_ln.n)], inc=s_xn)
        for j in range(4):
            G = g * TPg + j
            def tpf(h, j=j, par=par):
                for k in range(8):
                    dc = j * 8 + k
                    ins = h.transpose(ptb[(g * TPg + j) % 2][:, k * 128:(k + 1) * 128],
                                      xn[par][:, dc * 128:(dc + 1) * 128],
                                      ident_bf[:, :])
                return ins
            pr.op("tensor", tpf,
                  waits=[(s_xn, g + 1), (s_w, W_RDY), (s_tpe, G - 1)]
                        + ([(s_pre, 64)] if g == 0 else []), inc=s_tp)
            pr.op("vector", lambda h, j=j, par=par, G=G: h.tensor_copy(
                xnt[par][:, j * 8:(j + 1) * 8, :].rearrange("p c d -> p (c d)"),
                ptb[G % 2][:, 0:1024]),
                waits=[(s_tp, G + 1)], inc=s_tpe)
        segs = [(pq, 0, 512), (pkv, 512, 768), (None, 0, 512), (plv, 768, 1280)]
        for si, (pslot, lo, hi) in enumerate(segs):
            def mmf(h, si=si, pslot=pslot, lo=lo, hi=hi, par=par):
                for dc in range(ND):
                    if si == 2:
                        ins = h.matmul(plkc[:, 0:512], xnt[par][:, dc, :],
                                       wlkc[:, dc, :], start=(dc == 0),
                                       stop=(dc == ND - 1))
                    else:
                        ins = h.matmul(pslot[:, 0:hi - lo], xnt[par][:, dc, :],
                                       ws[:, dc, lo:hi], start=(dc == 0),
                                       stop=(dc == ND - 1))
                return ins
            pr.op("tensor", mmf,
                  waits=[(s_tpe, g * TPg + 4), (s_ev, g * EVg), (s_pre, 64)],
                  inc=s_mm)
        M0 = g * 4
        evw = [(s_ro, (g - 1) * ROg + 6), (s_out, (strip - 1) * 14 * 16)]
        pr.op("scalar", lambda h: h.copy(
            qkl[:, 0:4, :].rearrange("p h d -> p (h d)"), pq[:, 0:512]),
            waits=[(s_mm, M0 + 1)] + evw, inc=s_ev)
        pr.op("scalar", lambda h: h.copy(qkl[:, 4, :], pkv[:, 0:128]),
              waits=[(s_mm, M0 + 2)], inc=s_ev)
        pr.op("scalar", lambda h, tt=tt: h.copy(vst[:, tt, 0, :], pkv[:, 128:256]),
              inc=s_ev)
        pr.op("scalar", lambda h: h.copy(
            qkl[:, 5:9, :].rearrange("p h d -> p (h d)"), plkc[:, 0:512]),
            waits=[(s_mm, M0 + 3)], inc=s_ev)
        pr.op("scalar", lambda h, tt=tt: h.copy(
            vst[:, tt, 1:5, :].rearrange("p h d -> p (h d)"), plv[:, 0:512]),
            waits=[(s_mm, M0 + 4)], inc=s_ev)
        RO0 = g * ROg
        groups = [(0, 4, 0), (4, 1, 2), (5, 4, 2)]
        for (h0, nh, tf) in groups:
            pr.op("vector", lambda h, h0=h0, nh=nh, tf=tf, par=par: h.tensor_tensor(
                ropeb[:, h0:h0 + nh, :], qkl[:, h0:h0 + nh, :],
                bass.AP(rtab[par], tf * 128, [[4 * 128, 128], [0, nh], [1, 128]]),
                op=ALU.mult),
                waits=[(s_ev, g * EVg + EVg), (s_rt, (g + 1) * 16),
                       (s_tp, g * TPg),
                       (s_sc, g * SCg)],
                inc=s_ro)
        for (h0, nh, tf) in groups:
            pr.op("vector", lambda h, h0=h0, nh=nh, tf=tf, par=par: h.tensor_tensor(
                bass.AP(ropet, h0 * 128, [[9 * 128, 128], [128, nh], [64, 2], [1, 64]]),
                bass.AP(qkl, h0 * 128 + 64, [[9 * 128, 128], [128, nh], [-64, 2], [1, 64]]),
                bass.AP(rtab[par], (tf + 1) * 128, [[4 * 128, 128], [0, nh], [64, 2], [1, 64]]),
                op=ALU.mult), inc=s_ro)
        pr.op("vector", lambda h: h.tensor_tensor(
            ropeb[:, :, :], ropeb[:, :, :], ropet[:, :, :], op=ALU.add),
            waits=[(s_ro, RO0 + 6)], inc=s_ro)
        for hh in range(9):
            pr.op("scalar", lambda h, hh=hh, g=g: h.activation(
                sqscr[:, :], ropeb[:, hh, :], AF.Square,
                accum_out=qn2[:, g, hh:hh + 1]),
                waits=[(s_ro, RO0 + ROg)], inc=s_sc)
        for j, (o0, cnt) in enumerate([(0, 8), (8, 1)]):
            G = g * TPg + 4 + j
            def tpq(h, o0=o0, cnt=cnt, G=G):
                for k in range(cnt):
                    ins = h.transpose(ptb[G % 2][:, k * 128:(k + 1) * 128],
                                      ropeb[:, o0 + k, :], ident_bf[:, :])
                return ins
            pr.op("tensor", tpq, waits=[(s_ro, RO0 + ROg), (s_tpe, G - 1)], inc=s_tp)
            pr.op("vector", lambda h, o0=o0, cnt=cnt, G=G, tt=tt: h.tensor_copy(
                tst[:, tt, o0:o0 + cnt, :].rearrange("p h d -> p (h d)"),
                ptb[G % 2][:, 0:cnt * 128]),
                waits=[(s_tp, G + 1), (s_out, (strip - 1) * 14 * 16)], inc=s_tpe)
        if tt == 3:
            OUTW = [(s_tpe, (g + 1) * TPg), (s_ev, (g + 1) * EVg)]
            for hh in range(4):
                pr.op("sync", lambda h, s=strip, hh=hh: h.dma_start(
                    out=io["qT_d"].ap()[hh, :, s * 512:(s + 1) * 512]
                        .rearrange("p (q d) -> p q d", q=4),
                    in_=bass.AP(tst, hh * 128,
                                [[4 * 9 * 128, 128], [9 * 128, 4], [1, 128]])),
                    waits=OUTW, inc=s_out, inck=16)
            pr.op("sync", lambda h, s=strip: h.dma_start(
                out=io["kT_d"].ap()[:, s * 512:(s + 1) * 512]
                    .rearrange("p (q d) -> p q d", q=4),
                in_=bass.AP(tst, 4 * 128,
                            [[4 * 9 * 128, 128], [9 * 128, 4], [1, 128]])),
                inc=s_out, inck=16)
            for hh in range(4):
                pr.op("sync", lambda h, s=strip, hh=hh: h.dma_start(
                    out=io["lkT_d"].ap()[hh, :, s * 512:(s + 1) * 512]
                        .rearrange("p (q d) -> p q d", q=4),
                    in_=bass.AP(tst, (5 + hh) * 128,
                                [[4 * 9 * 128, 128], [9 * 128, 4], [1, 128]])),
                    inc=s_out, inck=16)
            pr.op("sync", lambda h, s=strip: h.dma_start(
                out=io["v_d"].ap()[s * 512:(s + 1) * 512, :]
                    .rearrange("(q p) d -> p q d", p=128),
                in_=vst[:, :, 0, :]), inc=s_out, inck=16)
            for hh in range(4):
                pr.op("sync", lambda h, s=strip, hh=hh: h.dma_start(
                    out=io["lv_d"].ap()[hh, s * 512:(s + 1) * 512, :]
                        .rearrange("(q p) d -> p q d", p=128),
                    in_=vst[:, :, 1 + hh, :]), inc=s_out, inck=16)

    for g in range(NT):
        pr.op("tensor", lambda h, g=g: h.transpose(
            pnt[g % 2][0:9, 0:128], qn2[:, g, :], ident_f[:, :]),
            waits=[(s_sc, (g + 1) * SCg), (s_nt, 2 * g - 1)], inc=s_nt)
        pr.op("vector", lambda h, g=g: h.tensor_copy(
            nrows[0:9, g * 128:(g + 1) * 128], pnt[g % 2][0:9, 0:128]),
            waits=[(s_nt, 2 * g + 1)], inc=s_nt)
    pr.op("scalar", lambda h: h.activation(nrows[0:9, :], nrows[0:9, :], AF.Ln),
          waits=[(s_nt, 2 * NT)], inc=s_sc)
    pr.op("scalar", lambda h: h.activation(nrows[0:9, :], nrows[0:9, :], AF.Exp,
                                           scale=0.5),
          waits=[(s_sc, s_sc.n)], inc=s_sc)
    pr.op("sync", lambda h: h.dma_start(out=io["norm_d"].ap()[0:9, :],
                                        in_=nrows[0:9, :]),
          waits=[(s_sc, s_sc.n)], inc=s_out, inck=16)
    return {"nrows": nrows}


def make_p_sems(stack, nc):
    names = ["p_w", "p_pre", "p_x", "p_rt", "p_ln", "p_lns", "p_xn", "p_tp",
             "p_tpe", "p_mm", "p_ev", "p_ro", "p_sc", "p_out", "p_nt"]
    return {n: Sem(stack, nc, n) for n in names}


def build_p_debug():
    nc = bass.Bass("TRN2", target_bir_lowering=False)
    io = declare_p_inputs(nc)
    io.update(declare_p_scratch(nc, external=True))
    with contextlib.ExitStack() as ctx:
        sm = make_p_sems(ctx, nc)
        pr = Prog(nc)
        emit_phase_p(pr, ctx, io, sm)
        pr.op("sync", None, waits=[(sm["p_out"], sm["p_out"].n)])
        block = ctx.enter_context(nc.Block())
        pr.replay(block)
    return nc


def host_phase_p_inputs(c, inp):
    qs = slice(512 * c, 512 * c + 512)
    ks = slice(128 * c, 128 * c + 128)
    wP = np.concatenate(
        [inp["L0_attn_w_q"][:, qs], inp["L0_attn_w_k"][:, ks],
         inp["L0_attn_w_v"][:, ks], inp["L0_attn_w_lv"][:, qs]], axis=1)
    return {
        "x_full": np.ascontiguousarray(
            np.asarray(inp["L0_attn_hidden_states"], dtype=np.float32).reshape(T, D)),
        "wP": np.ascontiguousarray(wP, dtype=np.float32),
        "w_lqT": np.ascontiguousarray(np.asarray(inp["L0_attn_w_lq"]).T, dtype=np.float32),
        "w_lk_c": np.ascontiguousarray(np.asarray(inp["L0_attn_w_lk"])[:, qs], dtype=np.float32),
        "ropeT": host_rope_table(),
    }


def declare_a_out(nc, external):
    kw = {"kind": "ExternalOutput"} if external else {}
    return {"contrib0": nc.dram_tensor("contrib0", [1024, S], BF16, **kw),
            "contrib1": nc.dram_tensor("contrib1", [1024, S], BF16, **kw)}


def make_a_sems(stack, nc):
    names = ["a_kv", "a_q", "a_n", "a_mb", "a_mbe", "a_qk", "a_sub", "a_exp",
             "a_msk", "a_pv", "a_srw", "a_oev", "a_out", "a_kmx"]
    return {n: Sem(stack, nc, n) for n in names}


def emit_phase_a(pr, ctx, io, sm, after=(), post_b0=None):
    nc = pr.nc
    E = ctx.enter_context

    kTs = [E(nc.sbuf_tensor(f"a_kT{i}", [128, S], BF16)) for i in range(2)]
    vs = [E(nc.sbuf_tensor(f"a_v{i}", [128, 32, 128], BF16)) for i in range(2)]
    qTs = [E(nc.sbuf_tensor(f"a_qT{i}", [128, S], BF16)) for i in range(2)]
    mnst0 = E(nc.sbuf_tensor("a_mnst", [1, S], BF16))
    mnst = [mnst0, mnst0]
    knst = mnst0
    mrow0 = E(nc.sbuf_tensor("a_mrow", [1, S], BF16))
    mrow = [mrow0, mrow0]
    kmx = E(nc.sbuf_tensor("a_kmx", [1, 16], F32))
    pbuf = [E(nc.sbuf_tensor(f"a_pbuf{i}", [128, 2048], BF16)) for i in range(2)]
    msn = [E(nc.sbuf_tensor(f"a_msn{i}", [1, 1], BF16)) for i in range(2)]
    msb = [E(nc.sbuf_tensor(f"a_msb{i}", [128, 1], F32)) for i in range(2)]
    pbuf2 = E(nc.sbuf_tensor("a_pbuf2", [128, 2048], BF16))
    rrow = E(nc.sbuf_tensor("a_rrow", [1, 512], F32))
    rb = [E(nc.sbuf_tensor(f"a_rb{i}", [128, 512], F32)) for i in range(2)]
    outs = [E(nc.sbuf_tensor(f"a_out{i}", [128, 512], BF16)) for i in range(2)]
    ones1b = E(nc.sbuf_tensor("a_ones1b", [1, 128], BF16))
    ones1f = E(nc.sbuf_tensor("a_ones1f", [1, 128], F32))
    ones128 = E(nc.sbuf_tensor("a_ones128", [128, 1], BF16))

    ps = [E(nc.psum_tensor(f"a_ps{i}", [128, 1024], F32)) for i in range(2)]
    po = [E(nc.psum_tensor(f"a_po{i}", [128, 512], F32)) for i in range(2)]
    psm = E(nc.psum_tensor("a_psm", [128, 512], F32))
    pmb = E(nc.psum_tensor("a_pmb", [128, 512], F32))

    s_kv = sm["a_kv"]; s_q = sm["a_q"]; s_n = sm["a_n"]
    s_mb = sm["a_mb"]; s_mbe = sm["a_mbe"]
    s_qk = sm["a_qk"]; s_sub = sm["a_sub"]; s_exp = sm["a_exp"]
    s_msk = sm["a_msk"]; s_pv = sm["a_pv"]; s_srw = sm["a_srw"]
    s_oev = sm["a_oev"]; s_out = sm["a_out"]; s_kmx = sm["a_kmx"]

    pr.op("gpsimd", lambda h: h.memset(ones1b[:, :], 1.0), waits=list(after))
    pr.op("gpsimd", lambda h: h.memset(ones1f[:, :], 1.0))
    pr.op("gpsimd", lambda h: h.memset(ones128[:, :], 1.0), inc=s_kmx)

    runs = []
    for b in range(B):
        runs.append(("attn", b, [0, 1, 2, 3], 4, 0))
        for hl in range(4):
            runs.append(("lat", b, [hl], 5 + hl, 512))

    for r, (kind, b, qheads, krow, outbase) in enumerate(runs):
        pr.op("sync", lambda h, b=b, krow=krow: h.dma_start(
            out=knst[0:1, :], in_=io["norm_d"][krow:krow + 1, b * S:(b + 1) * S]),
            waits=list(after) + [(s_kmx, s_kmx.n)], inc=s_n, inck=16)
        pr.op("vector", lambda h, r=r: h.reduce_max(
            kmx[0:1, r:r + 1], knst[0:1, :], axis=AX.X),
            waits=[(s_n, s_n.n), (s_kmx, s_kmx.n)], inc=s_kmx)
    KMX_RDY = s_kmx.n

    qk_after_run, pv_after_run = [], []
    qk_after_qrun = []
    last_exp_half = [0, 0]
    last_pv_par = [0, 0]
    last_pv_diag = 0
    mnst_rd = [0, 0]
    mrow_used = 0
    msb_used = [0, 0]
    qrun = 0
    strip_g = 0

    for r, (kind, b, qheads, krow, outbase) in enumerate(runs):
        kvp = r % 2
        tok0 = b * S
        kv_waits = list(after) if r == 0 else \
            [(s_qk, qk_after_run[r - 2]), (s_pv, pv_after_run[r - 2])] if r >= 2 else []
        if kind == "attn":
            pr.op("sync", lambda h, kvp=kvp, tok0=tok0: h.dma_start(
                out=kTs[kvp][:, :], in_=io["kT_d"][:, tok0:tok0 + S]),
                waits=kv_waits, inc=s_kv, inck=16)
            pr.op("sync", lambda h, kvp=kvp, tok0=tok0: h.dma_start(
                out=vs[kvp][:, :, :],
                in_=io["v_d"][tok0:tok0 + S, :].rearrange("(c p) d -> p c d", p=128)),
                inc=s_kv, inck=16)
        else:
            hl = qheads[0]
            pr.op("sync", lambda h, kvp=kvp, tok0=tok0, hl=hl: h.dma_start(
                out=kTs[kvp][:, :], in_=io["lkT_d"][hl, :, tok0:tok0 + S]),
                waits=kv_waits, inc=s_kv, inck=16)
            pr.op("sync", lambda h, kvp=kvp, tok0=tok0, hl=hl: h.dma_start(
                out=vs[kvp][:, :, :],
                in_=io["lv_d"][hl, tok0:tok0 + S, :].rearrange("(c p) d -> p c d", p=128)),
                inc=s_kv, inck=16)
        KV_RDY = s_kv.n

        for qh in qheads:
            qp = qrun % 2
            qgate = [(s_qk, qk_after_qrun[qrun - 2])] if qrun >= 2 else list(after)
            pr.op("sync", lambda h, qp=qp, qh=qh, tok0=tok0: h.dma_start(
                out=qTs[qp][:, :], in_=io["qT_d"][qh, :, tok0:tok0 + S]),
                waits=qgate, inc=s_q, inck=16)
            Q_RDY = s_q.n
            pr.op("sync", lambda h, qp=qp, qh=qh, tok0=tok0: h.dma_start(
                out=mnst[qp][0:1, :], in_=io["norm_d"][qh:qh + 1, tok0:tok0 + S]),
                waits=[(s_srw, s_srw.n), (s_kmx, KMX_RDY)], inc=s_n, inck=16)
            N_RDY = s_n.n
            pr.op("vector", lambda h, qp=qp, r=r: h.tensor_scalar(
                out=mrow[qp][0:1, :], in0=mnst[qp][0:1, :],
                scalar1=kmx[0:1, r:r + 1], scalar2=-1.0,
                op0=ALU.mult, op1=ALU.mult),
                waits=[(s_kmx, KMX_RDY), (s_n, N_RDY)],
                inc=s_srw)
            MR = s_srw.n
            mnst_rd[qp] = MR

            steps = [(s, g2) for s in range(8) for g2 in range(s + 1)]
            prev = None

            def emit_pv(st):
                s, g2 = st
                sp = (strip_g0 + s) % 2
                NC = 4 * (s + 1)
                diag = (g2 == s)
                gp = g2 % 2
                src = pbuf2 if diag else pbuf[gp]
                def pvf(h, g2=g2, src=src, kvp=kvp, NC=NC, sp=sp):
                    for j in range(4):
                        kc = g2 * 4 + j
                        h.matmul(po[sp][:, :], vs[kvp][:, kc, :],
                                 src[:, j * 512:(j + 1) * 512],
                                 start=(kc == 0), stop=(kc == NC - 1))
                        ins = h.matmul(psm[0:1, :], ones128[:, 0:1],
                                       src[:, j * 512:(j + 1) * 512],
                                       start=(kc == 0), stop=(kc == NC - 1))
                    return ins
                pv_waits = [(s_exp, exp_of[st])]
                if diag:
                    pv_waits.append((s_msk, msk_of[st]))
                if g2 == 0:
                    pv_waits += [(s_oev, max(0, strip_g0 + s - 1)), (s_srw, s_srw.n)]
                pr.op("tensor", pvf, waits=pv_waits, inc=s_pv, inck=4)
                if diag:
                    nonlocal last_pv_diag
                    last_pv_diag = s_pv.n
                else:
                    last_pv_par[gp] = s_pv.n
                if diag:
                    emit_epilogue(s)

            def emit_epilogue(s):
                msb_used[(strip_g0 + s) % 2] = s_exp.n
                sp = (strip_g0 + s) % 2
                sg = strip_g0 + s
                q0 = s * 512
                pr.op("vector", lambda h: h.reciprocal(rrow[0:1, :], psm[0:1, :]),
                      waits=[(s_pv, s_pv.n)], inc=s_srw)
                pr.op("tensor", lambda h: h.matmul(
                    pmb[:, :], ones1f[0:1, :], rrow[0:1, :], start=True, stop=True),
                    waits=[(s_srw, s_srw.n), (s_mbe, s_mbe.n)], inc=s_mb)
                pr.op("vector", lambda h, sp=sp: h.tensor_copy(rb[sp][:, :], pmb[:, :]),
                      waits=[(s_mb, s_mb.n), (s_oev, max(0, sg - 1))], inc=s_mbe)
                pr.op("vector", lambda h, sp=sp: h.tensor_tensor(
                    out=outs[sp][:, :], in0=po[sp][:, :], in1=rb[sp][:, :], op=ALU.mult),
                    waits=[(s_mbe, s_mbe.n), (s_out, 16 * max(0, sg - 1))],
                    inc=s_oev)
                row0 = outbase + qh * 128
                ct = io["contrib0"] if b == 0 else io["contrib1"]
                pr.op("sync", lambda h, sp=sp, row0=row0, q0=q0, ct=ct: h.dma_start(
                    out=ct[row0:row0 + 128, q0:q0 + 512],
                    in_=outs[sp][:, :]),
                    waits=[(s_oev, s_oev.n)], inc=s_out, inck=16)

            strip_g0 = strip_g
            exp_of, msk_of = {}, {}
            msb_rdy = {}

            def emit_msb_prep(s8):
                sp8 = (strip_g0 + s8) % 2
                pr.op("vector", lambda h, sp8=sp8, s8=s8, qp=qp: h.tensor_reduce(
                    msn[sp8][0:1, 0:1], mrow[qp][0:1, s8 * 512:(s8 + 1) * 512],
                    axis=AX.X, op=ALU.min),
                    waits=[(s_srw, MR), (s_exp, msb_used[sp8])], inc=s_srw)
                pr.op("tensor", lambda h, sp8=sp8: h.matmul(
                    pmb[:, 0:1], ones1b[0:1, :], msn[sp8][0:1, 0:1],
                    start=True, stop=True),
                    waits=[(s_srw, s_srw.n), (s_mbe, s_mbe.n)], inc=s_mb)
                pr.op("vector", lambda h, sp8=sp8: h.tensor_copy(
                    msb[sp8][:, 0:1], pmb[:, 0:1]),
                    waits=[(s_mb, s_mb.n)], inc=s_mbe)
                msb_rdy[s8] = s_mbe.n
            for st in steps:
                s, g2 = st
                q0 = s * 512
                gp = g2 % 2
                diag = (g2 == s)
                sp8 = (strip_g0 + s) % 2
                if g2 == 0:
                    emit_msb_prep(s)
                for half in range(2):
                    def qkf(h, g2=g2, half=half, kvp=kvp, qp=qp, q0=q0):
                        for j in range(2):
                            kc = g2 * 4 + half * 2 + j
                            ins = h.matmul(
                                ps[half][:, j * 512:(j + 1) * 512],
                                kTs[kvp][:, kc * 128:(kc + 1) * 128],
                                qTs[qp][:, q0:q0 + 512],
                                start=True, stop=True)
                        return ins
                    pr.op("tensor", qkf,
                          waits=[(s_kv, KV_RDY), (s_q, Q_RDY),
                                 (s_exp, last_exp_half[half])],
                          inc=s_qk)
                    pr.op("scalar", lambda h, gp=gp, half=half, sp8=sp8: h.activation(
                        bass.AP(pbuf[gp], half * 1024, [[2048, 128], [1, 1024]]),
                        ps[half][:, :], AF.Exp, bias=msb[sp8][:, 0:1]),
                        waits=[(s_qk, s_qk.n), (s_pv, last_pv_par[gp]),
                               (s_mbe, msb_rdy[s])],
                        inc=s_exp)
                    last_exp_half[half] = s_exp.n
                exp_of[st] = s_exp.n
                if diag:
                    for j in range(4):
                        kc = g2 * 4 + j
                        pr.op("gpsimd", lambda h, gp=gp, j=j, kc=kc, q0=q0:
                              h.affine_select(
                                  out=pbuf2[:, j * 512:(j + 1) * 512],
                                  in_=pbuf[gp][:, j * 512:(j + 1) * 512],
                                  pattern=[[1, 512]], compare_op=ALU.is_ge,
                                  fill=0.0, base=q0 - kc * 128,
                                  channel_multiplier=-1),
                              waits=[(s_exp, s_exp.n), (s_pv, last_pv_diag)],
                              inc=s_msk)
                    msk_of[st] = s_msk.n
                if prev is not None:
                    emit_pv(prev)
                prev = st
            emit_pv(prev)
            strip_g += 8
            qk_after_qrun.append(s_qk.n)
            mrow_used = s_qk.n
            qrun += 1
        qk_after_run.append(s_qk.n)
        pv_after_run.append(s_pv.n)
        if r == 4:
            sm["a_out_b0"] = s_out.n
            if post_b0 is not None:
                post_b0()
    return {}


def build_pa_debug():
    nc = bass.Bass("TRN2", target_bir_lowering=False)
    io = declare_p_inputs(nc)
    io.update(declare_p_scratch(nc, external=True))
    io.update(declare_a_out(nc, external=True))
    with contextlib.ExitStack() as ctx:
        sm = make_p_sems(ctx, nc)
        sm.update(make_a_sems(ctx, nc))
        pr = Prog(nc)
        with contextlib.ExitStack() as pctx:
            emit_phase_p(pr, pctx, io, sm)
        gate = [(sm["p_out"], sm["p_out"].n)]
        emit_phase_a(pr, ctx, io, sm, after=gate)
        pr.op("sync", None, waits=[(sm["a_out"], sm["a_out"].n)])
        block = ctx.enter_context(nc.Block())
        pr.replay(block)
    return nc


def declare_c_inputs(nc, gathered_external):
    io = {}
    kwg = {"kind": "ExternalInput"} if gathered_external else {"addr_space": "Shared"}
    io["gath0"] = nc.dram_tensor("gath0", [8192, S], BF16, **kwg)
    io["gath1"] = nc.dram_tensor("gath1", [8192, S], BF16, **kwg)
    io["wcat"] = nc.dram_tensor("wcat", [8192, 512], F32, kind="ExternalInput")
    io["x_sl"] = nc.dram_tensor("x_sl", [T, 512], F32, kind="ExternalInput")
    io["WbT_c"] = nc.dram_tensor("WbT_c", [D, 512], F32, kind="ExternalInput")
    io["AG_cat"] = nc.dram_tensor("AG_cat", [D, 72], F32, kind="ExternalInput")
    io["B_c"] = nc.dram_tensor("B_c", [64, 512], F32, kind="ExternalInput")
    io["y_sl"] = nc.dram_tensor("y_sl", [T, 512], F32, kind="ExternalOutput")
    io["o1_d"] = nc.dram_tensor("o1_d", [T, 512], BF16)
    io["x1_d"] = nc.dram_tensor("x1_d", [T, 512], BF16)
    io["ffn_d"] = nc.dram_tensor("ffn_d", [T, 512], BF16)
    for i in range(3):
        io[f"sti{i}"] = nc.dram_tensor(f"sti{i}", [2, 128, NT], F32)
        io[f"sto{i}"] = nc.dram_tensor(f"sto{i}", [2, 128, NT], F32)
    io["xfc_d"] = nc.dram_tensor("xfc_d", [512, T], BF16)
    io["xfg_d"] = nc.dram_tensor("xfg_d", [D, T], BF16, addr_space="Shared")
    return io


def make_c_sems(stack, nc):
    names = ["c_w", "c_g", "c_mm1", "c_st", "c_sts", "c_ev", "c_dma", "c_cc",
             "c_fin", "c_x", "c_tp", "c_tpe", "c_mm3", "c_hg", "c_t2", "c_out",
             "c_fins"]
    return {n: Sem(stack, nc, n) for n in names}


def alloc_c_sbuf(nc, ctx):
    E = ctx.enter_context
    bufs = {}
    bufs["stats"] = E(nc.sbuf_tensor("c_stats", [128, 6, NT], F32))
    bufs["fin"] = E(nc.sbuf_tensor("c_fin", [128, 6, NT], F32))
    bufs["ftmp"] = E(nc.sbuf_tensor("c_ftmp", [128, 2, NT], F32))
    bufs["scr5"] = E(nc.sbuf_tensor("c_scr5", [128, 512], BF16))
    bufs["epsv"] = E(nc.sbuf_tensor("c_epsv", [128, 1], F32))
    bufs["ones_bf"] = E(nc.sbuf_tensor("c_onesb", [128, 128], BF16))
    bufs["ident_bf"] = E(nc.sbuf_tensor("c_identb", [128, 128], BF16))
    bufs["hg"] = E(nc.sbuf_tensor("c_hg", [128, 72], F32))
    bufs["t2b"] = E(nc.sbuf_tensor("c_t2b", [128, 40], F32))
    bufs["wd"] = E(nc.sbuf_tensor("c_wd", [128, 8], F32))
    bufs["hsc"] = E(nc.sbuf_tensor("c_hsc", [128, 64], BF16))
    bufs["hscT"] = E(nc.sbuf_tensor("c_hscT", [64, 128], BF16))
    bufs["xt"] = [E(nc.sbuf_tensor(f"c_xt{i}", [128, 512], F32)) for i in range(2)]
    bufs["aux"] = [E(nc.sbuf_tensor(f"c_aux{i}", [128, 512], BF16)) for i in range(2)]
    bufs["ev2"] = [E(nc.sbuf_tensor(f"c_ev2{i}", [128, 512], BF16)) for i in range(2)]
    _t1 = E(nc.sbuf_tensor("c_tmp1", [128, 512], F32))
    bufs["tmp1"] = [_t1, _t1]
    bufs["outt"] = bufs["xt"]
    bufs["xfst"] = [E(nc.sbuf_tensor(f"c_xfst{i}", [128, 4, 128], BF16)) for i in range(2)]
    bufs["wcat_sb"] = E(nc.sbuf_tensor("c_wcat", [128, 64, 512], BF16))
    return bufs


def emit_phase_c(pr, ctx, io, sm, after=(), upto=None, bufs=None,
                 after_w=None, cc_gate0=1, cc_gate1=1):
    nc = pr.nc
    E = ctx.enter_context

    s_w = sm["c_w"]; s_g = sm["c_g"]; s_mm1 = sm["c_mm1"]
    s_st = sm["c_st"]; s_sts = sm["c_sts"]; s_ev = sm["c_ev"]
    s_dma = sm["c_dma"]; s_cc = sm["c_cc"]; s_fin = sm["c_fin"]
    s_x = sm["c_x"]; s_tp = sm["c_tp"]; s_tpe = sm["c_tpe"]
    s_mm3 = sm["c_mm3"]; s_hg = sm["c_hg"]; s_t2 = sm["c_t2"]
    s_out = sm["c_out"]; s_fns = sm["c_fins"]

    if bufs is None:
        bufs = alloc_c_sbuf(nc, ctx)
    if after_w is None:
        after_w = list(after)
    stats = bufs["stats"]; fin = bufs["fin"]; ftmp = bufs["ftmp"]
    scr5 = bufs["scr5"]; epsv = bufs["epsv"]; ones_bf = bufs["ones_bf"]
    ident_bf = bufs["ident_bf"]; hg = bufs["hg"]; t2b = bufs["t2b"]
    wd = bufs["wd"]; hsc = bufs["hsc"]; hscT = bufs["hscT"]
    xt = bufs["xt"]; aux = bufs["aux"]; ev2 = bufs["ev2"]
    tmp1 = bufs["tmp1"]; outt = bufs["outt"]; xfst = bufs["xfst"]

    po1 = [E(nc.psum_tensor(f"c_po1{i}", [128, 512], F32)) for i in range(2)]
    pf = [E(nc.psum_tensor(f"c_pf{i}", [128, 512], F32)) for i in range(2)]
    phg = [E(nc.psum_tensor(f"c_phg{i}", [128, 512], F32)) for i in range(2)]
    ptb = [E(nc.psum_tensor(f"c_ptb{i}", [128, 1024], BF16)) for i in range(2)]

    pr.op("gpsimd", lambda h: h.memset(epsv[:, :], EPS), waits=list(after_w))
    pr.op("gpsimd", lambda h: h.memset(ones_bf[:, :], 1.0))
    pr.op("gpsimd", lambda h: h.affine_select(
        out=ident_bf[:, :], in_=ones_bf[:, :], pattern=[[1, 128]],
        compare_op=ALU.is_equal, fill=0.0, base=0, channel_multiplier=-1),
        inc=s_w)
    ID_RDY = s_w.n

    def stats_ar(idx):
        pr.op("sync", lambda h, idx=idx: h.dma_start(
            out=io[f"sti{idx}"].ap().rearrange("s p t -> p s t"),
            in_=stats[:, 2 * idx:2 * idx + 2, :]),
            waits=[(s_dma, s_dma.n)], inc=s_dma, inck=16)
        pr.op("gpsimd", lambda h, idx=idx: h.collective_compute(
            "AllReduce", ALU.add, replica_groups=[list(range(8))],
            ins=[io[f"sti{idx}"].ap().opt()], outs=[io[f"sto{idx}"].ap().opt()]),
            waits=[(s_dma, s_dma.n)], inc=s_cc)
        pr.op("sync", lambda h, idx=idx: h.dma_start(
            out=stats[:, 2 * idx:2 * idx + 2, :],
            in_=io[f"sto{idx}"].ap().rearrange("s p t -> p s t")),
            waits=[(s_cc, s_cc.n)], inc=s_dma, inck=16)
        AR = s_dma.n
        pr.op("vector", lambda h, idx=idx: h.tensor_scalar(
            out=fin[:, 2 * idx, :], in0=stats[:, 2 * idx, :],
            scalar1=1.0 / D, scalar2=None, op0=ALU.mult, op1=ALU.bypass),
            waits=[(s_dma, AR)], inc=s_fin)
        pr.op("vector", lambda h, idx=idx: h.tensor_tensor(
            out=ftmp[:, 0, :], in0=fin[:, 2 * idx, :], in1=fin[:, 2 * idx, :],
            op=ALU.mult), waits=[(s_fin, s_fin.n)], inc=s_fin)
        pr.op("vector", lambda h, idx=idx: h.tensor_scalar(
            out=ftmp[:, 1, :], in0=stats[:, 2 * idx + 1, :],
            scalar1=1.0 / D, scalar2=None, op0=ALU.mult, op1=ALU.bypass),
            waits=[(s_dma, AR)], inc=s_fin)
        pr.op("vector", lambda h: h.tensor_tensor(
            out=ftmp[:, 1, :], in0=ftmp[:, 1, :], in1=ftmp[:, 0, :],
            op=ALU.subtract), waits=[(s_fin, s_fin.n)], inc=s_fin)
        pr.op("scalar", lambda h: h.activation(
            ftmp[:, 0, :], ftmp[:, 1, :], AF.Ln, bias=epsv[:, 0:1]),
            waits=[(s_fin, s_fin.n)], inc=s_fns)
        pr.op("scalar", lambda h, idx=idx: h.activation(
            fin[:, 2 * idx + 1, :], ftmp[:, 0, :], AF.Exp, scale=-0.5),
            waits=[(s_fns, s_fns.n)], inc=s_fns)

    po1_rd = [0, 0]
    evdma = [0, 0]
    with contextlib.ExitStack() as c1ctx:
        wcat_sb = bufs["wcat_sb"]
        gbuf = [c1ctx.enter_context(nc.sbuf_tensor(f"c_gb{i}", [128, 64, 256], BF16))
                for i in range(2)]
        pr.op("gpsimd", lambda h: h.dma_start(
            out=wcat_sb[:, :, :],
            in_=io["wcat"].ap().rearrange("(c p) f -> p c f", p=128)),
            waits=list(after_w), inc=s_w, inck=16)
        W1_RDY = s_w.n
        for tg in range(NT // 2):
            gp = tg % 2
            gsrc = io["gath0"] if tg < 16 else io["gath1"]
            gcol = (tg % 16) * 256
            pr.op("sync", lambda h, gp=gp, gsrc=gsrc, gcol=gcol: h.dma_start(
                out=gbuf[gp][:, :, :],
                in_=gsrc[:, gcol:gcol + 256]
                    .rearrange("(c p) t -> p c t", p=128)),
                waits=[(s_mm1, max(0, 2 * tg - 2)),
                       (s_cc, cc_gate0 if tg < 16 else cc_gate1)]
                      + (list(after) if tg < 2 else []),
                inc=s_g, inck=16)
            G_RDY = s_g.n
            for ti in range(2):
                g = tg * 2 + ti
                par = g % 2
                def mm1(h, gp=gp, ti=ti, par=par):
                    for fc in range(64):
                        ins = h.matmul(po1[par][:, :],
                                       gbuf[gp][:, fc, ti * 128:(ti + 1) * 128],
                                       wcat_sb[:, fc, :],
                                       start=(fc == 0), stop=(fc == 63))
                    return ins
                pr.op("tensor", mm1,
                      waits=[(s_g, G_RDY), (s_w, W1_RDY), (s_ev, max(0, g - 1)),
                             (s_st, po1_rd[par])],
                      inc=s_mm1)
                MM = s_mm1.n
                if upto != "c1m":
                    pr.op("vector", lambda h, par=par, g=g: h.reduce_sum(
                        stats[:, 0, g:g + 1], po1[par][:, :], axis=AX.X),
                        waits=[(s_mm1, MM)], inc=s_st)
                    po1_rd[par] = s_st.n
                if upto not in ("c1m", "c1s"):
                    pr.op("scalar", lambda h, par=par, g=g: h.activation(
                        scr5[:, :], po1[par][:, :], AF.Square,
                        accum_out=stats[:, 1, g:g + 1]),
                        waits=[(s_mm1, MM), (s_st, po1_rd[par])], inc=s_sts)
                pr.op("scalar", lambda h, par=par: h.copy(
                    ev2[par][:, :], po1[par][:, :]),
                    waits=[(s_dma, evdma[par]), (s_mm1, MM)], inc=s_ev)
                pr.op("sync", lambda h, g=g, par=par: h.dma_start(
                    out=io["o1_d"][g * 128:(g + 1) * 128, :], in_=ev2[par][:, :]),
                    waits=[(s_ev, s_ev.n)], inc=s_dma, inck=16)
                evdma[par] = s_dma.n
    C1_MM = s_mm1.n
    if upto in ("c1", "c1m", "c1s"):
        pr.op("sync", None, waits=[(s_sts, s_sts.n), (s_st, s_st.n), (s_dma, s_dma.n)])
        return {}
    pr.op("sync", None, waits=[(s_sts, s_sts.n), (s_st, s_st.n), (s_dma, s_dma.n)],
          inc=s_dma, inck=1)
    stats_ar(0)
    FIN2 = s_fns.n
    if upto == "ar1":
        pr.op("sync", None, waits=[(s_fns, FIN2)])
        return {}

    x1_done = [0, 0]
    x1_st = [0, 0]
    for g in range(NT):
        par = g % 2
        pr.op("sync", lambda h, g=g, par=par: h.dma_start(
            out=xt[par][:, :], in_=io["x_sl"][g * 128:(g + 1) * 128, :]),
            waits=[(s_st, x1_st[par]), (s_dma, x1_done[par])], inc=s_x, inck=16)
        pr.op("sync", lambda h, g=g, par=par: h.dma_start(
            out=aux[par][:, :], in_=io["o1_d"][g * 128:(g + 1) * 128, :]),
            inc=s_x, inck=16)
        X_RDY = s_x.n
        pr.op("vector", lambda h, g=g, par=par: h.tensor_scalar(
            out=tmp1[par][:, :], in0=aux[par][:, :],
            scalar1=fin[:, 0, g:g + 1], scalar2=fin[:, 1, g:g + 1],
            op0=ALU.subtract, op1=ALU.mult),
            waits=[(s_x, X_RDY), (s_fns, FIN2)], inc=s_st)
        pr.op("vector", lambda h, par=par: h.tensor_tensor(
            out=aux[par][:, :], in0=tmp1[par][:, :], in1=xt[par][:, :], op=ALU.add),
            waits=[(s_st, s_st.n)], inc=s_st)
        X1_RDY = s_st.n
        pr.op("vector", lambda h, par=par, g=g: h.reduce_sum(
            stats[:, 2, g:g + 1], aux[par][:, :], axis=AX.X),
            waits=[(s_st, X1_RDY)], inc=s_st)
        x1_st[par] = s_st.n
        pr.op("scalar", lambda h, par=par, g=g: h.activation(
            scr5[:, :], aux[par][:, :], AF.Square, accum_out=stats[:, 3, g:g + 1]),
            waits=[(s_st, X1_RDY)], inc=s_sts)
        pr.op("sync", lambda h, g=g, par=par: h.dma_start(
            out=io["x1_d"][g * 128:(g + 1) * 128, :], in_=aux[par][:, :]),
            waits=[(s_sts, s_sts.n)], inc=s_dma, inck=16)
        x1_done[par] = s_dma.n
    if upto == "c2a":
        pr.op("sync", None, waits=[(s_sts, s_sts.n), (s_st, s_st.n), (s_dma, s_dma.n)])
        return {}
    pr.op("sync", None, waits=[(s_sts, s_sts.n), (s_st, s_st.n), (s_dma, s_dma.n)],
          inc=s_dma, inck=1)
    stats_ar(1)
    FIN3 = s_fns.n

    xf_dma = [0, 0]
    for g in range(NT):
        par = g % 2
        pr.op("sync", lambda h, g=g, par=par: h.dma_start(
            out=aux[par][:, :], in_=io["x1_d"][g * 128:(g + 1) * 128, :]),
            waits=[(s_st, x1_st[par]), (s_fns, FIN3)], inc=s_x, inck=16)
        X_RDY = s_x.n
        pr.op("vector", lambda h, g=g, par=par: h.tensor_scalar(
            out=scr5[:, :], in0=aux[par][:, :],
            scalar1=fin[:, 2, g:g + 1], scalar2=fin[:, 3, g:g + 1],
            op0=ALU.subtract, op1=ALU.mult),
            waits=[(s_x, X_RDY), (s_fns, FIN3), (s_tp, max(0, s_tp.n))], inc=s_st)
        x1_st[par] = s_st.n
        XF_RDY = s_st.n
        def tpf(h, g=g):
            for j in range(4):
                ins = h.transpose(ptb[g % 2][:, j * 128:(j + 1) * 128],
                                  scr5[:, j * 128:(j + 1) * 128], ident_bf[:, :])
            return ins
        pr.op("tensor", tpf, waits=[(s_st, XF_RDY), (s_w, ID_RDY),
                                    (s_tpe, max(0, g - 1))], inc=s_tp)
        pr.op("vector", lambda h, g=g: h.tensor_copy(
            xfst[g % 2][:, :, :].rearrange("p c t -> p (c t)"),
            ptb[g % 2][:, 0:512]),
            waits=[(s_tp, s_tp.n), (s_dma, xf_dma[g % 2])], inc=s_tpe)
        pr.op("sync", lambda h, g=g: h.dma_start(
            out=io["xfc_d"][:, g * 128:(g + 1) * 128]
                .rearrange("(c p) t -> p c t", p=128),
            in_=xfst[g % 2][:, :, :]),
            waits=[(s_tpe, s_tpe.n)], inc=s_dma, inck=16)
        xf_dma[g % 2] = s_dma.n
    if upto == "c2b":
        pr.op("sync", None, waits=[(s_dma, s_dma.n)])
        return {}
    pr.op("gpsimd", lambda h: h.collective_compute(
        "AllGather", ALU.bypass, replica_groups=[list(range(8))],
        ins=[io["xfc_d"].ap().opt()], outs=[io["xfg_d"].ap().opt()]),
        waits=[(s_dma, s_dma.n)], inc=s_cc)
    AG2 = s_cc.n

    pf_rd = [0, 0]
    ev3 = [0, 0]
    with contextlib.ExitStack() as c3ctx:
        wb_sb = c3ctx.enter_context(nc.sbuf_tensor("c_wb", [128, ND, 512], BF16))
        ag_sb = c3ctx.enter_context(nc.sbuf_tensor("c_ag", [128, ND, 72], BF16))
        b_sb = c3ctx.enter_context(nc.sbuf_tensor("c_b", [64, 512], BF16))
        xfb = [c3ctx.enter_context(nc.sbuf_tensor(f"c_xfb{i}", [128, ND, 256], BF16))
               for i in range(2)]
        pr.op("gpsimd", lambda h: h.dma_start(
            out=wb_sb[:, :, :],
            in_=io["WbT_c"].ap().rearrange("(c p) f -> p c f", p=128)),
            waits=[(s_mm1, C1_MM)], inc=s_w, inck=16)
        pr.op("gpsimd", lambda h: h.dma_start(
            out=ag_sb[:, :, :],
            in_=io["AG_cat"].ap().rearrange("(c p) f -> p c f", p=128)),
            inc=s_w, inck=16)
        pr.op("gpsimd", lambda h: h.dma_start(
            out=b_sb[0:64, :], in_=io["B_c"].ap()), inc=s_w, inck=16)
        W3_RDY = s_w.n
        for tg in range(NT // 2):
            gp = tg % 2
            pr.op("sync", lambda h, tg=tg, gp=gp: h.dma_start(
                out=xfb[gp][:, :, :],
                in_=io["xfg_d"][:, tg * 256:(tg + 1) * 256]
                    .rearrange("(c p) t -> p c t", p=128)),
                waits=[(s_cc, AG2), (s_mm3, max(0, 2 * (2 * tg - 2)))],
                inc=s_g, inck=16)
            G_RDY = s_g.n
            for ti in range(2):
                g = tg * 2 + ti
                par = g % 2
                def mmf(h, gp=gp, ti=ti, par=par):
                    for dc in range(ND):
                        h.matmul(pf[par][:, :],
                                 xfb[gp][:, dc, ti * 128:(ti + 1) * 128],
                                 wb_sb[:, dc, :],
                                 start=(dc == 0), stop=False)
                        ins = h.matmul(phg[par][:, 0:72],
                                       xfb[gp][:, dc, ti * 128:(ti + 1) * 128],
                                       ag_sb[:, dc, :],
                                       start=(dc == 0), stop=(dc == ND - 1))
                    return ins
                pr.op("tensor", mmf,
                      waits=[(s_g, G_RDY), (s_w, W3_RDY), (s_hg, max(0, g - 1)),
                             (s_ev, ev3[par]), (s_st, pf_rd[par])],
                      inc=s_mm3)
                MM = s_mm3.n
                pr.op("vector", lambda h, par=par: h.tensor_copy(
                    hg[:, :], phg[par][:, 0:72]), waits=[(s_mm3, MM)], inc=s_hg)
                HG = s_hg.n
                pr.op("vector", lambda h: h.reduce_max(
                    t2b[:, 0:1], hg[:, 64:72], axis=AX.X),
                    waits=[(s_hg, HG), (s_t2, s_t2.n)], inc=s_t2)
                pr.op("vector", lambda h: h.tensor_scalar(
                    out=t2b[:, 1:2], in0=t2b[:, 0:1], scalar1=-1.0, scalar2=None,
                    op0=ALU.mult, op1=ALU.bypass),
                    waits=[(s_t2, s_t2.n)], inc=s_t2)
                pr.op("scalar", lambda h: h.activation(
                    t2b[:, 8:16], hg[:, 64:72], AF.Exp, bias=t2b[:, 1:2]),
                    waits=[(s_t2, s_t2.n)], inc=s_t2)
                pr.op("vector", lambda h: h.reduce_max(
                    t2b[:, 2:3], t2b[:, 8:16], axis=AX.X),
                    waits=[(s_t2, s_t2.n)], inc=s_t2)
                pr.op("vector", lambda h: h.tensor_scalar(
                    out=t2b[:, 16:24], in0=t2b[:, 8:16], scalar1=t2b[:, 2:3],
                    scalar2=None, op0=ALU.is_ge, op1=ALU.bypass),
                    waits=[(s_t2, s_t2.n)], inc=s_t2)
                pr.op("vector", lambda h: h.tensor_scalar(
                    out=t2b[:, 24:32], in0=t2b[:, 16:24], scalar1=2.0, scalar2=None,
                    op0=ALU.mult, op1=ALU.bypass),
                    waits=[(s_t2, s_t2.n)], inc=s_t2)
                pr.op("vector", lambda h: h.tensor_tensor(
                    out=t2b[:, 24:32], in0=t2b[:, 8:16], in1=t2b[:, 24:32],
                    op=ALU.subtract), waits=[(s_t2, s_t2.n)], inc=s_t2)
                pr.op("vector", lambda h: h.reduce_max(
                    t2b[:, 3:4], t2b[:, 24:32], axis=AX.X),
                    waits=[(s_t2, s_t2.n)], inc=s_t2)
                pr.op("vector", lambda h: h.tensor_scalar(
                    out=t2b[:, 32:40], in0=t2b[:, 24:32], scalar1=t2b[:, 3:4],
                    scalar2=None, op0=ALU.is_ge, op1=ALU.bypass),
                    waits=[(s_t2, s_t2.n)], inc=s_t2)
                pr.op("vector", lambda h: h.tensor_tensor(
                    out=t2b[:, 16:24], in0=t2b[:, 16:24], in1=t2b[:, 32:40],
                    op=ALU.add), waits=[(s_t2, s_t2.n)], inc=s_t2)
                pr.op("vector", lambda h: h.tensor_tensor(
                    out=t2b[:, 4:5], in0=t2b[:, 2:3], in1=t2b[:, 3:4],
                    op=ALU.add), waits=[(s_t2, s_t2.n)], inc=s_t2)
                pr.op("vector", lambda h: h.reciprocal(
                    t2b[:, 5:6], t2b[:, 4:5]), waits=[(s_t2, s_t2.n)], inc=s_t2)
                pr.op("vector", lambda h: h.tensor_tensor(
                    out=t2b[:, 8:16], in0=t2b[:, 8:16], in1=t2b[:, 16:24],
                    op=ALU.mult), waits=[(s_t2, s_t2.n)], inc=s_t2)
                pr.op("vector", lambda h: h.tensor_scalar(
                    out=wd[:, 0:8], in0=t2b[:, 8:16], scalar1=t2b[:, 5:6],
                    scalar2=2.0, op0=ALU.mult, op1=ALU.mult),
                    waits=[(s_t2, s_t2.n)], inc=s_t2)
                for e in range(8):
                    pr.op("vector", lambda h, e=e: h.tensor_scalar(
                        out=hsc[:, e * 8:(e + 1) * 8], in0=hg[:, e * 8:(e + 1) * 8],
                        scalar1=wd[:, e:e + 1], scalar2=None,
                        op0=ALU.mult, op1=ALU.bypass),
                        waits=[(s_t2, s_t2.n)], inc=s_t2)
                pr.op("tensor", lambda h, g=g: h.transpose(
                    ptb[g % 2][0:64, 0:128], hsc[:, :], ident_bf[:, :]),
                    waits=[(s_t2, s_t2.n), (s_tpe, s_tpe.n), (s_w, ID_RDY)],
                    inc=s_tp)
                pr.op("vector", lambda h, g=g: h.tensor_copy(
                    hscT[0:64, :], ptb[g % 2][0:64, 0:128]),
                    waits=[(s_tp, s_tp.n)], inc=s_tpe)
                pr.op("tensor", lambda h, par=par: h.matmul(
                    pf[par][:, :], hscT[0:64, :], b_sb[0:64, :],
                    start=False, stop=True),
                    waits=[(s_tpe, s_tpe.n)], inc=s_mm3)
                MM2 = s_mm3.n
                pr.op("vector", lambda h, par=par, g=g: h.reduce_sum(
                    stats[:, 4, g:g + 1], pf[par][:, :], axis=AX.X),
                    waits=[(s_mm3, MM2)], inc=s_st)
                pf_rd[par] = s_st.n
                pr.op("scalar", lambda h, par=par, g=g: h.activation(
                    scr5[:, :], pf[par][:, :], AF.Square,
                    accum_out=stats[:, 5, g:g + 1]),
                    waits=[(s_mm3, MM2), (s_st, pf_rd[par])], inc=s_sts)
                pr.op("scalar", lambda h, par=par: h.copy(
                    ev2[par][:, :], pf[par][:, :]),
                    waits=[(s_dma, evdma[par])], inc=s_ev)
                ev3[par] = s_ev.n
                pr.op("sync", lambda h, g=g, par=par: h.dma_start(
                    out=io["ffn_d"][g * 128:(g + 1) * 128, :], in_=ev2[par][:, :]),
                    waits=[(s_ev, s_ev.n)], inc=s_dma, inck=16)
                evdma[par] = s_dma.n
    if upto == "c3":
        pr.op("sync", None, waits=[(s_sts, s_sts.n), (s_st, s_st.n), (s_dma, s_dma.n)])
        return {}
    pr.op("sync", None, waits=[(s_sts, s_sts.n), (s_st, s_st.n), (s_dma, s_dma.n)],
          inc=s_dma, inck=1)
    stats_ar(2)
    FIN4 = s_fns.n

    c4_st = [0, 0]
    for g in range(NT):
        par = g % 2
        pr.op("sync", lambda h, g=g, par=par: h.dma_start(
            out=ev2[par][:, :], in_=io["x1_d"][g * 128:(g + 1) * 128, :]),
            waits=[(s_out, 16 * max(0, g - 1)), (s_st, c4_st[par]), (s_fns, FIN4),
                   (s_dma, evdma[par])],
            inc=s_x, inck=16)
        pr.op("sync", lambda h, g=g, par=par: h.dma_start(
            out=aux[par][:, :], in_=io["ffn_d"][g * 128:(g + 1) * 128, :]),
            inc=s_x, inck=16)
        X_RDY = s_x.n
        pr.op("vector", lambda h, g=g, par=par: h.tensor_scalar(
            out=tmp1[par][:, :], in0=aux[par][:, :],
            scalar1=fin[:, 4, g:g + 1], scalar2=fin[:, 5, g:g + 1],
            op0=ALU.subtract, op1=ALU.mult),
            waits=[(s_x, X_RDY), (s_fns, FIN4)], inc=s_st)
        pr.op("vector", lambda h, g=g, par=par: h.tensor_tensor(
            out=outt[par][:, :], in0=tmp1[par][:, :], in1=ev2[par][:, :], op=ALU.add),
            waits=[(s_st, s_st.n)], inc=s_st)
        c4_st[par] = s_st.n
        pr.op("sync", lambda h, g=g, par=par: h.dma_start(
            out=io["y_sl"][g * 128:(g + 1) * 128, :], in_=outt[par][:, :]),
            waits=[(s_st, s_st.n)], inc=s_out, inck=16)
    pr.op("sync", None, waits=[(s_out, s_out.n)])
    return {}


def build_c_debug(upto=None):
    nc = bass.Bass("TRN2", target_bir_lowering=False)
    io = declare_c_inputs(nc, gathered_external=True)
    with contextlib.ExitStack() as ctx:
        sm = make_c_sems(ctx, nc)
        pr = Prog(nc)
        emit_phase_c(pr, ctx, io, sm, upto=upto, cc_gate0=0, cc_gate1=0)
        block = ctx.enter_context(nc.Block())
        pr.replay(block)
    return nc


def host_phase_c_inputs(c, inp, gathered=None):
    Fc = slice(512 * c, 512 * c + 512)
    w_o = np.asarray(inp["L0_attn_w_o"], dtype=np.float32)
    w_lo = np.asarray(inp["L0_attn_w_lo"], dtype=np.float32)
    rows = []
    for r in range(8):
        fr = slice(4 * r * 128, (4 * r + 4) * 128)
        rows.append(w_o[fr, Fc])
        rows.append(w_lo[fr, Fc])
    wcat = np.ascontiguousarray(np.concatenate(rows, axis=0))
    A = np.asarray(inp["L0_ffn_A"], dtype=np.float32).reshape(64, D).T
    gw = np.asarray(inp["L0_ffn_gate_W"], dtype=np.float32).T
    Bm = np.asarray(inp["L0_ffn_B"], dtype=np.float32)
    B_perm = np.ascontiguousarray(Bm.transpose(0, 2, 1).reshape(64, D))
    out = {
        "wcat": wcat,
        "x_sl": np.ascontiguousarray(
            np.asarray(inp["L0_attn_hidden_states"], dtype=np.float32)
            .reshape(T, D)[:, Fc]),
        "WbT_c": np.ascontiguousarray(
            np.asarray(inp["L0_ffn_W_base"], dtype=np.float32).T[:, Fc]),
        "AG_cat": np.ascontiguousarray(np.concatenate([A, gw], axis=1)),
        "B_c": np.ascontiguousarray(B_perm[:, Fc]),
    }
    if gathered is not None:
        out["gath0"] = np.ascontiguousarray(gathered[:, 0:S])
        out["gath1"] = np.ascontiguousarray(gathered[:, S:])
    return out


def build_full():
    nc = bass.Bass("TRN2", target_bir_lowering=False)
    io = declare_p_inputs(nc)
    io.update(declare_p_scratch(nc, external=False))
    io.update(declare_a_out(nc, external=False))
    io.update(declare_c_inputs(nc, gathered_external=False))
    with contextlib.ExitStack() as ctx:
        sm = make_p_sems(ctx, nc)
        sm.update(make_a_sems(ctx, nc))
        sm.update(make_c_sems(ctx, nc))
        pr = Prog(nc)
        with contextlib.ExitStack() as pctx:
            emit_phase_p(pr, pctx, io, sm)
        gate_a = [(sm["p_out"], sm["p_out"].n)]
        cbufs = alloc_c_sbuf(nc, ctx)
        s_cc = sm["c_cc"]
        holder = {}

        def post_b0():
            pr.op("gpsimd", lambda h: h.collective_compute(
                "AllGather", ALU.bypass, replica_groups=[list(range(8))],
                ins=[io["contrib0"].ap().opt()], outs=[io["gath0"].ap().opt()]),
                waits=[(sm["a_out"], sm["a_out_b0"])], inc=s_cc)
            holder["cg0"] = s_cc.n
        with contextlib.ExitStack() as actx:
            emit_phase_a(pr, actx, io, sm, after=gate_a, post_b0=post_b0)
        cg0 = holder["cg0"]
        pr.op("gpsimd", lambda h: h.collective_compute(
            "AllGather", ALU.bypass, replica_groups=[list(range(8))],
            ins=[io["contrib1"].ap().opt()], outs=[io["gath1"].ap().opt()]),
            waits=[(sm["a_out"], sm["a_out"].n)], inc=s_cc)
        cg1 = s_cc.n
        a_done = [(sm["a_qk"], sm["a_qk"].n), (sm["a_pv"], sm["a_pv"].n),
                  (sm["a_out"], sm["a_out"].n)]
        emit_phase_c(pr, ctx, io, sm, after=a_done, bufs=cbufs,
                     after_w=gate_a, cc_gate0=cg0, cc_gate1=cg1)
        block = ctx.enter_context(nc.Block())
        pr.replay(block)
    return nc


_NC_CACHE = {}


def kernel(**inputs):
    if "full" not in _NC_CACHE:
        _NC_CACHE["full"] = build_full()
    nc = _NC_CACHE["full"]
    in_maps = []
    for c in range(8):
        m = host_phase_p_inputs(c, inputs)
        m.update(host_phase_c_inputs(c, inputs))
        in_maps.append(m)
    res = run_bass_kernel_spmd(nc, in_maps, core_ids=list(range(8)))
    y = np.concatenate([res.results[c]["y_sl"] for c in range(8)], axis=1)
    return np.ascontiguousarray(y.reshape(B, S, D))


def kernel_unfused(**inputs):
    nc1 = build_pa_debug()
    in1 = [host_phase_p_inputs(c, inputs) for c in range(8)]
    res1 = run_bass_kernel_spmd(nc1, in1, core_ids=list(range(8)))
    gathered = np.concatenate(
        [np.concatenate([res1.results[c][f"contrib{i}"] for c in range(8)], axis=0)
         for i in range(2)], axis=1)
    nc2 = build_c_debug()
    in2 = [host_phase_c_inputs(c, inputs, gathered) for c in range(8)]
    res2 = run_bass_kernel_spmd(nc2, in2, core_ids=list(range(8)))
    y = np.concatenate([res2.results[c]["y_sl"] for c in range(8)], axis=1)
    return y.reshape(B, S, D)


# version 15
# speedup vs baseline: 1.1214x; 1.0123x over previous
import contextlib
import math

import numpy as np
import ml_dtypes
import concourse.bass as bass
import concourse.mybir as mybir
from concourse.bass_utils import run_bass_kernel_spmd

F32 = mybir.dt.float32
BF16 = mybir.dt.bfloat16
AF = mybir.ActivationFunctionType
ALU = mybir.AluOpType
AX = mybir.AxisListType

B, S, D = 2, 4096, 4096
T = B * S
HD = 128
SCALE = 1.0 / math.sqrt(HD)
EPS = 1e-5
THETA = 10000.0
NT = T // 128
ND = D // 128


class Sem:
    def __init__(self, stack, nc, name):
        self.h = stack.enter_context(nc.semaphore(name))
        self.n = 0

    def add(self, k):
        self.n += k
        return self.n


class Prog:
    ENGS = ("sync", "gpsimd", "tensor", "vector", "scalar")

    def __init__(self, nc):
        self.nc = nc
        self.q = {e: [] for e in self.ENGS}
        self.last_wait = {e: {} for e in self.ENGS}

    def op(self, eng, fn, waits=(), inc=None, inck=1):
        lw = self.last_wait[eng]
        wlist = []
        for s, v in waits:
            if v <= 0 or lw.get(id(s), 0) >= v:
                continue
            lw[id(s)] = v
            wlist.append((s, v))
        if inc is not None:
            inc.add(inck)
        self.q[eng].append((tuple(wlist), fn, inc, inck))

    def replay(self, block):
        for e in self.ENGS:
            if not self.q[e]:
                continue
            qq = self.q[e]

            def body(h, qq=qq):
                for waits, fn, inc, inck in qq:
                    for s, v in waits:
                        h.wait_ge(s.h, v)
                    if fn is None:
                        if inc is not None:
                            h.nop().then_inc(inc.h, inck)
                        continue
                    ins = fn(h)
                    if inc is not None:
                        ins.then_inc(inc.h, inck)
            getattr(block, e)(body)


def _f64_rope():
    inv = 1.0 / THETA ** (np.arange(0, HD, 2, dtype=np.float64) / HD)
    t = np.arange(S, dtype=np.float64)
    fr = np.outer(t, inv)
    emb = np.concatenate([fr, fr], axis=1)
    cos, sin = np.cos(emb), np.sin(emb)
    sinS = sin.copy()
    sinS[:, :64] *= -1.0
    return cos, sinS


def host_rope_table():
    cos, sinS = _f64_rope()
    cos2, sinS2 = np.tile(cos, (B, 1)), np.tile(sinS, (B, 1))
    tab = np.concatenate([cos2, sinS2, cos2 * SCALE, sinS2 * SCALE], axis=1)
    return np.ascontiguousarray(tab).astype(ml_dtypes.bfloat16)


def declare_p_inputs(nc):
    io = {}
    io["x_full"] = nc.dram_tensor("x_full", [T, D], F32, kind="ExternalInput")
    io["wP"] = nc.dram_tensor("wP", [D, 1280], F32, kind="ExternalInput")
    io["w_lqT"] = nc.dram_tensor("w_lqT", [1024, D], F32, kind="ExternalInput")
    io["w_lk_c"] = nc.dram_tensor("w_lk_c", [1024, 512], F32, kind="ExternalInput")
    io["ropeT"] = nc.dram_tensor("ropeT", [T, 512], BF16, kind="ExternalInput")
    return io


def declare_p_scratch(nc, external):
    kw = {"kind": "ExternalOutput"} if external else {}
    io = {}
    io["qT_d"] = nc.dram_tensor("qT_d", [4, 128, T], BF16, **kw)
    io["kT_d"] = nc.dram_tensor("kT_d", [128, T], BF16, **kw)
    io["lkT_d"] = nc.dram_tensor("lkT_d", [4, 128, T], BF16, **kw)
    io["v_d"] = nc.dram_tensor("v_d", [T, 128], BF16, **kw)
    io["lv_d"] = nc.dram_tensor("lv_d", [4, T, 128], BF16, **kw)
    io["norm_d"] = nc.dram_tensor("norm_d", [9, T], BF16, **kw)
    return io


def emit_phase_p(pr, ctx, io, sm):
    nc = pr.nc
    E = ctx.enter_context

    wlkc = E(nc.sbuf_tensor("p_wlkc", [128, ND, 512], BF16))
    with contextlib.ExitStack() as tmp:
        wlqT_sb = tmp.enter_context(nc.sbuf_tensor("p_wlqT", [128, 8, D], BF16))
        wlk_sb = tmp.enter_context(nc.sbuf_tensor("p_wlk", [128, 8, 512], BF16))
        plkc0 = tmp.enter_context(nc.psum_tensor("p_plkc0", [128, 512], F32))
        s_w = sm["p_w"]
        s_pre = sm["p_pre"]
        pr.op("gpsimd", lambda h: h.dma_start(
            out=wlqT_sb[:, :, :],
            in_=io["w_lqT"].ap().rearrange("(c p) d -> p c d", p=128)),
            inc=s_w, inck=16)
        pr.op("gpsimd", lambda h: h.dma_start(
            out=wlk_sb[:, :, :],
            in_=io["w_lk_c"].ap().rearrange("(c p) f -> p c f", p=128)),
            inc=s_w, inck=16)
        for dc in range(ND):
            def mmf(h, dc=dc):
                for lc in range(8):
                    ins = h.matmul(plkc0[:, :], wlqT_sb[:, lc, dc * 128:(dc + 1) * 128],
                                   wlk_sb[:, lc, :], start=(lc == 0), stop=(lc == 7))
                return ins
            pr.op("tensor", mmf, waits=[(s_w, 32), (s_pre, 2 * dc)], inc=s_pre)
            pr.op("vector", lambda h, dc=dc: h.tensor_copy(wlkc[:, dc, :], plkc0[:, :]),
                  waits=[(s_pre, 2 * dc + 1)], inc=s_pre)

    xs = [E(nc.sbuf_tensor(f"p_x{i}", [128, D], BF16)) for i in range(2)]
    xn = [E(nc.sbuf_tensor(f"p_xn{i}", [128, D], BF16)) for i in range(2)]
    st = [E(nc.sbuf_tensor(f"p_st{i}", [128, 8], F32)) for i in range(2)]
    xnt = [E(nc.sbuf_tensor(f"p_xnt{i}", [128, ND, 128], BF16)) for i in range(2)]
    ws = E(nc.sbuf_tensor("p_ws", [128, ND, 1280], BF16))
    rtab = [E(nc.sbuf_tensor(f"p_rtab{i}", [128, 4, 128], BF16)) for i in range(2)]
    qkl = E(nc.sbuf_tensor("p_qkl", [128, 9, 128], BF16))
    ropeb = E(nc.sbuf_tensor("p_ropeb", [128, 9, 128], BF16))
    ropet = E(nc.sbuf_tensor("p_ropet", [128, 9, 128], BF16))
    vst = E(nc.sbuf_tensor("p_vst", [128, 4, 5, 128], BF16))
    tst = E(nc.sbuf_tensor("p_tst", [128, 4, 9, 128], BF16))
    sqscr = E(nc.sbuf_tensor("p_sqscr", [128, 128], BF16))
    qn2 = E(nc.sbuf_tensor("p_qn2", [128, NT, 9], F32))
    nrows = E(nc.sbuf_tensor("p_nrows", [9, T], BF16))
    epsv = E(nc.sbuf_tensor("p_epsv", [128, 1], F32))
    ones_bf = E(nc.sbuf_tensor("p_onesb", [128, 128], BF16))
    ident_bf = E(nc.sbuf_tensor("p_identb", [128, 128], BF16))
    ones_f = E(nc.sbuf_tensor("p_onesf", [128, 128], F32))
    ident_f = E(nc.sbuf_tensor("p_identf", [128, 128], F32))
    pq = E(nc.psum_tensor("p_pq", [128, 512], F32))
    pkv = E(nc.psum_tensor("p_pkv", [128, 512], F32))
    plkc = E(nc.psum_tensor("p_plkc", [128, 512], F32))
    plv = E(nc.psum_tensor("p_plv", [128, 512], F32))
    ptb = [E(nc.psum_tensor(f"p_ptb{i}", [128, 1024], BF16)) for i in range(2)]
    pnt = [E(nc.psum_tensor(f"p_pnt{i}", [128, 512], F32)) for i in range(2)]

    s_w = sm["p_w"]; s_pre = sm["p_pre"]
    s_x = sm["p_x"]; s_rt = sm["p_rt"]
    s_ln = sm["p_ln"]; s_lns = sm["p_lns"]; s_xn = sm["p_xn"]
    s_tp = sm["p_tp"]; s_tpe = sm["p_tpe"]
    s_mm = sm["p_mm"]; s_ev = sm["p_ev"]
    s_ro = sm["p_ro"]; s_sc = sm["p_sc"]
    s_out = sm["p_out"]; s_nt = sm["p_nt"]

    pr.op("gpsimd", lambda h: h.memset(epsv[:, :], EPS), waits=[(sm["p_pre"], 64)])
    pr.op("gpsimd", lambda h: h.memset(ones_bf[:, :], 1.0))
    pr.op("gpsimd", lambda h: h.memset(ones_f[:, :], 1.0))
    pr.op("gpsimd", lambda h: h.affine_select(
        out=ident_bf[:, :], in_=ones_bf[:, :], pattern=[[1, 128]],
        compare_op=ALU.is_equal, fill=0.0, base=0, channel_multiplier=-1),
        inc=s_w)
    pr.op("gpsimd", lambda h: h.affine_select(
        out=ident_f[:, :], in_=ones_f[:, :], pattern=[[1, 128]],
        compare_op=ALU.is_equal, fill=0.0, base=0, channel_multiplier=-1),
        inc=s_w)
    pr.op("gpsimd", lambda h: h.dma_start(
        out=ws[:, :, :], in_=io["wP"].ap().rearrange("(c p) f -> p c f", p=128)),
        inc=s_w, inck=16)
    W_RDY = s_w.n

    TPg = 6
    ROg = 7
    EVg = 5
    SCg = 9
    LNg = 4
    LSg = 3

    for g in range(NT):
        par = g % 2
        strip, tt = g // 4, g % 4
        pr.op("gpsimd", lambda h, g=g, par=par: h.dma_start(
            out=xs[par][:, :], in_=io["x_full"][g * 128:(g + 1) * 128, :]),
            waits=[(s_xn, g - 1)], inc=s_x, inck=16)
        pr.op("sync", lambda h, g=g, par=par: h.dma_start(
            out=rtab[par][:, :, :],
            in_=io["ropeT"][g * 128:(g + 1) * 128, :]
                .rearrange("p (f d) -> p f d", d=128)),
            waits=[(s_ro, (g - 1) * ROg)] + ([(s_pre, 64)] if g < 2 else []),
            inc=s_rt, inck=16)
        pr.op("vector", lambda h, par=par: h.reduce_sum(
            st[par][:, 0:1], xs[par][:, :], axis=AX.X),
            waits=[(s_x, (g + 1) * 16), (s_xn, g - 1)], inc=s_ln)
        pr.op("scalar", lambda h, par=par: h.activation(
            xn[par][:, :], xs[par][:, :], AF.Square, accum_out=st[par][:, 1:2]),
            waits=[(s_x, (g + 1) * 16), (s_tp, ((g - 2) * TPg + 4) if g >= 2 else 0)],
            inc=s_lns)
        pr.op("vector", lambda h, par=par: h.tensor_scalar(
            out=st[par][:, 2:3], in0=st[par][:, 0:1], scalar1=1.0 / D, scalar2=None,
            op0=ALU.mult, op1=ALU.bypass), waits=[(s_ln, s_ln.n)], inc=s_ln)
        pr.op("vector", lambda h, par=par: h.tensor_scalar(
            out=st[par][:, 3:4], in0=st[par][:, 2:3], scalar1=st[par][:, 2:3],
            scalar2=None, op0=ALU.mult, op1=ALU.bypass),
            waits=[(s_ln, s_ln.n)], inc=s_ln)
        pr.op("vector", lambda h, par=par: h.tensor_scalar(
            out=st[par][:, 4:5], in0=st[par][:, 1:2], scalar1=1.0 / D,
            scalar2=st[par][:, 3:4], op0=ALU.mult, op1=ALU.subtract),
            waits=[(s_ln, s_ln.n), (s_lns, s_lns.n)], inc=s_ln)
        pr.op("scalar", lambda h, par=par: h.activation(
            st[par][:, 5:6], st[par][:, 4:5], AF.Ln, bias=epsv[:, 0:1]),
            waits=[(s_ln, s_ln.n)], inc=s_lns)
        pr.op("scalar", lambda h, par=par: h.activation(
            st[par][:, 6:7], st[par][:, 5:6], AF.Exp, scale=-0.5),
            waits=[(s_lns, s_lns.n)], inc=s_lns)
        pr.op("vector", lambda h, par=par: h.tensor_scalar(
            out=st[par][:, 7:8], in0=st[par][:, 2:3], scalar1=st[par][:, 6:7],
            scalar2=-1.0, op0=ALU.mult, op1=ALU.mult),
            waits=[(s_lns, s_lns.n)], inc=s_ln)
        pr.op("scalar", lambda h, par=par: h.activation(
            xn[par][:, :], xs[par][:, :], AF.Identity,
            bias=st[par][:, 7:8], scale=st[par][:, 6:7]),
            waits=[(s_ln, s_ln.n)], inc=s_xn)
        for j in range(4):
            G = g * TPg + j
            def tpf(h, j=j, par=par):
                for k in range(8):
                    dc = j * 8 + k
                    ins = h.transpose(ptb[(g * TPg + j) % 2][:, k * 128:(k + 1) * 128],
                                      xn[par][:, dc * 128:(dc + 1) * 128],
                                      ident_bf[:, :])
                return ins
            pr.op("tensor", tpf,
                  waits=[(s_xn, g + 1), (s_w, W_RDY), (s_tpe, G - 1)]
                        + ([(s_pre, 64)] if g == 0 else []), inc=s_tp)
            pr.op("vector", lambda h, j=j, par=par, G=G: h.tensor_copy(
                xnt[par][:, j * 8:(j + 1) * 8, :].rearrange("p c d -> p (c d)"),
                ptb[G % 2][:, 0:1024]),
                waits=[(s_tp, G + 1)], inc=s_tpe)
        segs = [(pq, 0, 512), (pkv, 512, 768), (None, 0, 512), (plv, 768, 1280)]
        for si, (pslot, lo, hi) in enumerate(segs):
            def mmf(h, si=si, pslot=pslot, lo=lo, hi=hi, par=par):
                for dc in range(ND):
                    if si == 2:
                        ins = h.matmul(plkc[:, 0:512], xnt[par][:, dc, :],
                                       wlkc[:, dc, :], start=(dc == 0),
                                       stop=(dc == ND - 1))
                    else:
                        ins = h.matmul(pslot[:, 0:hi - lo], xnt[par][:, dc, :],
                                       ws[:, dc, lo:hi], start=(dc == 0),
                                       stop=(dc == ND - 1))
                return ins
            pr.op("tensor", mmf,
                  waits=[(s_tpe, g * TPg + 4), (s_ev, g * EVg), (s_pre, 64)],
                  inc=s_mm)
        M0 = g * 4
        evw = [(s_ro, (g - 1) * ROg + 6), (s_out, (strip - 1) * 14 * 16)]
        pr.op("scalar", lambda h: h.copy(
            qkl[:, 0:4, :].rearrange("p h d -> p (h d)"), pq[:, 0:512]),
            waits=[(s_mm, M0 + 1)] + evw, inc=s_ev)
        pr.op("scalar", lambda h: h.copy(qkl[:, 4, :], pkv[:, 0:128]),
              waits=[(s_mm, M0 + 2)], inc=s_ev)
        pr.op("scalar", lambda h, tt=tt: h.copy(vst[:, tt, 0, :], pkv[:, 128:256]),
              inc=s_ev)
        pr.op("scalar", lambda h: h.copy(
            qkl[:, 5:9, :].rearrange("p h d -> p (h d)"), plkc[:, 0:512]),
            waits=[(s_mm, M0 + 3)], inc=s_ev)
        pr.op("scalar", lambda h, tt=tt: h.copy(
            vst[:, tt, 1:5, :].rearrange("p h d -> p (h d)"), plv[:, 0:512]),
            waits=[(s_mm, M0 + 4)], inc=s_ev)
        RO0 = g * ROg
        groups = [(0, 4, 0), (4, 1, 2), (5, 4, 2)]
        for (h0, nh, tf) in groups:
            pr.op("vector", lambda h, h0=h0, nh=nh, tf=tf, par=par: h.tensor_tensor(
                ropeb[:, h0:h0 + nh, :], qkl[:, h0:h0 + nh, :],
                bass.AP(rtab[par], tf * 128, [[4 * 128, 128], [0, nh], [1, 128]]),
                op=ALU.mult),
                waits=[(s_ev, g * EVg + EVg), (s_rt, (g + 1) * 16),
                       (s_tp, g * TPg),
                       (s_sc, g * SCg)],
                inc=s_ro)
        for (h0, nh, tf) in groups:
            pr.op("vector", lambda h, h0=h0, nh=nh, tf=tf, par=par: h.tensor_tensor(
                bass.AP(ropet, h0 * 128, [[9 * 128, 128], [128, nh], [64, 2], [1, 64]]),
                bass.AP(qkl, h0 * 128 + 64, [[9 * 128, 128], [128, nh], [-64, 2], [1, 64]]),
                bass.AP(rtab[par], (tf + 1) * 128, [[4 * 128, 128], [0, nh], [64, 2], [1, 64]]),
                op=ALU.mult), inc=s_ro)
        pr.op("vector", lambda h: h.tensor_tensor(
            ropeb[:, :, :], ropeb[:, :, :], ropet[:, :, :], op=ALU.add),
            waits=[(s_ro, RO0 + 6)], inc=s_ro)
        for hh in range(9):
            pr.op("scalar", lambda h, hh=hh, g=g: h.activation(
                sqscr[:, :], ropeb[:, hh, :], AF.Square,
                accum_out=qn2[:, g, hh:hh + 1]),
                waits=[(s_ro, RO0 + ROg)], inc=s_sc)
        for j, (o0, cnt) in enumerate([(0, 8), (8, 1)]):
            G = g * TPg + 4 + j
            def tpq(h, o0=o0, cnt=cnt, G=G):
                for k in range(cnt):
                    ins = h.transpose(ptb[G % 2][:, k * 128:(k + 1) * 128],
                                      ropeb[:, o0 + k, :], ident_bf[:, :])
                return ins
            pr.op("tensor", tpq, waits=[(s_ro, RO0 + ROg), (s_tpe, G - 1)], inc=s_tp)
            pr.op("vector", lambda h, o0=o0, cnt=cnt, G=G, tt=tt: h.tensor_copy(
                tst[:, tt, o0:o0 + cnt, :].rearrange("p h d -> p (h d)"),
                ptb[G % 2][:, 0:cnt * 128]),
                waits=[(s_tp, G + 1), (s_out, (strip - 1) * 14 * 16)], inc=s_tpe)
        if tt == 3:
            OUTW = [(s_tpe, (g + 1) * TPg), (s_ev, (g + 1) * EVg)]
            for hh in range(4):
                pr.op("sync", lambda h, s=strip, hh=hh: h.dma_start(
                    out=io["qT_d"].ap()[hh, :, s * 512:(s + 1) * 512]
                        .rearrange("p (q d) -> p q d", q=4),
                    in_=bass.AP(tst, hh * 128,
                                [[4 * 9 * 128, 128], [9 * 128, 4], [1, 128]])),
                    waits=OUTW, inc=s_out, inck=16)
            pr.op("sync", lambda h, s=strip: h.dma_start(
                out=io["kT_d"].ap()[:, s * 512:(s + 1) * 512]
                    .rearrange("p (q d) -> p q d", q=4),
                in_=bass.AP(tst, 4 * 128,
                            [[4 * 9 * 128, 128], [9 * 128, 4], [1, 128]])),
                inc=s_out, inck=16)
            for hh in range(4):
                pr.op("sync", lambda h, s=strip, hh=hh: h.dma_start(
                    out=io["lkT_d"].ap()[hh, :, s * 512:(s + 1) * 512]
                        .rearrange("p (q d) -> p q d", q=4),
                    in_=bass.AP(tst, (5 + hh) * 128,
                                [[4 * 9 * 128, 128], [9 * 128, 4], [1, 128]])),
                    inc=s_out, inck=16)
            pr.op("sync", lambda h, s=strip: h.dma_start(
                out=io["v_d"].ap()[s * 512:(s + 1) * 512, :]
                    .rearrange("(q p) d -> p q d", p=128),
                in_=vst[:, :, 0, :]), inc=s_out, inck=16)
            for hh in range(4):
                pr.op("sync", lambda h, s=strip, hh=hh: h.dma_start(
                    out=io["lv_d"].ap()[hh, s * 512:(s + 1) * 512, :]
                        .rearrange("(q p) d -> p q d", p=128),
                    in_=vst[:, :, 1 + hh, :]), inc=s_out, inck=16)

    for g in range(NT):
        pr.op("tensor", lambda h, g=g: h.transpose(
            pnt[g % 2][0:9, 0:128], qn2[:, g, :], ident_f[:, :]),
            waits=[(s_sc, (g + 1) * SCg), (s_nt, 2 * g - 1)], inc=s_nt)
        pr.op("vector", lambda h, g=g: h.tensor_copy(
            nrows[0:9, g * 128:(g + 1) * 128], pnt[g % 2][0:9, 0:128]),
            waits=[(s_nt, 2 * g + 1)], inc=s_nt)
    pr.op("scalar", lambda h: h.activation(nrows[0:9, :], nrows[0:9, :], AF.Ln),
          waits=[(s_nt, 2 * NT)], inc=s_sc)
    pr.op("scalar", lambda h: h.activation(nrows[0:9, :], nrows[0:9, :], AF.Exp,
                                           scale=0.5),
          waits=[(s_sc, s_sc.n)], inc=s_sc)
    pr.op("sync", lambda h: h.dma_start(out=io["norm_d"].ap()[0:9, :],
                                        in_=nrows[0:9, :]),
          waits=[(s_sc, s_sc.n)], inc=s_out, inck=16)
    return {"nrows": nrows}


def make_p_sems(stack, nc):
    names = ["p_w", "p_pre", "p_x", "p_rt", "p_ln", "p_lns", "p_xn", "p_tp",
             "p_tpe", "p_mm", "p_ev", "p_ro", "p_sc", "p_out", "p_nt"]
    return {n: Sem(stack, nc, n) for n in names}


def build_p_debug():
    nc = bass.Bass("TRN2", target_bir_lowering=False)
    io = declare_p_inputs(nc)
    io.update(declare_p_scratch(nc, external=True))
    with contextlib.ExitStack() as ctx:
        sm = make_p_sems(ctx, nc)
        pr = Prog(nc)
        emit_phase_p(pr, ctx, io, sm)
        pr.op("sync", None, waits=[(sm["p_out"], sm["p_out"].n)])
        block = ctx.enter_context(nc.Block())
        pr.replay(block)
    return nc


def host_phase_p_inputs(c, inp):
    qs = slice(512 * c, 512 * c + 512)
    ks = slice(128 * c, 128 * c + 128)
    wP = np.concatenate(
        [inp["L0_attn_w_q"][:, qs], inp["L0_attn_w_k"][:, ks],
         inp["L0_attn_w_v"][:, ks], inp["L0_attn_w_lv"][:, qs]], axis=1)
    return {
        "x_full": np.ascontiguousarray(
            np.asarray(inp["L0_attn_hidden_states"], dtype=np.float32).reshape(T, D)),
        "wP": np.ascontiguousarray(wP, dtype=np.float32),
        "w_lqT": np.ascontiguousarray(np.asarray(inp["L0_attn_w_lq"]).T, dtype=np.float32),
        "w_lk_c": np.ascontiguousarray(np.asarray(inp["L0_attn_w_lk"])[:, qs], dtype=np.float32),
        "ropeT": host_rope_table(),
    }


def declare_a_out(nc, external):
    kw = {"kind": "ExternalOutput"} if external else {}
    return {"contrib0": nc.dram_tensor("contrib0", [1024, S], BF16, **kw),
            "contrib1": nc.dram_tensor("contrib1", [1024, S], BF16, **kw)}


def make_a_sems(stack, nc):
    names = ["a_kv", "a_q", "a_n", "a_mb", "a_mbe", "a_qk", "a_sub", "a_exp",
             "a_msk", "a_pv", "a_srw", "a_oev", "a_out", "a_kmx"]
    return {n: Sem(stack, nc, n) for n in names}


def emit_phase_a(pr, ctx, io, sm, after=(), post_b0=None):
    nc = pr.nc
    E = ctx.enter_context

    kTs = [E(nc.sbuf_tensor(f"a_kT{i}", [128, S], BF16)) for i in range(2)]
    vs = [E(nc.sbuf_tensor(f"a_v{i}", [128, 32, 128], BF16)) for i in range(2)]
    qTs = [E(nc.sbuf_tensor(f"a_qT{i}", [128, S], BF16)) for i in range(2)]
    mnst0 = E(nc.sbuf_tensor("a_mnst", [1, S], BF16))
    mnst = [mnst0, mnst0]
    knst = mnst0
    mrow0 = E(nc.sbuf_tensor("a_mrow", [1, S], BF16))
    mrow = [mrow0, mrow0]
    kmx = E(nc.sbuf_tensor("a_kmx", [1, 16], F32))
    pbuf = [E(nc.sbuf_tensor(f"a_pbuf{i}", [128, 2048], BF16)) for i in range(2)]
    msn = [E(nc.sbuf_tensor(f"a_msn{i}", [1, 1], BF16)) for i in range(2)]
    msb = [E(nc.sbuf_tensor(f"a_msb{i}", [128, 1], F32)) for i in range(2)]
    pbuf2 = E(nc.sbuf_tensor("a_pbuf2", [128, 2048], BF16))
    rrow = E(nc.sbuf_tensor("a_rrow", [1, 512], F32))
    rb = [E(nc.sbuf_tensor(f"a_rb{i}", [128, 512], F32)) for i in range(2)]
    outs = [E(nc.sbuf_tensor(f"a_out{i}", [128, 512], BF16)) for i in range(2)]
    ones1b = E(nc.sbuf_tensor("a_ones1b", [1, 128], BF16))
    ones1f = E(nc.sbuf_tensor("a_ones1f", [1, 128], F32))
    ones128 = E(nc.sbuf_tensor("a_ones128", [128, 1], BF16))

    ps = [E(nc.psum_tensor(f"a_ps{i}", [128, 1024], F32)) for i in range(2)]
    po = [E(nc.psum_tensor(f"a_po{i}", [128, 512], F32)) for i in range(2)]
    psm = E(nc.psum_tensor("a_psm", [128, 512], F32))
    pmb = E(nc.psum_tensor("a_pmb", [128, 512], F32))

    s_kv = sm["a_kv"]; s_q = sm["a_q"]; s_n = sm["a_n"]
    s_mb = sm["a_mb"]; s_mbe = sm["a_mbe"]
    s_qk = sm["a_qk"]; s_sub = sm["a_sub"]; s_exp = sm["a_exp"]
    s_msk = sm["a_msk"]; s_pv = sm["a_pv"]; s_srw = sm["a_srw"]
    s_oev = sm["a_oev"]; s_out = sm["a_out"]; s_kmx = sm["a_kmx"]

    pr.op("gpsimd", lambda h: h.memset(ones1b[:, :], 1.0), waits=list(after))
    pr.op("gpsimd", lambda h: h.memset(ones1f[:, :], 1.0))
    pr.op("gpsimd", lambda h: h.memset(ones128[:, :], 1.0), inc=s_kmx)

    runs = []
    for b in range(B):
        runs.append(("attn", b, [0, 1, 2, 3], 4, 0))
        for hl in range(4):
            runs.append(("lat", b, [hl], 5 + hl, 512))

    for r, (kind, b, qheads, krow, outbase) in enumerate(runs):
        pr.op("sync", lambda h, b=b, krow=krow: h.dma_start(
            out=knst[0:1, :], in_=io["norm_d"][krow:krow + 1, b * S:(b + 1) * S]),
            waits=list(after) + [(s_kmx, s_kmx.n)], inc=s_n, inck=16)
        pr.op("vector", lambda h, r=r: h.reduce_max(
            kmx[0:1, r:r + 1], knst[0:1, :], axis=AX.X),
            waits=[(s_n, s_n.n), (s_kmx, s_kmx.n)], inc=s_kmx)
    KMX_RDY = s_kmx.n

    qk_after_run, pv_after_run = [], []
    qk_after_qrun = []
    last_exp_half = [0, 0]
    last_pv_par = [0, 0]
    last_pv_diag = 0
    mnst_rd = [0, 0]
    mrow_used = 0
    msb_used = [0, 0]
    qrun = 0
    strip_g = 0

    for r, (kind, b, qheads, krow, outbase) in enumerate(runs):
        kvp = r % 2
        tok0 = b * S
        kv_waits = list(after) if r == 0 else \
            [(s_qk, qk_after_run[r - 2]), (s_pv, pv_after_run[r - 2])] if r >= 2 else []
        if kind == "attn":
            pr.op("sync", lambda h, kvp=kvp, tok0=tok0: h.dma_start(
                out=kTs[kvp][:, :], in_=io["kT_d"][:, tok0:tok0 + S]),
                waits=kv_waits, inc=s_kv, inck=16)
            pr.op("sync", lambda h, kvp=kvp, tok0=tok0: h.dma_start(
                out=vs[kvp][:, :, :],
                in_=io["v_d"][tok0:tok0 + S, :].rearrange("(c p) d -> p c d", p=128)),
                inc=s_kv, inck=16)
        else:
            hl = qheads[0]
            pr.op("sync", lambda h, kvp=kvp, tok0=tok0, hl=hl: h.dma_start(
                out=kTs[kvp][:, :], in_=io["lkT_d"][hl, :, tok0:tok0 + S]),
                waits=kv_waits, inc=s_kv, inck=16)
            pr.op("sync", lambda h, kvp=kvp, tok0=tok0, hl=hl: h.dma_start(
                out=vs[kvp][:, :, :],
                in_=io["lv_d"][hl, tok0:tok0 + S, :].rearrange("(c p) d -> p c d", p=128)),
                inc=s_kv, inck=16)
        KV_RDY = s_kv.n

        for qh in qheads:
            qp = qrun % 2
            qgate = [(s_qk, qk_after_qrun[qrun - 2])] if qrun >= 2 else list(after)
            pr.op("sync", lambda h, qp=qp, qh=qh, tok0=tok0: h.dma_start(
                out=qTs[qp][:, :], in_=io["qT_d"][qh, :, tok0:tok0 + S]),
                waits=qgate, inc=s_q, inck=16)
            Q_RDY = s_q.n
            pr.op("sync", lambda h, qp=qp, qh=qh, tok0=tok0: h.dma_start(
                out=mnst[qp][0:1, :], in_=io["norm_d"][qh:qh + 1, tok0:tok0 + S]),
                waits=[(s_srw, s_srw.n), (s_kmx, KMX_RDY)], inc=s_n, inck=16)
            N_RDY = s_n.n
            pr.op("vector", lambda h, qp=qp, r=r: h.tensor_scalar(
                out=mrow[qp][0:1, :], in0=mnst[qp][0:1, :],
                scalar1=kmx[0:1, r:r + 1], scalar2=-1.0,
                op0=ALU.mult, op1=ALU.mult),
                waits=[(s_kmx, KMX_RDY), (s_n, N_RDY)],
                inc=s_srw)
            MR = s_srw.n
            mnst_rd[qp] = MR

            steps = [(s, g2) for s in range(8) for g2 in range(s + 1)]
            prev = None

            def emit_pv(st):
                s, g2 = st
                sp = (strip_g0 + s) % 2
                NC = 4 * (s + 1)
                diag = (g2 == s)
                gp = g2 % 2
                src = pbuf2 if diag else pbuf[gp]
                def pvf(h, g2=g2, src=src, kvp=kvp, NC=NC, sp=sp):
                    for j in range(4):
                        kc = g2 * 4 + j
                        h.matmul(po[sp][:, :], vs[kvp][:, kc, :],
                                 src[:, j * 512:(j + 1) * 512],
                                 start=(kc == 0), stop=(kc == NC - 1))
                        ins = h.matmul(psm[0:1, :], ones128[:, 0:1],
                                       src[:, j * 512:(j + 1) * 512],
                                       start=(kc == 0), stop=(kc == NC - 1))
                    return ins
                pv_waits = [(s_exp, exp_of[st])]
                if diag:
                    pv_waits.append((s_msk, msk_of[st]))
                if g2 == 0:
                    pv_waits += [(s_oev, max(0, strip_g0 + s - 1)), (s_srw, s_srw.n)]
                pr.op("tensor", pvf, waits=pv_waits, inc=s_pv, inck=4)
                if diag:
                    nonlocal last_pv_diag
                    last_pv_diag = s_pv.n
                else:
                    last_pv_par[gp] = s_pv.n
                if diag:
                    emit_epilogue(s)

            def emit_epilogue(s):
                msb_used[(strip_g0 + s) % 2] = s_exp.n
                sp = (strip_g0 + s) % 2
                sg = strip_g0 + s
                q0 = s * 512
                pr.op("vector", lambda h: h.reciprocal(rrow[0:1, :], psm[0:1, :]),
                      waits=[(s_pv, s_pv.n)], inc=s_srw)
                pr.op("tensor", lambda h: h.matmul(
                    pmb[:, :], ones1f[0:1, :], rrow[0:1, :], start=True, stop=True),
                    waits=[(s_srw, s_srw.n), (s_mbe, s_mbe.n)], inc=s_mb)
                pr.op("vector", lambda h, sp=sp: h.tensor_copy(rb[sp][:, :], pmb[:, :]),
                      waits=[(s_mb, s_mb.n), (s_oev, max(0, sg - 1))], inc=s_mbe)
                pr.op("vector", lambda h, sp=sp: h.tensor_tensor(
                    out=outs[sp][:, :], in0=po[sp][:, :], in1=rb[sp][:, :], op=ALU.mult),
                    waits=[(s_mbe, s_mbe.n), (s_out, 16 * max(0, sg - 1))],
                    inc=s_oev)
                row0 = outbase + qh * 128
                ct = io["contrib0"] if b == 0 else io["contrib1"]
                pr.op("sync", lambda h, sp=sp, row0=row0, q0=q0, ct=ct: h.dma_start(
                    out=ct[row0:row0 + 128, q0:q0 + 512],
                    in_=outs[sp][:, :]),
                    waits=[(s_oev, s_oev.n)], inc=s_out, inck=16)

            strip_g0 = strip_g
            exp_of, msk_of = {}, {}
            msb_rdy = {}

            def emit_msb_prep(s8):
                sp8 = (strip_g0 + s8) % 2
                pr.op("vector", lambda h, sp8=sp8, s8=s8, qp=qp: h.tensor_reduce(
                    msn[sp8][0:1, 0:1], mrow[qp][0:1, s8 * 512:(s8 + 1) * 512],
                    axis=AX.X, op=ALU.min),
                    waits=[(s_srw, MR), (s_exp, msb_used[sp8])], inc=s_srw)
                pr.op("tensor", lambda h, sp8=sp8: h.matmul(
                    pmb[:, 0:1], ones1b[0:1, :], msn[sp8][0:1, 0:1],
                    start=True, stop=True),
                    waits=[(s_srw, s_srw.n), (s_mbe, s_mbe.n)], inc=s_mb)
                pr.op("vector", lambda h, sp8=sp8: h.tensor_copy(
                    msb[sp8][:, 0:1], pmb[:, 0:1]),
                    waits=[(s_mb, s_mb.n)], inc=s_mbe)
                msb_rdy[s8] = s_mbe.n
            for st in steps:
                s, g2 = st
                q0 = s * 512
                gp = g2 % 2
                diag = (g2 == s)
                sp8 = (strip_g0 + s) % 2
                if g2 == 0:
                    emit_msb_prep(s)
                for half in range(2):
                    def qkf(h, g2=g2, half=half, kvp=kvp, qp=qp, q0=q0):
                        for j in range(2):
                            kc = g2 * 4 + half * 2 + j
                            ins = h.matmul(
                                ps[half][:, j * 512:(j + 1) * 512],
                                kTs[kvp][:, kc * 128:(kc + 1) * 128],
                                qTs[qp][:, q0:q0 + 512],
                                start=True, stop=True)
                        return ins
                    pr.op("tensor", qkf,
                          waits=[(s_kv, KV_RDY), (s_q, Q_RDY),
                                 (s_exp, last_exp_half[half])],
                          inc=s_qk)
                    pr.op("scalar", lambda h, gp=gp, half=half, sp8=sp8: h.activation(
                        bass.AP(pbuf[gp], half * 1024, [[2048, 128], [1, 1024]]),
                        ps[half][:, :], AF.Exp, bias=msb[sp8][:, 0:1]),
                        waits=[(s_qk, s_qk.n), (s_pv, last_pv_par[gp]),
                               (s_mbe, msb_rdy[s])],
                        inc=s_exp)
                    last_exp_half[half] = s_exp.n
                exp_of[st] = s_exp.n
                if diag:
                    for j in range(4):
                        kc = g2 * 4 + j
                        pr.op("gpsimd", lambda h, gp=gp, j=j, kc=kc, q0=q0:
                              h.affine_select(
                                  out=pbuf2[:, j * 512:(j + 1) * 512],
                                  in_=pbuf[gp][:, j * 512:(j + 1) * 512],
                                  pattern=[[1, 512]], compare_op=ALU.is_ge,
                                  fill=0.0, base=q0 - kc * 128,
                                  channel_multiplier=-1),
                              waits=[(s_exp, s_exp.n), (s_pv, last_pv_diag)],
                              inc=s_msk)
                    msk_of[st] = s_msk.n
                if prev is not None:
                    emit_pv(prev)
                prev = st
            emit_pv(prev)
            strip_g += 8
            qk_after_qrun.append(s_qk.n)
            mrow_used = s_qk.n
            qrun += 1
        qk_after_run.append(s_qk.n)
        pv_after_run.append(s_pv.n)
        if r == 4:
            sm["a_out_b0"] = s_out.n
            if post_b0 is not None:
                post_b0()
    return {}


def build_pa_debug():
    nc = bass.Bass("TRN2", target_bir_lowering=False)
    io = declare_p_inputs(nc)
    io.update(declare_p_scratch(nc, external=True))
    io.update(declare_a_out(nc, external=True))
    with contextlib.ExitStack() as ctx:
        sm = make_p_sems(ctx, nc)
        sm.update(make_a_sems(ctx, nc))
        pr = Prog(nc)
        with contextlib.ExitStack() as pctx:
            emit_phase_p(pr, pctx, io, sm)
        gate = [(sm["p_out"], sm["p_out"].n)]
        emit_phase_a(pr, ctx, io, sm, after=gate)
        pr.op("sync", None, waits=[(sm["a_out"], sm["a_out"].n)])
        block = ctx.enter_context(nc.Block())
        pr.replay(block)
    return nc


def declare_c_inputs(nc, gathered_external):
    io = {}
    kwg = {"kind": "ExternalInput"} if gathered_external else {"addr_space": "Shared"}
    io["gath0"] = nc.dram_tensor("gath0", [8192, S], BF16, **kwg)
    io["gath1"] = nc.dram_tensor("gath1", [8192, S], BF16, **kwg)
    io["wcat"] = nc.dram_tensor("wcat", [8192, 512], F32, kind="ExternalInput")
    io["x_sl"] = nc.dram_tensor("x_sl", [T, 512], F32, kind="ExternalInput")
    io["WbT_c"] = nc.dram_tensor("WbT_c", [D, 512], F32, kind="ExternalInput")
    io["AG_cat"] = nc.dram_tensor("AG_cat", [D, 72], F32, kind="ExternalInput")
    io["B_c"] = nc.dram_tensor("B_c", [64, 512], F32, kind="ExternalInput")
    io["y_sl"] = nc.dram_tensor("y_sl", [T, 512], F32, kind="ExternalOutput")
    io["o1_d"] = nc.dram_tensor("o1_d", [T, 512], BF16)
    io["x1_d"] = nc.dram_tensor("x1_d", [T, 512], BF16)
    io["ffn_d"] = nc.dram_tensor("ffn_d", [T, 512], BF16)
    for i in range(3):
        io[f"sti{i}"] = nc.dram_tensor(f"sti{i}", [2, 128, NT], F32)
        io[f"sto{i}"] = nc.dram_tensor(f"sto{i}", [2, 128, NT], F32,
                                       addr_space="Shared")
    io["xfc_d"] = nc.dram_tensor("xfc_d", [512, T], BF16)
    io["xfg_d"] = nc.dram_tensor("xfg_d", [D, T], BF16, addr_space="Shared")
    return io


def make_c_sems(stack, nc):
    names = ["c_w", "c_g", "c_mm1", "c_st", "c_sts", "c_ev", "c_dma", "c_cc",
             "c_fin", "c_x", "c_tp", "c_tpe", "c_mm3", "c_hg", "c_t2", "c_out",
             "c_fins"]
    return {n: Sem(stack, nc, n) for n in names}


def alloc_c_sbuf(nc, ctx):
    E = ctx.enter_context
    bufs = {}
    bufs["stats"] = E(nc.sbuf_tensor("c_stats", [128, 6, NT], F32))
    bufs["fin"] = E(nc.sbuf_tensor("c_fin", [128, 6, NT], F32))
    bufs["ftmp"] = E(nc.sbuf_tensor("c_ftmp", [128, 2, NT], F32))
    bufs["scr5"] = E(nc.sbuf_tensor("c_scr5", [128, 512], BF16))
    bufs["epsv"] = E(nc.sbuf_tensor("c_epsv", [128, 1], F32))
    bufs["ones_bf"] = E(nc.sbuf_tensor("c_onesb", [128, 128], BF16))
    bufs["ident_bf"] = E(nc.sbuf_tensor("c_identb", [128, 128], BF16))
    bufs["hg"] = E(nc.sbuf_tensor("c_hg", [128, 72], F32))
    bufs["t2b"] = E(nc.sbuf_tensor("c_t2b", [128, 40], F32))
    bufs["wd"] = E(nc.sbuf_tensor("c_wd", [128, 8], F32))
    bufs["hsc"] = E(nc.sbuf_tensor("c_hsc", [128, 64], BF16))
    bufs["hscT"] = E(nc.sbuf_tensor("c_hscT", [64, 128], BF16))
    bufs["xt"] = [E(nc.sbuf_tensor(f"c_xt{i}", [128, 512], F32)) for i in range(2)]
    bufs["aux"] = [E(nc.sbuf_tensor(f"c_aux{i}", [128, 512], BF16)) for i in range(2)]
    bufs["ev2"] = [E(nc.sbuf_tensor(f"c_ev2{i}", [128, 512], BF16)) for i in range(2)]
    _t1 = E(nc.sbuf_tensor("c_tmp1", [128, 512], F32))
    bufs["tmp1"] = [_t1, _t1]
    bufs["outt"] = bufs["xt"]
    bufs["xfst"] = [E(nc.sbuf_tensor(f"c_xfst{i}", [128, 4, 128], BF16)) for i in range(2)]
    bufs["wcat_sb"] = E(nc.sbuf_tensor("c_wcat", [128, 64, 512], BF16))
    return bufs


def emit_phase_c(pr, ctx, io, sm, after=(), upto=None, bufs=None,
                 after_w=None, cc_gate0=1, cc_gate1=1):
    nc = pr.nc
    E = ctx.enter_context

    s_w = sm["c_w"]; s_g = sm["c_g"]; s_mm1 = sm["c_mm1"]
    s_st = sm["c_st"]; s_sts = sm["c_sts"]; s_ev = sm["c_ev"]
    s_dma = sm["c_dma"]; s_cc = sm["c_cc"]; s_fin = sm["c_fin"]
    s_x = sm["c_x"]; s_tp = sm["c_tp"]; s_tpe = sm["c_tpe"]
    s_mm3 = sm["c_mm3"]; s_hg = sm["c_hg"]; s_t2 = sm["c_t2"]
    s_out = sm["c_out"]; s_fns = sm["c_fins"]

    if bufs is None:
        bufs = alloc_c_sbuf(nc, ctx)
    if after_w is None:
        after_w = list(after)
    stats = bufs["stats"]; fin = bufs["fin"]; ftmp = bufs["ftmp"]
    scr5 = bufs["scr5"]; epsv = bufs["epsv"]; ones_bf = bufs["ones_bf"]
    ident_bf = bufs["ident_bf"]; hg = bufs["hg"]; t2b = bufs["t2b"]
    wd = bufs["wd"]; hsc = bufs["hsc"]; hscT = bufs["hscT"]
    xt = bufs["xt"]; aux = bufs["aux"]; ev2 = bufs["ev2"]
    tmp1 = bufs["tmp1"]; outt = bufs["outt"]; xfst = bufs["xfst"]

    po1 = [E(nc.psum_tensor(f"c_po1{i}", [128, 512], F32)) for i in range(2)]
    pf = [E(nc.psum_tensor(f"c_pf{i}", [128, 512], F32)) for i in range(2)]
    phg = [E(nc.psum_tensor(f"c_phg{i}", [128, 512], F32)) for i in range(2)]
    ptb = [E(nc.psum_tensor(f"c_ptb{i}", [128, 1024], BF16)) for i in range(2)]

    pr.op("gpsimd", lambda h: h.memset(epsv[:, :], EPS), waits=list(after_w))
    pr.op("gpsimd", lambda h: h.memset(ones_bf[:, :], 1.0))
    pr.op("gpsimd", lambda h: h.affine_select(
        out=ident_bf[:, :], in_=ones_bf[:, :], pattern=[[1, 128]],
        compare_op=ALU.is_equal, fill=0.0, base=0, channel_multiplier=-1),
        inc=s_w)
    ID_RDY = s_w.n

    def stats_ar(idx):
        pr.op("sync", lambda h, idx=idx: h.dma_start(
            out=io[f"sti{idx}"].ap().rearrange("s p t -> p s t"),
            in_=stats[:, 2 * idx:2 * idx + 2, :]),
            waits=[(s_dma, s_dma.n)], inc=s_dma, inck=16)
        pr.op("gpsimd", lambda h, idx=idx: h.collective_compute(
            "AllReduce", ALU.add, replica_groups=[list(range(8))],
            ins=[io[f"sti{idx}"].ap().opt()], outs=[io[f"sto{idx}"].ap().opt()]),
            waits=[(s_dma, s_dma.n)], inc=s_cc)
        pr.op("sync", lambda h, idx=idx: h.dma_start(
            out=stats[:, 2 * idx:2 * idx + 2, :],
            in_=io[f"sto{idx}"].ap().rearrange("s p t -> p s t")),
            waits=[(s_cc, s_cc.n)], inc=s_dma, inck=16)
        AR = s_dma.n
        pr.op("vector", lambda h, idx=idx: h.tensor_scalar(
            out=fin[:, 2 * idx, :], in0=stats[:, 2 * idx, :],
            scalar1=1.0 / D, scalar2=None, op0=ALU.mult, op1=ALU.bypass),
            waits=[(s_dma, AR)], inc=s_fin)
        pr.op("vector", lambda h, idx=idx: h.tensor_tensor(
            out=ftmp[:, 0, :], in0=fin[:, 2 * idx, :], in1=fin[:, 2 * idx, :],
            op=ALU.mult), waits=[(s_fin, s_fin.n)], inc=s_fin)
        pr.op("vector", lambda h, idx=idx: h.tensor_scalar(
            out=ftmp[:, 1, :], in0=stats[:, 2 * idx + 1, :],
            scalar1=1.0 / D, scalar2=None, op0=ALU.mult, op1=ALU.bypass),
            waits=[(s_dma, AR)], inc=s_fin)
        pr.op("vector", lambda h: h.tensor_tensor(
            out=ftmp[:, 1, :], in0=ftmp[:, 1, :], in1=ftmp[:, 0, :],
            op=ALU.subtract), waits=[(s_fin, s_fin.n)], inc=s_fin)
        pr.op("scalar", lambda h: h.activation(
            ftmp[:, 0, :], ftmp[:, 1, :], AF.Ln, bias=epsv[:, 0:1]),
            waits=[(s_fin, s_fin.n)], inc=s_fns)
        pr.op("scalar", lambda h, idx=idx: h.activation(
            fin[:, 2 * idx + 1, :], ftmp[:, 0, :], AF.Exp, scale=-0.5),
            waits=[(s_fns, s_fns.n)], inc=s_fns)

    po1_rd = [0, 0]
    evdma = [0, 0]
    with contextlib.ExitStack() as c1ctx:
        wcat_sb = bufs["wcat_sb"]
        gbuf = [c1ctx.enter_context(nc.sbuf_tensor(f"c_gb{i}", [128, 64, 256], BF16))
                for i in range(2)]
        pr.op("gpsimd", lambda h: h.dma_start(
            out=wcat_sb[:, :, :],
            in_=io["wcat"].ap().rearrange("(c p) f -> p c f", p=128)),
            waits=list(after_w), inc=s_w, inck=16)
        W1_RDY = s_w.n
        for tg in range(NT // 2):
            gp = tg % 2
            gsrc = io["gath0"] if tg < 16 else io["gath1"]
            gcol = (tg % 16) * 256
            pr.op("sync", lambda h, gp=gp, gsrc=gsrc, gcol=gcol: h.dma_start(
                out=gbuf[gp][:, :, :],
                in_=gsrc[:, gcol:gcol + 256]
                    .rearrange("(c p) t -> p c t", p=128)),
                waits=[(s_mm1, max(0, 2 * tg - 2)),
                       (s_cc, cc_gate0 if tg < 16 else cc_gate1)]
                      + (list(after) if tg < 2 else []),
                inc=s_g, inck=16)
            G_RDY = s_g.n
            for ti in range(2):
                g = tg * 2 + ti
                par = g % 2
                def mm1(h, gp=gp, ti=ti, par=par):
                    for fc in range(64):
                        ins = h.matmul(po1[par][:, :],
                                       gbuf[gp][:, fc, ti * 128:(ti + 1) * 128],
                                       wcat_sb[:, fc, :],
                                       start=(fc == 0), stop=(fc == 63))
                    return ins
                pr.op("tensor", mm1,
                      waits=[(s_g, G_RDY), (s_w, W1_RDY), (s_ev, max(0, g - 1)),
                             (s_st, po1_rd[par])],
                      inc=s_mm1)
                MM = s_mm1.n
                if upto != "c1m":
                    pr.op("vector", lambda h, par=par, g=g: h.reduce_sum(
                        stats[:, 0, g:g + 1], po1[par][:, :], axis=AX.X),
                        waits=[(s_mm1, MM)], inc=s_st)
                    po1_rd[par] = s_st.n
                if upto not in ("c1m", "c1s"):
                    pr.op("scalar", lambda h, par=par, g=g: h.activation(
                        scr5[:, :], po1[par][:, :], AF.Square,
                        accum_out=stats[:, 1, g:g + 1]),
                        waits=[(s_mm1, MM), (s_st, po1_rd[par])], inc=s_sts)
                pr.op("scalar", lambda h, par=par: h.copy(
                    ev2[par][:, :], po1[par][:, :]),
                    waits=[(s_dma, evdma[par]), (s_mm1, MM)], inc=s_ev)
                pr.op("sync", lambda h, g=g, par=par: h.dma_start(
                    out=io["o1_d"][g * 128:(g + 1) * 128, :], in_=ev2[par][:, :]),
                    waits=[(s_ev, s_ev.n)], inc=s_dma, inck=16)
                evdma[par] = s_dma.n
    C1_MM = s_mm1.n
    if upto in ("c1", "c1m", "c1s"):
        pr.op("sync", None, waits=[(s_sts, s_sts.n), (s_st, s_st.n), (s_dma, s_dma.n)])
        return {}
    pr.op("sync", None, waits=[(s_sts, s_sts.n), (s_st, s_st.n), (s_dma, s_dma.n)],
          inc=s_dma, inck=1)
    stats_ar(0)
    FIN2 = s_fns.n
    if upto == "ar1":
        pr.op("sync", None, waits=[(s_fns, FIN2)])
        return {}

    x1_done = [0, 0]
    x1_st = [0, 0]
    for g in range(NT):
        par = g % 2
        pr.op("sync", lambda h, g=g, par=par: h.dma_start(
            out=xt[par][:, :], in_=io["x_sl"][g * 128:(g + 1) * 128, :]),
            waits=[(s_st, x1_st[par]), (s_dma, x1_done[par])], inc=s_x, inck=16)
        pr.op("sync", lambda h, g=g, par=par: h.dma_start(
            out=aux[par][:, :], in_=io["o1_d"][g * 128:(g + 1) * 128, :]),
            inc=s_x, inck=16)
        X_RDY = s_x.n
        pr.op("vector", lambda h, g=g, par=par: h.tensor_scalar(
            out=tmp1[par][:, :], in0=aux[par][:, :],
            scalar1=fin[:, 0, g:g + 1], scalar2=fin[:, 1, g:g + 1],
            op0=ALU.subtract, op1=ALU.mult),
            waits=[(s_x, X_RDY), (s_fns, FIN2)], inc=s_st)
        pr.op("vector", lambda h, par=par: h.tensor_tensor(
            out=aux[par][:, :], in0=tmp1[par][:, :], in1=xt[par][:, :], op=ALU.add),
            waits=[(s_st, s_st.n)], inc=s_st)
        X1_RDY = s_st.n
        pr.op("vector", lambda h, par=par, g=g: h.reduce_sum(
            stats[:, 2, g:g + 1], aux[par][:, :], axis=AX.X),
            waits=[(s_st, X1_RDY)], inc=s_st)
        x1_st[par] = s_st.n
        pr.op("scalar", lambda h, par=par, g=g: h.activation(
            scr5[:, :], aux[par][:, :], AF.Square, accum_out=stats[:, 3, g:g + 1]),
            waits=[(s_st, X1_RDY)], inc=s_sts)
        pr.op("sync", lambda h, g=g, par=par: h.dma_start(
            out=io["x1_d"][g * 128:(g + 1) * 128, :], in_=aux[par][:, :]),
            waits=[(s_sts, s_sts.n)], inc=s_dma, inck=16)
        x1_done[par] = s_dma.n
    if upto == "c2a":
        pr.op("sync", None, waits=[(s_sts, s_sts.n), (s_st, s_st.n), (s_dma, s_dma.n)])
        return {}
    pr.op("sync", None, waits=[(s_sts, s_sts.n), (s_st, s_st.n), (s_dma, s_dma.n)],
          inc=s_dma, inck=1)
    stats_ar(1)
    FIN3 = s_fns.n

    xf_dma = [0, 0]
    for g in range(NT):
        par = g % 2
        pr.op("sync", lambda h, g=g, par=par: h.dma_start(
            out=aux[par][:, :], in_=io["x1_d"][g * 128:(g + 1) * 128, :]),
            waits=[(s_st, x1_st[par]), (s_fns, FIN3)], inc=s_x, inck=16)
        X_RDY = s_x.n
        pr.op("vector", lambda h, g=g, par=par: h.tensor_scalar(
            out=scr5[:, :], in0=aux[par][:, :],
            scalar1=fin[:, 2, g:g + 1], scalar2=fin[:, 3, g:g + 1],
            op0=ALU.subtract, op1=ALU.mult),
            waits=[(s_x, X_RDY), (s_fns, FIN3), (s_tp, max(0, s_tp.n))], inc=s_st)
        x1_st[par] = s_st.n
        XF_RDY = s_st.n
        def tpf(h, g=g):
            for j in range(4):
                ins = h.transpose(ptb[g % 2][:, j * 128:(j + 1) * 128],
                                  scr5[:, j * 128:(j + 1) * 128], ident_bf[:, :])
            return ins
        pr.op("tensor", tpf, waits=[(s_st, XF_RDY), (s_w, ID_RDY),
                                    (s_tpe, max(0, g - 1))], inc=s_tp)
        pr.op("vector", lambda h, g=g: h.tensor_copy(
            xfst[g % 2][:, :, :].rearrange("p c t -> p (c t)"),
            ptb[g % 2][:, 0:512]),
            waits=[(s_tp, s_tp.n), (s_dma, xf_dma[g % 2])], inc=s_tpe)
        pr.op("sync", lambda h, g=g: h.dma_start(
            out=io["xfc_d"][:, g * 128:(g + 1) * 128]
                .rearrange("(c p) t -> p c t", p=128),
            in_=xfst[g % 2][:, :, :]),
            waits=[(s_tpe, s_tpe.n)], inc=s_dma, inck=16)
        xf_dma[g % 2] = s_dma.n
    if upto == "c2b":
        pr.op("sync", None, waits=[(s_dma, s_dma.n)])
        return {}
    pr.op("gpsimd", lambda h: h.collective_compute(
        "AllGather", ALU.bypass, replica_groups=[list(range(8))],
        ins=[io["xfc_d"].ap().opt()], outs=[io["xfg_d"].ap().opt()]),
        waits=[(s_dma, s_dma.n)], inc=s_cc)
    AG2 = s_cc.n

    pf_rd = [0, 0]
    ev3 = [0, 0]
    with contextlib.ExitStack() as c3ctx:
        wb_sb = c3ctx.enter_context(nc.sbuf_tensor("c_wb", [128, ND, 512], BF16))
        ag_sb = c3ctx.enter_context(nc.sbuf_tensor("c_ag", [128, ND, 72], BF16))
        b_sb = c3ctx.enter_context(nc.sbuf_tensor("c_b", [64, 512], BF16))
        xfb = [c3ctx.enter_context(nc.sbuf_tensor(f"c_xfb{i}", [128, ND, 256], BF16))
               for i in range(2)]
        pr.op("gpsimd", lambda h: h.dma_start(
            out=wb_sb[:, :, :],
            in_=io["WbT_c"].ap().rearrange("(c p) f -> p c f", p=128)),
            waits=[(s_mm1, C1_MM)], inc=s_w, inck=16)
        pr.op("gpsimd", lambda h: h.dma_start(
            out=ag_sb[:, :, :],
            in_=io["AG_cat"].ap().rearrange("(c p) f -> p c f", p=128)),
            inc=s_w, inck=16)
        pr.op("gpsimd", lambda h: h.dma_start(
            out=b_sb[0:64, :], in_=io["B_c"].ap()), inc=s_w, inck=16)
        W3_RDY = s_w.n
        for tg in range(NT // 2):
            gp = tg % 2
            pr.op("sync", lambda h, tg=tg, gp=gp: h.dma_start(
                out=xfb[gp][:, :, :],
                in_=io["xfg_d"][:, tg * 256:(tg + 1) * 256]
                    .rearrange("(c p) t -> p c t", p=128)),
                waits=[(s_cc, AG2), (s_mm3, max(0, 2 * (2 * tg - 2)))],
                inc=s_g, inck=16)
            G_RDY = s_g.n
            for ti in range(2):
                g = tg * 2 + ti
                par = g % 2
                def mmf(h, gp=gp, ti=ti, par=par):
                    for dc in range(ND):
                        h.matmul(pf[par][:, :],
                                 xfb[gp][:, dc, ti * 128:(ti + 1) * 128],
                                 wb_sb[:, dc, :],
                                 start=(dc == 0), stop=False)
                        ins = h.matmul(phg[par][:, 0:72],
                                       xfb[gp][:, dc, ti * 128:(ti + 1) * 128],
                                       ag_sb[:, dc, :],
                                       start=(dc == 0), stop=(dc == ND - 1))
                    return ins
                pr.op("tensor", mmf,
                      waits=[(s_g, G_RDY), (s_w, W3_RDY), (s_hg, max(0, g - 1)),
                             (s_ev, ev3[par]), (s_st, pf_rd[par])],
                      inc=s_mm3)
                MM = s_mm3.n
                pr.op("vector", lambda h, par=par: h.tensor_copy(
                    hg[:, :], phg[par][:, 0:72]), waits=[(s_mm3, MM)], inc=s_hg)
                HG = s_hg.n
                pr.op("vector", lambda h: h.reduce_max(
                    t2b[:, 0:1], hg[:, 64:72], axis=AX.X),
                    waits=[(s_hg, HG), (s_t2, s_t2.n)], inc=s_t2)
                pr.op("vector", lambda h: h.tensor_scalar(
                    out=t2b[:, 1:2], in0=t2b[:, 0:1], scalar1=-1.0, scalar2=None,
                    op0=ALU.mult, op1=ALU.bypass),
                    waits=[(s_t2, s_t2.n)], inc=s_t2)
                pr.op("scalar", lambda h: h.activation(
                    t2b[:, 8:16], hg[:, 64:72], AF.Exp, bias=t2b[:, 1:2]),
                    waits=[(s_t2, s_t2.n)], inc=s_t2)
                pr.op("vector", lambda h: h.reduce_max(
                    t2b[:, 2:3], t2b[:, 8:16], axis=AX.X),
                    waits=[(s_t2, s_t2.n)], inc=s_t2)
                pr.op("vector", lambda h: h.tensor_scalar(
                    out=t2b[:, 16:24], in0=t2b[:, 8:16], scalar1=t2b[:, 2:3],
                    scalar2=None, op0=ALU.is_ge, op1=ALU.bypass),
                    waits=[(s_t2, s_t2.n)], inc=s_t2)
                pr.op("vector", lambda h: h.tensor_scalar(
                    out=t2b[:, 24:32], in0=t2b[:, 16:24], scalar1=2.0, scalar2=None,
                    op0=ALU.mult, op1=ALU.bypass),
                    waits=[(s_t2, s_t2.n)], inc=s_t2)
                pr.op("vector", lambda h: h.tensor_tensor(
                    out=t2b[:, 24:32], in0=t2b[:, 8:16], in1=t2b[:, 24:32],
                    op=ALU.subtract), waits=[(s_t2, s_t2.n)], inc=s_t2)
                pr.op("vector", lambda h: h.reduce_max(
                    t2b[:, 3:4], t2b[:, 24:32], axis=AX.X),
                    waits=[(s_t2, s_t2.n)], inc=s_t2)
                pr.op("vector", lambda h: h.tensor_scalar(
                    out=t2b[:, 32:40], in0=t2b[:, 24:32], scalar1=t2b[:, 3:4],
                    scalar2=None, op0=ALU.is_ge, op1=ALU.bypass),
                    waits=[(s_t2, s_t2.n)], inc=s_t2)
                pr.op("vector", lambda h: h.tensor_tensor(
                    out=t2b[:, 16:24], in0=t2b[:, 16:24], in1=t2b[:, 32:40],
                    op=ALU.add), waits=[(s_t2, s_t2.n)], inc=s_t2)
                pr.op("vector", lambda h: h.tensor_tensor(
                    out=t2b[:, 4:5], in0=t2b[:, 2:3], in1=t2b[:, 3:4],
                    op=ALU.add), waits=[(s_t2, s_t2.n)], inc=s_t2)
                pr.op("vector", lambda h: h.reciprocal(
                    t2b[:, 5:6], t2b[:, 4:5]), waits=[(s_t2, s_t2.n)], inc=s_t2)
                pr.op("vector", lambda h: h.tensor_tensor(
                    out=t2b[:, 8:16], in0=t2b[:, 8:16], in1=t2b[:, 16:24],
                    op=ALU.mult), waits=[(s_t2, s_t2.n)], inc=s_t2)
                pr.op("vector", lambda h: h.tensor_scalar(
                    out=wd[:, 0:8], in0=t2b[:, 8:16], scalar1=t2b[:, 5:6],
                    scalar2=2.0, op0=ALU.mult, op1=ALU.mult),
                    waits=[(s_t2, s_t2.n)], inc=s_t2)
                for e in range(8):
                    pr.op("vector", lambda h, e=e: h.tensor_scalar(
                        out=hsc[:, e * 8:(e + 1) * 8], in0=hg[:, e * 8:(e + 1) * 8],
                        scalar1=wd[:, e:e + 1], scalar2=None,
                        op0=ALU.mult, op1=ALU.bypass),
                        waits=[(s_t2, s_t2.n)], inc=s_t2)
                pr.op("tensor", lambda h, g=g: h.transpose(
                    ptb[g % 2][0:64, 0:128], hsc[:, :], ident_bf[:, :]),
                    waits=[(s_t2, s_t2.n), (s_tpe, s_tpe.n), (s_w, ID_RDY)],
                    inc=s_tp)
                pr.op("vector", lambda h, g=g: h.tensor_copy(
                    hscT[0:64, :], ptb[g % 2][0:64, 0:128]),
                    waits=[(s_tp, s_tp.n)], inc=s_tpe)
                pr.op("tensor", lambda h, par=par: h.matmul(
                    pf[par][:, :], hscT[0:64, :], b_sb[0:64, :],
                    start=False, stop=True),
                    waits=[(s_tpe, s_tpe.n)], inc=s_mm3)
                MM2 = s_mm3.n
                pr.op("vector", lambda h, par=par, g=g: h.reduce_sum(
                    stats[:, 4, g:g + 1], pf[par][:, :], axis=AX.X),
                    waits=[(s_mm3, MM2)], inc=s_st)
                pf_rd[par] = s_st.n
                pr.op("scalar", lambda h, par=par, g=g: h.activation(
                    scr5[:, :], pf[par][:, :], AF.Square,
                    accum_out=stats[:, 5, g:g + 1]),
                    waits=[(s_mm3, MM2), (s_st, pf_rd[par])], inc=s_sts)
                pr.op("scalar", lambda h, par=par: h.copy(
                    ev2[par][:, :], pf[par][:, :]),
                    waits=[(s_dma, evdma[par])], inc=s_ev)
                ev3[par] = s_ev.n
                pr.op("sync", lambda h, g=g, par=par: h.dma_start(
                    out=io["ffn_d"][g * 128:(g + 1) * 128, :], in_=ev2[par][:, :]),
                    waits=[(s_ev, s_ev.n)], inc=s_dma, inck=16)
                evdma[par] = s_dma.n
    if upto == "c3":
        pr.op("sync", None, waits=[(s_sts, s_sts.n), (s_st, s_st.n), (s_dma, s_dma.n)])
        return {}
    pr.op("sync", None, waits=[(s_sts, s_sts.n), (s_st, s_st.n), (s_dma, s_dma.n)],
          inc=s_dma, inck=1)
    stats_ar(2)
    FIN4 = s_fns.n

    c4_st = [0, 0]
    for g in range(NT):
        par = g % 2
        pr.op("sync", lambda h, g=g, par=par: h.dma_start(
            out=ev2[par][:, :], in_=io["x1_d"][g * 128:(g + 1) * 128, :]),
            waits=[(s_out, 16 * max(0, g - 1)), (s_st, c4_st[par]), (s_fns, FIN4),
                   (s_dma, evdma[par])],
            inc=s_x, inck=16)
        pr.op("sync", lambda h, g=g, par=par: h.dma_start(
            out=aux[par][:, :], in_=io["ffn_d"][g * 128:(g + 1) * 128, :]),
            inc=s_x, inck=16)
        X_RDY = s_x.n
        pr.op("vector", lambda h, g=g, par=par: h.tensor_scalar(
            out=tmp1[par][:, :], in0=aux[par][:, :],
            scalar1=fin[:, 4, g:g + 1], scalar2=fin[:, 5, g:g + 1],
            op0=ALU.subtract, op1=ALU.mult),
            waits=[(s_x, X_RDY), (s_fns, FIN4)], inc=s_st)
        pr.op("vector", lambda h, g=g, par=par: h.tensor_tensor(
            out=outt[par][:, :], in0=tmp1[par][:, :], in1=ev2[par][:, :], op=ALU.add),
            waits=[(s_st, s_st.n)], inc=s_st)
        c4_st[par] = s_st.n
        pr.op("sync", lambda h, g=g, par=par: h.dma_start(
            out=io["y_sl"][g * 128:(g + 1) * 128, :], in_=outt[par][:, :]),
            waits=[(s_st, s_st.n)], inc=s_out, inck=16)
    pr.op("sync", None, waits=[(s_out, s_out.n)])
    return {}


def build_c_debug(upto=None):
    nc = bass.Bass("TRN2", target_bir_lowering=False)
    io = declare_c_inputs(nc, gathered_external=True)
    with contextlib.ExitStack() as ctx:
        sm = make_c_sems(ctx, nc)
        pr = Prog(nc)
        emit_phase_c(pr, ctx, io, sm, upto=upto, cc_gate0=0, cc_gate1=0)
        block = ctx.enter_context(nc.Block())
        pr.replay(block)
    return nc


def host_phase_c_inputs(c, inp, gathered=None):
    Fc = slice(512 * c, 512 * c + 512)
    w_o = np.asarray(inp["L0_attn_w_o"], dtype=np.float32)
    w_lo = np.asarray(inp["L0_attn_w_lo"], dtype=np.float32)
    rows = []
    for r in range(8):
        fr = slice(4 * r * 128, (4 * r + 4) * 128)
        rows.append(w_o[fr, Fc])
        rows.append(w_lo[fr, Fc])
    wcat = np.ascontiguousarray(np.concatenate(rows, axis=0))
    A = np.asarray(inp["L0_ffn_A"], dtype=np.float32).reshape(64, D).T
    gw = np.asarray(inp["L0_ffn_gate_W"], dtype=np.float32).T
    Bm = np.asarray(inp["L0_ffn_B"], dtype=np.float32)
    B_perm = np.ascontiguousarray(Bm.transpose(0, 2, 1).reshape(64, D))
    out = {
        "wcat": wcat,
        "x_sl": np.ascontiguousarray(
            np.asarray(inp["L0_attn_hidden_states"], dtype=np.float32)
            .reshape(T, D)[:, Fc]),
        "WbT_c": np.ascontiguousarray(
            np.asarray(inp["L0_ffn_W_base"], dtype=np.float32).T[:, Fc]),
        "AG_cat": np.ascontiguousarray(np.concatenate([A, gw], axis=1)),
        "B_c": np.ascontiguousarray(B_perm[:, Fc]),
    }
    if gathered is not None:
        out["gath0"] = np.ascontiguousarray(gathered[:, 0:S])
        out["gath1"] = np.ascontiguousarray(gathered[:, S:])
    return out


def build_full():
    nc = bass.Bass("TRN2", target_bir_lowering=False)
    io = declare_p_inputs(nc)
    io.update(declare_p_scratch(nc, external=False))
    io.update(declare_a_out(nc, external=False))
    io.update(declare_c_inputs(nc, gathered_external=False))
    with contextlib.ExitStack() as ctx:
        sm = make_p_sems(ctx, nc)
        sm.update(make_a_sems(ctx, nc))
        sm.update(make_c_sems(ctx, nc))
        pr = Prog(nc)
        with contextlib.ExitStack() as pctx:
            emit_phase_p(pr, pctx, io, sm)
        gate_a = [(sm["p_out"], sm["p_out"].n)]
        cbufs = alloc_c_sbuf(nc, ctx)
        s_cc = sm["c_cc"]
        holder = {}

        def post_b0():
            pr.op("gpsimd", lambda h: h.collective_compute(
                "AllGather", ALU.bypass, replica_groups=[list(range(8))],
                ins=[io["contrib0"].ap().opt()], outs=[io["gath0"].ap().opt()]),
                waits=[(sm["a_out"], sm["a_out_b0"])], inc=s_cc)
            holder["cg0"] = s_cc.n
        with contextlib.ExitStack() as actx:
            emit_phase_a(pr, actx, io, sm, after=gate_a, post_b0=post_b0)
        cg0 = holder["cg0"]
        pr.op("gpsimd", lambda h: h.collective_compute(
            "AllGather", ALU.bypass, replica_groups=[list(range(8))],
            ins=[io["contrib1"].ap().opt()], outs=[io["gath1"].ap().opt()]),
            waits=[(sm["a_out"], sm["a_out"].n)], inc=s_cc)
        cg1 = s_cc.n
        a_done = [(sm["a_qk"], sm["a_qk"].n), (sm["a_pv"], sm["a_pv"].n),
                  (sm["a_out"], sm["a_out"].n)]
        emit_phase_c(pr, ctx, io, sm, after=a_done, bufs=cbufs,
                     after_w=gate_a, cc_gate0=cg0, cc_gate1=cg1)
        block = ctx.enter_context(nc.Block())
        pr.replay(block)
    return nc


_NC_CACHE = {}


def kernel(**inputs):
    if "full" not in _NC_CACHE:
        _NC_CACHE["full"] = build_full()
    nc = _NC_CACHE["full"]
    in_maps = []
    for c in range(8):
        m = host_phase_p_inputs(c, inputs)
        m.update(host_phase_c_inputs(c, inputs))
        in_maps.append(m)
    res = run_bass_kernel_spmd(nc, in_maps, core_ids=list(range(8)))
    y = np.concatenate([res.results[c]["y_sl"] for c in range(8)], axis=1)
    return np.ascontiguousarray(y.reshape(B, S, D))


def kernel_unfused(**inputs):
    nc1 = build_pa_debug()
    in1 = [host_phase_p_inputs(c, inputs) for c in range(8)]
    res1 = run_bass_kernel_spmd(nc1, in1, core_ids=list(range(8)))
    gathered = np.concatenate(
        [np.concatenate([res1.results[c][f"contrib{i}"] for c in range(8)], axis=0)
         for i in range(2)], axis=1)
    nc2 = build_c_debug()
    in2 = [host_phase_c_inputs(c, inputs, gathered) for c in range(8)]
    res2 = run_bass_kernel_spmd(nc2, in2, core_ids=list(range(8)))
    y = np.concatenate([res2.results[c]["y_sl"] for c in range(8)], axis=1)
    return y.reshape(B, S, D)
